# version 08
# baseline (speedup 1.0000x reference)
import numpy as np
import concourse.bass as bass
import concourse.mybir as mybir
from concourse.bass_utils import run_bass_kernel_spmd

F32 = mybir.dt.float32
BF16 = mybir.dt.bfloat16
AF = mybir.ActivationFunctionType
OP = mybir.AluOpType

B, T, D = 4, 2048, 2048
H, DH, DR, DL = 16, 128, 64, 512
NHH = H // 2
TOK = T // 2
SCALE = float(1.0 / np.sqrt(DH))
NEG = -1.0e30

N_SUB = 4
DBG = {"ln": True, "ckv": True, "gather": True, "pairs": 4, "heads": 2,
       "rope": True, "attn": True, "oproj": True, "attn_parts": 5}


def _host_consts():
    BASE, MAX_SEQ, ORIG_MAX = 10000.0, 131072, 4096
    BETA_FAST, BETA_SLOW = 32.0, 1.0
    SCALE_ROPE = 1.0 / np.sqrt(DR)
    base_inv = 1.0 / BASE ** (np.arange(0, DR, 2, dtype=np.float64) / DR)
    scale = MAX_SEQ / ORIG_MAX
    freqs = np.arange(DR // 2, dtype=np.float64)
    ramp = np.clip((freqs - BETA_SLOW) / (BETA_FAST - BETA_SLOW), 0.0, 1.0)
    inv = base_inv * (1 - ramp) + base_inv / scale * ramp
    yarn = 0.1 * np.log(MAX_SEQ / ORIG_MAX) + 1.0
    t = np.arange(T, dtype=np.float64)
    fr = t[None, :] * inv[:, None]
    emb = np.concatenate([fr, fr], axis=0)
    cos, sin = np.cos(emb), np.sin(emb)
    cosq = (cos * yarn * SCALE_ROPE).astype(np.float32)
    sinq = (sin * yarn * SCALE_ROPE).astype(np.float32)
    cosk = (cos * yarn).astype(np.float32)
    sink = (sin * yarn).astype(np.float32)
    def fold(s):
        f = s.copy()
        f[0:32] = -f[0:32]
        return f
    cosq2 = np.concatenate([cosq, cosq], axis=0)
    sinq2 = np.concatenate([fold(sinq), fold(sinq)], axis=0)
    cosk2 = np.concatenate([cosk, cosk], axis=0)
    sink2 = np.concatenate([fold(sink), fold(sink)], axis=0)

    kk = np.arange(T)[:, None]
    qq = np.arange(T)[None, :]
    dist = qq - kk
    um = (((dist >= -256) & (dist <= 256)) | (kk % 64 == 0) | (kk < 128)) & (kk <= qq)
    import ml_dtypes
    mask_a = np.where(um, np.float32(0.0), np.float32(NEG)).astype(ml_dtypes.bfloat16)
    kk5 = np.arange(512)[:, None]
    qq5 = np.arange(512)[None, :]
    mask_c = np.where(kk5 <= qq5, np.float32(0.0), np.float32(NEG)).astype(ml_dtypes.bfloat16)
    ident = np.eye(128, dtype=np.float32)
    return dict(cosq2=cosq2, sinq2=sinq2, cosk2=cosk2, sink2=sink2,
                mask_a=mask_a, mask_c=mask_c, ident=ident)


def _chunked(w):
    Din, Dout = w.shape
    n_dc, n_oc = Din // 128, Dout // 128
    return np.ascontiguousarray(
        w.reshape(n_dc, 128, n_oc, 128).transpose(2, 0, 1, 3))


class Ctr:
    def __init__(self):
        self.c = {}

    def bump(self, key, n=1):
        self.c[key] = self.c.get(key, 0) + n
        return self.c[key]

    def val(self, key):
        return self.c.get(key, 0)


def build_program(n_sub=4, dbg=None):
    dbg = dbg or DBG
    nc = bass.Bass("TRN2", target_bir_lowering=False)

    def inp(name, shape):
        return nc.dram_tensor(name, list(shape), F32, kind="ExternalInput").ap()

    x0 = inp("x0", (16, 128, TOK))
    cosq2 = inp("cosq2", (128, T))
    sinq2 = inp("sinq2", (128, T))
    cosk2 = inp("cosk2", (128, T))
    sink2 = inp("sink2", (128, T))
    mask_a = nc.dram_tensor("mask_a", [T, T], BF16, kind="ExternalInput").ap()
    mask_c = nc.dram_tensor("mask_c", [512, 512], BF16, kind="ExternalInput").ap()
    ident = inp("ident", (128, 128))
    wq_a = inp("wq_a", (NHH, 16, 128, 128))
    wdkv = inp("wdkv", (4, 16, 128, 128))
    wuk = inp("wuk", (NHH, 4, 128, 128))
    wuv = inp("wuv", (NHH, 4, 128, 128))
    wqp = inp("wqp", (4, 16, 128, 128))
    wkp = inp("wkp", (4, 16, 128, 128))
    wo_a = inp("wo_a", (16, 8, 128, 128))
    wq_b = inp("wq_b", (NHH, 16, 128, 128))
    wk_b = inp("wk_b", (NHH, 16, 128, 128))
    wv_b = inp("wv_b", (NHH, 16, 128, 128))
    wo_b = inp("wo_b", (16, 8, 128, 128))
    w1 = inp("w1", (16, 16, 128, 128))
    b1 = inp("b1", (16, 128, 1))
    w2 = inp("w2", (16, 128, 8))
    b2 = inp("b2", (8, 1))
    v2w = inp("v2w", (4, 16, 128, 128))
    u2w = inp("u2w", (16, 4, 128, 128))

    y = nc.dram_tensor("y", [16, 128, TOK], F32, kind="ExternalOutput").ap()

    g1_in = nc.dram_tensor("g1_in", [4096, TOK], BF16)
    g1_out = nc.dram_tensor("g1_out", [2, 4096, TOK], BF16)
    g1c_in = nc.dram_tensor("g1c_in", [1024, TOK], BF16)
    g1c_out = nc.dram_tensor("g1c_out", [2, 1024, TOK], BF16)
    g2_in = nc.dram_tensor("g2_in", [4096, TOK], BF16)
    g2_out = nc.dram_tensor("g2_out", [2, 4096, TOK], BF16)
    rs_ins = [nc.dram_tensor(f"rs_in{i}", [2, 512, TOK], F32) for i in range(4)]
    rs_outs = [nc.dram_tensor(f"rs_out{i}", [512, TOK], F32) for i in range(4)]
    opc = nc.dram_tensor("opc", [2, NHH * 128, T], BF16)
    rowd = nc.dram_tensor("rowd", [8, T], F32)
    alpd = nc.dram_tensor("alpd", [8, TOK], F32)

    gidx = {0: (g1_in, g1_out), 1: (g2_in, g2_out), 2: (g1c_in, g1c_out)}
    C = Ctr()

    from contextlib import ExitStack
    with ExitStack() as ex:
        def sbuf(name, shape, dt=F32):
            return ex.enter_context(nc.sbuf_tensor(name, list(shape), dt))

        xr = sbuf("xr", (128, 16 * TOK))
        arena = sbuf("arena", (128, 16 * TOK))
        def AH(c0, n):
            return arena[:, c0 // 2:(c0 + n) // 2].bitcast(BF16)

        def AL(c0, n):
            return arena[:, 8192 + c0 // 2:8192 + (c0 + n) // 2].bitcast(BF16)

        pq0 = arena[:, 0 * T:1 * T]
        pq1 = arena[:, 1 * T:2 * T]
        pk0 = arena[:, 2 * T:3 * T]
        pk1 = arena[:, 3 * T:4 * T]
        pv0 = arena[:, 4 * T:5 * T]
        pv1 = arena[:, 5 * T:6 * T]
        pqr = arena[:, 6 * T:7 * T]
        pkr = arena[:, 7 * T:8 * T]
        wsh = sbuf("wsh", (128, 2 * 16 * 128), BF16)
        wsl = sbuf("wsl", (128, 2 * 16 * 128), BF16)
        wf32 = sbuf("wf32", (128, 2 * 512), F32)
        xs_hl = sbuf("xs_hl", (128, 3 * 1024), BF16)
        expb = sbuf("expb", (128, 3 * 512), F32)
        mskb = sbuf("mskb", (128, 3 * 512), BF16)
        tabA = sbuf("tabA", (128, 512))
        tabB = sbuf("tabB", (128, 512))
        rotb = sbuf("rotb", (128, 512))
        sc1 = sbuf("sc1", (128, 512))
        sc2 = sbuf("sc2", (128, 512))
        sc3 = sbuf("sc3", (128, 512))
        sc4 = sbuf("sc4", (128, 512))
        rsb = sbuf("rsb", (128, 1024))
        obh = sbuf("obh", (128, 512), BF16)
        obl = sbuf("obl", (128, 512), BF16)
        rowb = sbuf("rowb", (128, 1024))
        ones = sbuf("ones", (128, 8))
        idsb = sbuf("idsb", (128, 128))
        stat = sbuf("stat", (128, TOK))
        dmy = sbuf("dmy", (1, 8))
        hbuf = sbuf("hbuf", (128, 2 * 512))
        w2sb = sbuf("w2sb", (128, 16 * 8))
        b1sb = sbuf("b1sb", (128, 16))
        b2sb = sbuf("b2sb", (8, 1))
        vxh = sbuf("vxh", (128, 2 * TOK), BF16)
        vxl = sbuf("vxl", (128, 2 * TOK), BF16)

        pb = [ex.enter_context(nc.psum_tensor(f"pb{i}", [128, 512], F32))
              for i in range(8)]

        sW = ex.enter_context(nc.semaphore("sW"))
        sG = ex.enter_context(nc.semaphore("sG"))
        sT = ex.enter_context(nc.semaphore("sT"))
        sV = ex.enter_context(nc.semaphore("sV"))
        sA = ex.enter_context(nc.semaphore("sA"))
        sCC = ex.enter_context(nc.semaphore("sCC"))
        sF = [ex.enter_context(nc.semaphore("sF0")),
              ex.enter_context(nc.semaphore("sF1"))]
        sX = [ex.enter_context(nc.semaphore("sX0")),
              ex.enter_context(nc.semaphore("sX1")),
              ex.enter_context(nc.semaphore("sX2"))]
        sM = [ex.enter_context(nc.semaphore("sM0")),
              ex.enter_context(nc.semaphore("sM1")),
              ex.enter_context(nc.semaphore("sM2"))]
        sLN = ex.enter_context(nc.semaphore("sLN"))
        sR = ex.enter_context(nc.semaphore("sR"))
        sRS = ex.enter_context(nc.semaphore("sRS"))
        sRES = ex.enter_context(nc.semaphore("sRES"))
        sOUT = ex.enter_context(nc.semaphore("sOUT"))
        sB = ex.enter_context(nc.semaphore("sB"))
        sGP = ex.enter_context(nc.semaphore("sGP"))
        sOP = ex.enter_context(nc.semaphore("sOP"))

        block = ex.enter_context(nc.Block())
        TE, VE, AE, GE, SY = (block.tensor, block.vector, block.scalar,
                              block.gpsimd, block.sync)

        REG = {"cur": 0}

        def split_weights(wdram, oc, ndc):
            REG["cur"] ^= 1
            wbase = REG["cur"] * 2048
            te_pre = C.val("wsh_rd%d" % REG["cur"])
            for dc0 in range(0, ndc, 4):
                g = min(4, ndc - dc0)
                buf = (dc0 // 4) % 2
                vfree = C.val("wf_freeV%d" % buf)
                gfree = C.val("wf_freeG%d" % buf)
                first = (dc0 == 0)

                def ld(sy, wdram=wdram, oc=oc, dc0=dc0, g=g, buf=buf,
                       vfree=vfree, gfree=gfree, first=first, te_pre=te_pre):
                    if first and te_pre:
                        sy.wait_ge(sT, te_pre)
                    if vfree:
                        sy.wait_ge(sV, vfree)
                    if gfree:
                        sy.wait_ge(sGP, gfree)
                    src_ap = wdram[oc, dc0:dc0 + g, :, :].rearrange("j p c -> p j c")
                    dst_ap = wf32[:, buf * 512:buf * 512 + g * 128].rearrange(
                        "p (j c) -> p j c", j=g)
                    sy.dma_start(out=dst_ap, in_=src_ap).then_inc(sF[buf], 16)
                SY(ld)
                wcnt = C.bump("sF%d" % buf, 16)
                dst = wbase + dc0 * 128

                def hi(ve, buf=buf, dst=dst, wcnt=wcnt, g=g):
                    ve.wait_ge(sF[buf], wcnt)
                    ve.tensor_copy(wsh[:, dst:dst + g * 128],
                                   wf32[:, buf * 512:buf * 512 + g * 128]
                                   ).then_inc(sV, 1)
                VE(hi)
                vcnt = C.bump(sV, 1)
                C.c["wf_freeV%d" % buf] = vcnt

                def lo(gp, buf=buf, dst=dst, vcnt=vcnt, g=g):
                    gp.wait_ge(sV, vcnt)
                    gp.tensor_tensor(
                        wsl[:, dst:dst + g * 128],
                        wf32[:, buf * 512:buf * 512 + g * 128],
                        wsh[:, dst:dst + g * 128],
                        OP.subtract).then_inc(sGP, 1)
                GE(lo)
                gcnt = C.bump("sGP", 1)
                C.c["wf_freeG%d" % buf] = gcnt
                for dc in range(dc0, dc0 + g):
                    C.c["wsplit_dc%d" % dc] = (C.val(sV), C.val("sGP"))
            return wbase

        def mm3(te, psum_ap, wh, wl, rh, rl, start, stop):
            te.matmul(psum_ap, wh, rh, start=start, stop=False,
                      skip_group_check=True)
            te.matmul(psum_ap, wh, rl, start=False, stop=False,
                      skip_group_check=True)
            return te.matmul(psum_ap, wl, rh, start=False, stop=stop,
                             skip_group_check=True)

        def split_to(src_ap, gate_v, ship_sem):
            sem, spre = ship_sem

            def hi(ve, gate_v=gate_v, spre=spre, sem=sem):
                ve.wait_ge(sV, gate_v)
                if spre:
                    ve.wait_ge(sem, spre)
                ve.drain()
                ve.tensor_copy(obh[:], src_ap)
                ve.drain()
                ve.scalar_tensor_tensor(obl[:], src_ap, 1.0, obh[:],
                                        OP.mult, OP.subtract).then_inc(sV, 1)
            VE(hi)
            return C.bump(sV, 1)

        def ln_phase(gdst):
            for blk in range(2):
                t0 = blk * 512
                for dc in range(16):
                    sqb = (sc1, sc2)[dc % 2]
                    tpre = C.val("lnsq_free%d" % (dc % 2))
                    gate_dc = C.val("res_dc%d" % dc) or C.val("ln_gate")

                    def sq(ve, dc=dc, t0=t0, sqb=sqb, tpre=tpre,
                           gate=gate_dc):
                        if tpre:
                            ve.wait_ge(sT, tpre)
                        if gate:
                            ve.wait_ge(sV, gate)
                        ve.drain()
                        ve.tensor_tensor(sqb[:],
                                         xr[:, dc * TOK + t0:dc * TOK + t0 + 512],
                                         xr[:, dc * TOK + t0:dc * TOK + t0 + 512],
                                         OP.mult).then_inc(sV, 1)
                    VE(sq)
                    acnt = C.bump(sV, 1)

                    def mmst(te, dc=dc, t0=t0, sqb=sqb, acnt=acnt,
                             gate=gate_dc):
                        if gate:
                            te.wait_ge(sV, gate)
                        if dc == 0:
                            te.wait_ge(sV, C.val("ln_banks_free"))
                        te.matmul(pb[4][0:1, 0:512], ones[:, 0:1],
                                  xr[:, dc * TOK + t0:dc * TOK + t0 + 512],
                                  start=(dc == 0), stop=False, skip_group_check=True)
                        te.wait_ge(sV, acnt)
                        i = te.matmul(pb[5][0:1, 0:512], ones[:, 0:1], sqb[:],
                                      start=(dc == 0), stop=(dc == 15),
                                      skip_group_check=True)
                        i.then_inc(sT, 1)
                    TE(mmst)
                    tcnt = C.bump(sT, 1)
                    C.c["lnsq_free%d" % (dc % 2)] = tcnt
                tcnt = C.val(sT)
                wpre0 = C.val("sLN")

                def st(ve, blk=blk, tcnt=tcnt, wpre0=wpre0):
                    ve.wait_ge(sT, tcnt)
                    if wpre0:
                        ve.wait_ge(sLN, wpre0)
                    ve.drain()
                    ve.tensor_scalar(stat[0:1, 0:512], pb[4][0:1, 0:512],
                                     1.0 / 2048.0, 0.0, OP.mult, OP.add)
                    ve.tensor_scalar(stat[0:1, 512:1024], pb[5][0:1, 0:512],
                                     1.0 / 2048.0, 1e-5, OP.mult, OP.add).then_inc(sV, 1)
                VE(st)
                vcnt = C.bump(sV, 1)
                C.c["ln_banks_free"] = vcnt

                def shipm(sy, blk=blk, vcnt=vcnt):
                    sy.wait_ge(sV, vcnt)
                    sy.dma_start(out=rowd.ap()[0, blk * 512:blk * 512 + 512],
                                 in_=stat[0:1, 0:512]).then_inc(sLN, 16)
                SY(shipm)
                wcnt = C.bump("sLN", 16)

                def st2(ve, blk=blk, wcnt=wcnt):
                    ve.wait_ge(sLN, wcnt)
                    ve.tensor_tensor(stat[0:1, 0:512], stat[0:1, 0:512],
                                     stat[0:1, 0:512], OP.mult)
                    ve.drain()
                    ve.tensor_tensor(stat[0:1, 512:1024], stat[0:1, 512:1024],
                                     stat[0:1, 0:512], OP.subtract).then_inc(sV, 1)
                VE(st2)
                vcnt2 = C.bump(sV, 1)

                def rst(ae, blk=blk, vcnt2=vcnt2):
                    ae.wait_ge(sV, vcnt2)
                    ae.activation(out=stat[0:1, 512:1024], in_=stat[0:1, 512:1024],
                                  func=AF.Sqrt).then_inc(sA, 1)
                AE(rst)
                acnt = C.bump(sA, 1)

                def rcp(ve, blk=blk, acnt=acnt):
                    ve.wait_ge(sA, acnt)
                    ve.reciprocal(stat[0:1, 512:1024],
                                  stat[0:1, 512:1024]).then_inc(sV, 1)
                VE(rcp)
                vcnt3 = C.bump(sV, 1)

                def shipr(sy, blk=blk, vcnt3=vcnt3):
                    sy.wait_ge(sV, vcnt3)
                    sy.dma_start(out=rowd.ap()[1, blk * 512:blk * 512 + 512],
                                 in_=stat[0:1, 512:1024]).then_inc(sLN, 16)
                SY(shipr)
                C.bump("sLN", 16)

            wcnt0 = C.val("sLN")

            vgate = C.val(sV)

            def bcast(sy, wcnt0=wcnt0, vgate=vgate):
                sy.wait_ge(sLN, wcnt0)
                if vgate:
                    sy.wait_ge(sV, vgate)
                sy.dma_start(out=rowb[:, 0:512],
                             in_=rowd.ap()[0:1, 0:512].to_broadcast([128, 512])).then_inc(sLN, 16)
                sy.dma_start(out=rowb[:, 512:1024],
                             in_=rowd.ap()[1:2, 0:512].to_broadcast([128, 512])).then_inc(sLN, 16)
            SY(bcast)
            wcnt = C.bump("sLN", 32)

            for half in range(2):
                if half == 1:
                    vpre_h = C.val(sV)

                    def bcast2(sy, vpre_h=vpre_h):
                        sy.wait_ge(sV, vpre_h)
                        sy.dma_start(out=rowb[:, 0:512],
                                     in_=rowd.ap()[0:1, 512:1024].to_broadcast([128, 512])).then_inc(sLN, 16)
                        sy.dma_start(out=rowb[:, 512:1024],
                                     in_=rowd.ap()[1:2, 512:1024].to_broadcast([128, 512])).then_inc(sLN, 16)
                    SY(bcast2)
                    wcnt = C.bump("sLN", 32)
                for dc in range(16):
                    def app(ve, dc=dc, wcnt=wcnt, half=half,
                            gpre=C.val("sGP")):
                        if dc == 0:
                            ve.wait_ge(sLN, wcnt)
                        if gpre:
                            ve.wait_ge(sGP, gpre)
                        ve.drain()
                        c0 = dc * TOK + half * 512
                        ve.tensor_tensor(sc1[:], xr[:, c0:c0 + 512],
                                         rowb[:, 0:512], OP.subtract)
                        ve.drain()
                        ve.tensor_tensor(sc1[:], sc1[:],
                                         rowb[:, 512:1024], OP.mult).then_inc(sV, 1)
                    VE(app)
                    vcnt = C.bump(sV, 1)

                    def hi(ve, dc=dc, half=half, vcnt=vcnt):
                        ve.wait_ge(sV, vcnt)
                        ve.tensor_copy(AH(dc * TOK + half * 512, 512),
                                       sc1[:]).then_inc(sV, 1)
                    VE(hi)
                    vcnt2 = C.bump(sV, 1)

                    def lo(gp, dc=dc, half=half, vcnt2=vcnt2):
                        gp.wait_ge(sV, vcnt2)
                        gp.tensor_tensor(AL(dc * TOK + half * 512, 512),
                                         sc1[:],
                                         AH(dc * TOK + half * 512, 512),
                                         OP.subtract).then_inc(sGP, 1)
                    GE(lo)
                    C.bump("sGP", 1)
                    C.c["lnsp_v%d_%d" % (dc, half)] = vcnt2
                    C.c["lnsp_g%d_%d" % (dc, half)] = C.val("sGP")
            vsplit = C.val(sV)
            C.c["ln_split_doneG"] = C.val("sGP")
            if gdst is not None:
                gbuf, _ = gidx[gdst]
                for dc in range(16):
                    def shp(gp, dc=dc, vsplit=vsplit, gbuf=gbuf):
                        if dc == 0:
                            gp.wait_ge(sV, vsplit)
                            gp.wait_ge(sGP, C.val("sGP"))
                        gp.dma_start(out=gbuf.ap()[dc * 256:dc * 256 + 128, :],
                                     in_=AH(dc * TOK, TOK)).then_inc(sG, 16)
                        gp.dma_start(out=gbuf.ap()[dc * 256 + 128:dc * 256 + 256, :],
                                     in_=AL(dc * TOK, TOK)).then_inc(sG, 16)
                    GE(shp)
                    C.bump(sG, 32)
            C.c["ln_split_done"] = C.val(sV)
            C.c["ln_split_doneG"] = C.val("sGP")

        def gather(gdst):
            gcnt = C.val(sG)

            def coll(gp, gcnt=gcnt, gdst=gdst):
                gp.wait_ge(sG, gcnt)
                gi, go = gidx[gdst]
                gp.collective_compute(
                    "AllGather", OP.bypass,
                    replica_groups=[[0, 1], [2, 3], [4, 5], [6, 7]],
                    ins=[gi.ap()],
                    outs=[go.ap()],
                ).then_inc(sCC, 1)
            GE(coll)
            return C.bump(sCC, 1)

        def stream_proj(gdst, ndc, row0, dst_mode, cc_need, wbase):
            _, gout = gidx[gdst]
            dsto, mode = dst_mode
            wsp = [C.val("wsplit_dc%d" % d) for d in range(ndc)]
            for tt in range(4):
                rank, col0 = tt // 2, (tt % 2) * 512
                bank = pb[tt % 2]
                bfree = C.val("proj_bank_free%d" % (tt % 2))
                for dc in range(ndc):
                    buf = dc % 3
                    tfree = C.val("xs_free%d" % buf)

                    def ldx(sy, rank=rank, col0=col0, dc=dc, buf=buf, tfree=tfree,
                            gout=gout, row0=row0, cc_need=cc_need):
                        if tfree:
                            sy.wait_ge(sT, tfree)
                        sy.wait_ge(sCC, cc_need)
                        r0 = row0 + dc * 256
                        src_ap = gout.ap()[rank, r0:r0 + 256,
                                           col0:col0 + 512].rearrange(
                            "(h p) c -> p h c", p=128)
                        dst_ap = xs_hl[:, buf * 1024:buf * 1024 + 1024].rearrange(
                            "p (h c) -> p h c", h=2)
                        sy.dma_start(out=dst_ap, in_=src_ap).then_inc(sX[buf], 16)
                    SY(ldx)
                    wcnt = C.bump("sX%d" % buf, 16)

                    def mmp(te, dc=dc, buf=buf, wcnt=wcnt, ndc=ndc, bank=bank,
                            wsp=wsp, bfree=bfree, wbase=wbase):
                        te.wait_ge(sX[buf], wcnt)
                        te.wait_ge(sV, wsp[dc][0])
                        te.wait_ge(sGP, wsp[dc][1])
                        if dc == 0 and bfree:
                            te.wait_ge(sV, bfree)
                        w0 = wbase + dc * 128
                        i = mm3(te, bank[:, 0:512],
                                wsh[:, w0:w0 + 128], wsl[:, w0:w0 + 128],
                                xs_hl[:, buf * 1024:buf * 1024 + 512],
                                xs_hl[:, buf * 1024 + 512:buf * 1024 + 1024],
                                start=(dc == 0), stop=(dc == ndc - 1))
                        i.then_inc(sT, 1)
                    TE(mmp)
                    tcnt = C.bump(sT, 1)
                    C.c["xs_free%d" % buf] = tcnt
                tfin = C.val(sT)

                if mode == "vt":
                    def ev0(ve, tfin=tfin, bank=bank):
                        ve.wait_ge(sT, tfin)
                        ve.drain()
                        ve.tensor_copy(sc4[:], bank[:, 0:512]).then_inc(sV, 1)
                    VE(ev0)
                    vcnt = C.bump(sV, 1)
                    C.c["proj_bank_free%d" % (tt % 2)] = vcnt

                    def tr(te, vcnt=vcnt):
                        te.wait_ge(sV, vcnt)
                        te.wait_ge(sV, C.val("vt_ps_free"))
                        for j in range(4):
                            i = te.matmul(pb[6][:, j * 128:(j + 1) * 128],
                                          sc4[:, j * 128:(j + 1) * 128], idsb[:],
                                          is_transpose=True, start=(j == 0),
                                          stop=(j == 3), skip_group_check=True)
                        i.then_inc(sT, 1)
                    TE(tr)
                    tcnt2 = C.bump(sT, 1)

                    def ev1(ve, tt=tt, dsto=dsto, tcnt2=tcnt2):
                        ve.wait_ge(sT, tcnt2)
                        for j in range(4):
                            kt = tt * 4 + j
                            i = ve.tensor_copy(
                                arena[:, dsto * T + kt * 128:dsto * T + (kt + 1) * 128],
                                pb[6][:, j * 128:(j + 1) * 128])
                        i.then_inc(sV, 1)
                    VE(ev1)
                    C.c["vt_ps_free"] = C.bump(sV, 1)
                else:
                    scl = SCALE if mode == "qs" else 1.0

                    def ev(ve, tt=tt, dsto=dsto, scl=scl, tfin=tfin, bank=bank):
                        ve.wait_ge(sT, tfin)
                        ve.drain()
                        ve.tensor_scalar(
                            arena[:, dsto * T + tt * 512:dsto * T + tt * 512 + 512],
                            bank[:, 0:512], scl, 0.0, OP.mult,
                            OP.add).then_inc(sV, 1)
                    VE(ev)
                    C.c["proj_bank_free%d" % (tt % 2)] = C.bump(sV, 1)
            C.c["proj_banks_free"] = C.val(sV)
            C.c["wsh_rd%d" % (wbase // 2048)] = C.val(sT)

        def rope_pair():
            for (tabc, tabs, dof) in ((cosq2, sinq2, 6), (cosk2, sink2, 7)):
                apre = C.val("proj_banks_free")
                for cb in range(4):
                    vpre2 = C.val("rope_tab_rd")
                    c0 = dof * T + cb * 512

                    def ldt(sy, tabc=tabc, tabs=tabs, cb=cb, vpre2=vpre2, c0=c0,
                            apre=apre):
                        if vpre2:
                            sy.wait_ge(sV, vpre2)
                        if apre:
                            sy.wait_ge(sV, apre)
                        sy.dma_start(out=tabA[:],
                                     in_=tabc[:, cb * 512:(cb + 1) * 512]).then_inc(sR, 16)
                        sy.dma_start(out=tabB[:],
                                     in_=tabs[:, cb * 512:(cb + 1) * 512]).then_inc(sR, 16)
                        sy.dma_start(out=rotb[0:32, :],
                                     in_=arena[32:64, c0:c0 + 512]).then_inc(sR, 16)
                        sy.dma_start(out=rotb[32:64, :],
                                     in_=arena[0:32, c0:c0 + 512]).then_inc(sR, 16)
                        sy.dma_start(out=rotb[64:96, :],
                                     in_=arena[96:128, c0:c0 + 512]).then_inc(sR, 16)
                        sy.dma_start(out=rotb[96:128, :],
                                     in_=arena[64:96, c0:c0 + 512]).then_inc(sR, 16)
                    SY(ldt)
                    wcnt = C.bump("sR", 96)

                    def rp(ve, cb=cb, wcnt=wcnt, c0=c0):
                        ve.wait_ge(sR, wcnt)
                        ve.tensor_tensor(arena[:, c0:c0 + 512], arena[:, c0:c0 + 512],
                                         tabA[:], OP.mult)
                        ve.tensor_tensor(rotb[:], rotb[:], tabB[:], OP.mult)
                        ve.drain()
                        ve.tensor_tensor(arena[:, c0:c0 + 512], arena[:, c0:c0 + 512],
                                         rotb[:], OP.add).then_inc(sV, 1)
                    VE(rp)
                    C.bump(sV, 1)
                    C.c["rope_tab_rd"] = C.val(sV)

        def attention_head(sub, h, hh):
            qo = hh
            ko = 2 + hh
            vo = 4 + hh
            rr0 = hh * 64
            gate_pb = C.val("proj_banks_free")
            gate_rp = C.val("rope_tab_rd")
            for J in range(4):
                ntiles = 4 * J + 4
                for i in range(ntiles):
                    sc_bank = pb[i % 2]
                    diag = (sub == 1) and (i >= 4 * J)
                    use_mask = ((sub == 0) or diag) and dbg["attn_parts"] >= 2
                    mw = None
                    mb = i % 3
                    if use_mask:
                        vfree = C.val("msk_free%d" % mb)

                        def ldm(sy, i=i, J=J, mb=mb, vfree=vfree, sub=sub):
                            if vfree:
                                sy.wait_ge(sV, vfree)
                            if sub == 0:
                                src = mask_a[i * 128:(i + 1) * 128, J * 512:J * 512 + 512]
                            else:
                                ii = i - 4 * J
                                src = mask_c[ii * 128:(ii + 1) * 128, 0:512]
                            sy.dma_start(out=mskb[:, mb * 512:mb * 512 + 512],
                                         in_=src).then_inc(sM[mb], 16)
                        AE(ldm)
                        mw = C.bump("sM%d" % mb, 16)

                    apre = C.val("exp_done_t%d" % (i % 2))

                    def mmsc(te, i=i, J=J, sc_bank=sc_bank, qo=qo, ko=ko,
                             rr0=rr0, apre=apre, sub=sub, gate_pb=gate_pb,
                             gate_rp=gate_rp):
                        if apre:
                            te.wait_ge(sA, apre)
                        if i == 0 and J == 0:
                            if gate_pb:
                                te.wait_ge(sV, gate_pb)
                            if gate_rp:
                                te.wait_ge(sV, gate_rp)
                        ins = te.matmul(sc_bank[:, 0:512],
                                        arena[:, ko * T + i * 128:ko * T + (i + 1) * 128],
                                        arena[:, qo * T + J * 512:qo * T + J * 512 + 512],
                                        start=True,
                                        stop=(sub == 1), skip_group_check=True)
                        if sub == 0:
                            ins = te.matmul(sc_bank[:, 0:512],
                                            arena[rr0:rr0 + 64, 7 * T + i * 128:7 * T + (i + 1) * 128],
                                            arena[rr0:rr0 + 64, 6 * T + J * 512:6 * T + J * 512 + 512],
                                            start=False, stop=True, skip_group_check=True)
                        ins.then_inc(sT, 1)
                    TE(mmsc)
                    tcnt = C.bump(sT, 1)

                    if use_mask:
                        def mska(ve, sc_bank=sc_bank, mb=mb, tcnt=tcnt, mw=mw):
                            ve.wait_ge(sT, tcnt)
                            ve.wait_ge(sM[mb], mw)
                            ve.tensor_tensor(sc_bank[:, 0:512], sc_bank[:, 0:512],
                                             mskb[:, mb * 512:mb * 512 + 512],
                                             OP.add).then_inc(sV, 1)
                        VE(mska)
                        vcnt = C.bump(sV, 1)
                        C.c["msk_free%d" % mb] = vcnt
                        dep = ("V", vcnt)
                    else:
                        dep = ("T", tcnt)

                    if dbg["attn_parts"] < 3:
                        C.c["exp_done_t%d" % (i % 2)] = dep[1] if dep[0] == "V" else 0
                        if dep[0] != "V":
                            def drain(ve, sc_bank=sc_bank, tcnt=tcnt):
                                ve.wait_ge(sT, tcnt)
                                ve.tensor_copy(sc4[:], sc_bank[:, 0:512]).then_inc(sV, 1)
                            VE(drain)
                            C.c["exp_done_t%d" % (i % 2)] = 0
                            C.c["drain_v"] = C.bump(sV, 1)
                        continue
                    eb = i % 3
                    epre = C.val("exp_rd_%d" % eb)

                    def expf(ae, sc_bank=sc_bank, eb=eb, dep=dep, epre=epre):
                        if dep[0] == "V":
                            ae.wait_ge(sV, dep[1])
                        else:
                            ae.wait_ge(sT, dep[1])
                        if epre:
                            ae.wait_ge(sT, epre)
                        ae.activation(out=expb[:, eb * 512:eb * 512 + 512],
                                      in_=sc_bank[:, 0:512], func=AF.Exp).then_inc(sA, 1)
                    AE(expf)
                    acnt = C.bump(sA, 1)
                    C.c["exp_done_t%d" % (i % 2)] = acnt
                    C.c["exp_rd"] = acnt
                    if dbg["attn_parts"] < 4:
                        continue

                    def mmav(te, i=i, eb=eb, vo=vo, acnt=acnt, J=J,
                             avfree=C.val("avrs_free%d" % (J % 2))):
                        te.wait_ge(sA, acnt)
                        if i == 0 and avfree:
                            te.wait_ge(sV, avfree)
                        av_b, rs_b = (pb[2], pb[3]) if J % 2 == 0 else (pb[4], pb[5])
                        te.matmul(av_b[:, 0:512],
                                  arena[:, vo * T + i * 128:vo * T + (i + 1) * 128],
                                  expb[:, eb * 512:eb * 512 + 512],
                                  start=(i == 0), stop=False, skip_group_check=True)
                        i2 = te.matmul(rs_b[0:1, 0:512], ones[:, 0:1],
                                       expb[:, eb * 512:eb * 512 + 512],
                                       start=(i == 0), stop=False, skip_group_check=True)
                        i2.then_inc(sT, 1)
                    TE(mmav)
                    tcnt2 = C.bump(sT, 1)
                    C.c["exp_rd_%d" % eb] = tcnt2

                if dbg["attn_parts"] < 5:
                    continue
                tfin = C.val(sT)
                wship = C.val("sRS")

                def rsum(ve, tfin=tfin, wship=wship, J=J):
                    if wship:
                        ve.wait_ge(sRS, wship)
                    ve.wait_ge(sT, tfin)
                    ve.drain()
                    rs_b = pb[3] if J % 2 == 0 else pb[5]
                    ve.reciprocal(sc1[0:1, :], rs_b[0:1, 0:512]).then_inc(sV, 1)
                VE(rsum)
                vcnt = C.bump(sV, 1)

                def shiprs(sy, vcnt=vcnt):
                    sy.wait_ge(sV, vcnt)
                    sy.dma_start(out=rowd.ap()[2, 0:512], in_=sc1[0:1, :]).then_inc(sRS, 16)
                AE(shiprs)
                wcnt = C.bump("sRS", 16)
                gpre = C.val(sG)

                def bcrs(sy, wcnt=wcnt, gpre=gpre):
                    sy.wait_ge(sRS, wcnt)
                    if gpre:
                        sy.wait_ge(sG, gpre)
                    sy.dma_start(out=sc3[:],
                                 in_=rowd.ap()[2:3, 0:512].to_broadcast([128, 512])).then_inc(sRS, 16)
                AE(bcrs)
                wcnt2 = C.bump("sRS", 16)

                def norm(ve, wcnt2=wcnt2, gpre=C.val("sGP"), J=J):
                    ve.wait_ge(sRS, wcnt2)
                    if gpre:
                        ve.wait_ge(sGP, gpre)
                    ve.drain()
                    av_b = pb[2] if J % 2 == 0 else pb[4]
                    ve.tensor_tensor(sc4[:], av_b[:, 0:512], sc3[:],
                                     OP.mult).then_inc(sV, 1)
                VE(norm)
                vcnt2 = C.bump(sV, 1)
                C.c["avrs_free%d" % (J % 2)] = vcnt2
                C.c["avrs_free"] = vcnt2
                C.c["rs_ship"] = wcnt2

                ocnt = split_to(sc4[:], vcnt2, (sOP, C.val("sOP")))

                def oshp(sy, ocnt=ocnt, h=h, J=J):
                    sy.wait_ge(sV, ocnt)
                    sy.dma_start(out=opc.ap()[0, h * 128:(h + 1) * 128,
                                              J * 512:J * 512 + 512],
                                 in_=obh[:]).then_inc(sOP, 16)
                    sy.dma_start(out=opc.ap()[1, h * 128:(h + 1) * 128,
                                              J * 512:J * 512 + 512],
                                 in_=obl[:]).then_inc(sOP, 16)
                AE(oshp)
                C.bump("sOP", 32)

        def oproj_rs(sub):
            wo = wo_a if sub == 0 else wo_b
            gpre = C.val("sOP")
            for oc in range(16):
                wb = split_weights(wo, oc, 8)
                for tt in range(4):
                    for dc in range(8):
                        buf = dc % 3
                        tfree = C.val("xs_free%d" % buf)
                        wsplit = C.val("wsplit_dc%d" % dc)

                        def ldo(sy, dc=dc, buf=buf, tfree=tfree, tt=tt, gpre=gpre):
                            if tfree:
                                sy.wait_ge(sT, tfree)
                            sy.wait_ge(sOP, gpre)
                            src_ap = opc.ap()[:, dc * 128:(dc + 1) * 128,
                                              tt * 512:tt * 512 + 512].rearrange(
                                "h p c -> p h c")
                            dst_ap = xs_hl[:, buf * 1024:buf * 1024 + 1024].rearrange(
                                "p (h c) -> p h c", h=2)
                            sy.dma_start(out=dst_ap, in_=src_ap).then_inc(sX[buf], 16)
                        SY(ldo)
                        wcnt = C.bump("sX%d" % buf, 16)

                        def mmo(te, dc=dc, buf=buf, wcnt=wcnt, wsplit=wsplit,
                                wb=wb):
                            te.wait_ge(sX[buf], wcnt)
                            te.wait_ge(sV, wsplit[0])
                            te.wait_ge(sGP, wsplit[1])
                            if dc == 0:
                                te.wait_ge(sV, C.val("oproj_bank_free"))
                            i = mm3(te, pb[7][:, 0:512],
                                    wsh[:, wb + dc * 128:wb + dc * 128 + 128],
                                    wsl[:, wb + dc * 128:wb + dc * 128 + 128],
                                    xs_hl[:, buf * 1024:buf * 1024 + 512],
                                    xs_hl[:, buf * 1024 + 512:buf * 1024 + 1024],
                                    start=(dc == 0), stop=(dc == 7))
                            i.then_inc(sT, 1)
                        TE(mmo)
                        tcnt = C.bump(sT, 1)
                        C.c["xs_free%d" % buf] = tcnt

                    gfree = C.val("oev_shipped")

                    def evo(ve, tcnt=tcnt, gfree=gfree):
                        ve.wait_ge(sT, tcnt)
                        if gfree:
                            ve.wait_ge(sG, gfree)
                        ve.drain()
                        ve.tensor_copy(sc1[:], pb[7][:, 0:512]).then_inc(sV, 1)
                    VE(evo)
                    vcnt = C.bump(sV, 1)
                    C.c["oproj_bank_free"] = vcnt

                    def sho(gp, vcnt=vcnt, oc=oc, tt=tt):
                        gp.wait_ge(sV, vcnt)
                        gp.dma_start(out=rs_ins[oc // 4].ap()[tt // 2, (oc % 4) * 128:(oc % 4 + 1) * 128,
                                                              (tt % 2) * 512:(tt % 2) * 512 + 512],
                                     in_=sc1[:]).then_inc(sG, 16)
                    GE(sho)
                    C.c["oev_shipped"] = C.bump(sG, 16)
                C.c["wsh_rd%d" % (wb // 2048)] = C.val(sT)
                if oc % 4 == 3:
                    gq = C.val(sG)

                    def collq(gp, gq=gq, q=oc // 4):
                        gp.wait_ge(sG, gq)
                        gp.collective_compute(
                            "ReduceScatter", OP.add,
                            replica_groups=[[0, 1], [2, 3], [4, 5], [6, 7]],
                            ins=[rs_ins[q].ap()],
                            outs=[rs_outs[q].ap()],
                        ).then_inc(sCC, 1)
                    GE(collq)
                    C.c["rs_cc%d" % (oc // 4)] = C.bump(sCC, 1)

            for dc in range(16):
                vfree = C.val("res_free")
                cc2 = C.val("rs_cc%d" % (dc // 4))

                def ldr(sy, dc=dc, cc2=cc2, vfree=vfree):
                    if vfree:
                        sy.wait_ge(sV, vfree)
                    sy.wait_ge(sCC, cc2)
                    sy.dma_start(out=rsb[:, 0:512],
                                 in_=rs_outs[dc // 4].ap()[(dc % 4) * 128:(dc % 4 + 1) * 128,
                                                           0:512]).then_inc(sRES, 16)
                    sy.dma_start(out=rsb[:, 512:1024],
                                 in_=rs_outs[dc // 4].ap()[(dc % 4) * 128:(dc % 4 + 1) * 128,
                                                           512:1024]).then_inc(sRES, 16)
                SY(ldr)
                wcnt = C.bump("sRES", 32)

                def addr_(ve, dc=dc, wcnt=wcnt):
                    ve.wait_ge(sRES, wcnt)
                    ve.drain()
                    ve.tensor_tensor(xr[:, dc * TOK:dc * TOK + 512],
                                     xr[:, dc * TOK:dc * TOK + 512], rsb[:, 0:512], OP.add)
                    ve.tensor_tensor(xr[:, dc * TOK + 512:dc * TOK + 1024],
                                     xr[:, dc * TOK + 512:dc * TOK + 1024], rsb[:, 512:1024],
                                     OP.add).then_inc(sV, 1)
                VE(addr_)
                C.c["res_free"] = C.bump(sV, 1)
                C.c["res_dc%d" % dc] = C.val(sV)
            C.c["ln_gate"] = C.val(sV)

        def attn_a():
            if not dbg["ln"]:
                return
            ln_phase(gdst=0)
            cc = gather(0)
            for oc in range(4):
                wb = split_weights(wdkv, oc, 16)
                for tt in range(2):
                    bank = pb[2 + (oc % 2)]
                    bfree = C.val("ckv_bank_free%d" % (oc % 2))
                    lnv = [C.val("lnsp_v%d_%d" % (d, tt)) for d in range(16)]
                    lng = [C.val("lnsp_g%d_%d" % (d, tt)) for d in range(16)]
                    wsp = [C.val("wsplit_dc%d" % d) for d in range(16)]

                    def mmckv(te, tt=tt, wb=wb, bank=bank, bfree=bfree,
                              lnv=lnv, lng=lng, wsp=wsp):
                        if bfree:
                            te.wait_ge(sV, bfree)
                        for dc in range(16):
                            te.wait_ge(sV, max(wsp[dc][0], lnv[dc]))
                            te.wait_ge(sGP, max(wsp[dc][1], lng[dc]))
                            w0 = wb + dc * 128
                            i = mm3(te, bank[:, 0:512],
                                    wsh[:, w0:w0 + 128], wsl[:, w0:w0 + 128],
                                    AH(dc * TOK + tt * 512, 512),
                                    AL(dc * TOK + tt * 512, 512),
                                    start=(dc == 0), stop=(dc == 15))
                        i.then_inc(sT, 1)
                    TE(mmckv)
                    tcnt = C.bump(sT, 1)

                    def evc(ve, bank=bank, tcnt=tcnt, gpre=C.val("sGP")):
                        ve.wait_ge(sT, tcnt)
                        if gpre:
                            ve.wait_ge(sGP, gpre)
                        ve.drain()
                        ve.tensor_copy(sc4[:], bank[:, 0:512]).then_inc(sV, 1)
                    VE(evc)
                    vcnt = C.bump(sV, 1)
                    C.c["ckv_bank_free%d" % (oc % 2)] = vcnt

                    ocnt = split_to(sc4[:], vcnt, (sG, C.val(sG)))

                    def cshp(gp, ocnt=ocnt, oc=oc, tt=tt):
                        gp.wait_ge(sV, ocnt)
                        gp.dma_start(out=g1c_in.ap()[oc * 256:oc * 256 + 128,
                                                     tt * 512:tt * 512 + 512],
                                     in_=obh[:]).then_inc(sG, 16)
                        gp.dma_start(out=g1c_in.ap()[oc * 256 + 128:oc * 256 + 256,
                                                     tt * 512:tt * 512 + 512],
                                     in_=obl[:]).then_inc(sG, 16)
                    GE(cshp)
                    C.bump(sG, 32)
                C.c["wsh_rd%d" % (wb // 2048)] = C.val(sT)

            wq_next = [split_weights(wq_a, 0, 16), split_weights(wq_a, 1, 16)]
            ccc = gather(2)
            np_ = min(NHH // 2, dbg["pairs"])
            for hp in range(np_):
                stream_proj(gdst=0, ndc=16, row0=0, dst_mode=(0, "qs"),
                            cc_need=cc, wbase=wq_next[0])
                stream_proj(gdst=0, ndc=16, row0=0, dst_mode=(1, "qs"),
                            cc_need=cc, wbase=wq_next[1])
                wb = split_weights(wqp, hp, 16)
                stream_proj(gdst=0, ndc=16, row0=0, dst_mode=(6, "n"),
                            cc_need=cc, wbase=wb)
                wb = split_weights(wkp, hp, 16)
                stream_proj(gdst=0, ndc=16, row0=0, dst_mode=(7, "n"),
                            cc_need=cc, wbase=wb)
                wb = split_weights(wuk, 2 * hp, 4)
                stream_proj(gdst=2, ndc=4, row0=0, dst_mode=(2, "n"),
                            cc_need=ccc, wbase=wb)
                wb = split_weights(wuv, 2 * hp, 4)
                stream_proj(gdst=2, ndc=4, row0=0, dst_mode=(4, "vt"),
                            cc_need=ccc, wbase=wb)
                wb = split_weights(wuk, 2 * hp + 1, 4)
                stream_proj(gdst=2, ndc=4, row0=0, dst_mode=(3, "n"),
                            cc_need=ccc, wbase=wb)
                wb = split_weights(wuv, 2 * hp + 1, 4)
                stream_proj(gdst=2, ndc=4, row0=0, dst_mode=(5, "vt"),
                            cc_need=ccc, wbase=wb)
                if dbg["rope"]:
                    rope_pair()
                if hp + 1 < np_:
                    wq_next = [split_weights(wq_a, 2 * hp + 2, 16),
                               split_weights(wq_a, 2 * hp + 3, 16)]
                if dbg["attn"]:
                    for hh in range(dbg["heads"]):
                        attention_head(0, 2 * hp + hh, hh)
            C.c["ln_banks_free"] = C.val(sV)
            if dbg["oproj"]:
                oproj_rs(0)

        def attn_b():
            ln_phase(gdst=1)
            wq_next = [split_weights(wq_b, 0, 16), split_weights(wq_b, 1, 16)]
            cc = gather(1)
            for hp in range(NHH // 2):
                stream_proj(gdst=1, ndc=16, row0=0, dst_mode=(0, "qs"),
                            cc_need=cc, wbase=wq_next[0])
                stream_proj(gdst=1, ndc=16, row0=0, dst_mode=(1, "qs"),
                            cc_need=cc, wbase=wq_next[1])
                wb = split_weights(wk_b, 2 * hp, 16)
                stream_proj(gdst=1, ndc=16, row0=0, dst_mode=(2, "n"),
                            cc_need=cc, wbase=wb)
                wb = split_weights(wk_b, 2 * hp + 1, 16)
                stream_proj(gdst=1, ndc=16, row0=0, dst_mode=(3, "n"),
                            cc_need=cc, wbase=wb)
                wb = split_weights(wv_b, 2 * hp, 16)
                stream_proj(gdst=1, ndc=16, row0=0, dst_mode=(4, "vt"),
                            cc_need=cc, wbase=wb)
                wb = split_weights(wv_b, 2 * hp + 1, 16)
                stream_proj(gdst=1, ndc=16, row0=0, dst_mode=(5, "vt"),
                            cc_need=cc, wbase=wb)
                if hp + 1 < NHH // 2:
                    wq_next = [split_weights(wq_b, 2 * hp + 2, 16),
                               split_weights(wq_b, 2 * hp + 3, 16)]
                attention_head(1, 2 * hp, 0)
                attention_head(1, 2 * hp + 1, 1)
            C.c["ln_banks_free"] = C.val(sV)
            oproj_rs(1)

        def ffn():
            ln_phase(gdst=None)
            vpre = C.val("ln_split_done")

            def ldw2(sy):
                sy.wait_ge(sT, C.val("w2sb_rd"))
                for oc in range(16):
                    sy.dma_start(out=w2sb[:, oc * 8:(oc + 1) * 8],
                                 in_=w2[oc, :, :]).then_inc(sB, 16)
                    sy.dma_start(out=b1sb[:, oc:oc + 1],
                                 in_=b1[oc, :, :]).then_inc(sB, 16)
                sy.dma_start(out=b2sb[:, :], in_=b2[:, :]).then_inc(sB, 16)
            SY(ldw2)
            w2w = C.bump("sB", 16 * 33)

            for oc in range(16):
                wb = split_weights(w1, oc, 16)
                for tt in range(2):
                    if True:
                        slot = oc % 2
                        ocp = oc // 2
                        hfree = C.val("h_bank_free%d" % slot)

                        def mmh(te, tt=tt, slot=slot, oc=oc, wb=wb,
                                hfree=hfree,
                                wsp=[C.val("wsplit_dc%d" % d) for d in range(16)],
                                lnv=[C.val("lnsp_v%d_%d" % (d, tt)) for d in range(16)],
                                lng=[C.val("lnsp_g%d_%d" % (d, tt)) for d in range(16)]):
                            if hfree:
                                te.wait_ge(sA, hfree)
                            if oc == 0 and tt == 0:
                                te.wait_ge(sV, C.val("ln_banks_free"))
                                te.wait_ge(sV, C.val("proj_banks_free"))
                                te.wait_ge(sA, C.val("exp_rd"))
                                te.wait_ge(sV, C.val("avrs_free"))
                            for dc in range(16):
                                te.wait_ge(sV, max(wsp[dc][0], lnv[dc]))
                                te.wait_ge(sGP, max(wsp[dc][1], lng[dc]))
                                w0 = wb + dc * 128
                                i = mm3(te, pb[slot][:, 0:512],
                                        wsh[:, w0:w0 + 128], wsl[:, w0:w0 + 128],
                                        AH(dc * TOK + tt * 512, (dc * TOK + tt * 512 + 512) - (dc * TOK + tt * 512)),
                                        AL(dc * TOK + tt * 512, (dc * TOK + tt * 512 + 512) - (dc * TOK + tt * 512)),
                                        start=(dc == 0), stop=(dc == 15))
                            i.then_inc(sT, 1)
                        TE(mmh)
                        tcnt = C.bump(sT, 1)

                        def rel(ae, slot=slot, oc=oc, tcnt=tcnt, w2w=w2w):
                            ae.wait_ge(sT, tcnt)
                            ae.wait_ge(sT, C.val("w2_rd%d" % slot))
                            ae.wait_ge(sB, w2w)
                            ae.activation(out=hbuf[:, slot * 512:slot * 512 + 512],
                                          in_=pb[slot][:, 0:512], func=AF.Relu,
                                          bias=b1sb[:, oc:oc + 1], scale=1.0).then_inc(sA, 1)
                        AE(rel)
                        acnt = C.bump(sA, 1)
                        C.c["h_bank_free%d" % slot] = acnt

                        def mm2(te, slot=slot, oc=oc, tt=tt, acnt=acnt,
                                w2w=w2w):
                            te.wait_ge(sA, acnt)
                            if oc == 0:
                                te.wait_ge(sV, C.val("alpha_bank_free"))
                                te.wait_ge(sV, C.val("ln_banks_free"))
                            te.wait_ge(sB, w2w)
                            i = te.matmul(pb[4 + tt][0:8, 0:512],
                                          w2sb[:, oc * 8:(oc + 1) * 8],
                                          hbuf[:, slot * 512:slot * 512 + 512],
                                          start=(oc == 0),
                                          stop=(oc == 15),
                                          skip_group_check=True)
                            i.then_inc(sT, 1)
                        TE(mm2)
                        tcnt2 = C.bump(sT, 1)
                        C.c["w2_rd%d" % slot] = tcnt2
                C.c["wsh_rd%d" % (wb // 2048)] = C.val(sT)
            C.c["w2sb_rd"] = C.val(sT)
            C.c["wsh_rd"] = C.val(sT)

            tfin = C.val(sT)
            for tt in range(2):
                def aev(ve, tt=tt, tfin=tfin):
                    ve.wait_ge(sT, tfin)
                    ve.tensor_scalar(hbuf[0:8, tt * 512:tt * 512 + 512],
                                     pb[4 + tt][0:8, 0:512], b2sb[:, :], 0.0,
                                     OP.add, OP.add).then_inc(sV, 1)
                VE(aev)
                vcnt = C.bump(sV, 1)

                def aex(ae, tt=tt, vcnt=vcnt):
                    ae.wait_ge(sV, vcnt)
                    ae.activation(out=hbuf[0:8, tt * 512:tt * 512 + 512],
                                  in_=hbuf[0:8, tt * 512:tt * 512 + 512],
                                  func=AF.Exp).then_inc(sA, 1)
                AE(aex)
                acnt = C.bump(sA, 1)

                vt_free = C.val("vt_ps_free")

                def asum(te, tt=tt, acnt=acnt, vt_free=vt_free):
                    te.wait_ge(sA, acnt)
                    if vt_free:
                        te.wait_ge(sV, vt_free)
                    i = te.matmul(pb[6][0:1, 0:512],
                                  ones[0:8, 0:1], hbuf[0:8, tt * 512:tt * 512 + 512],
                                  start=True, stop=True,
                                  skip_group_check=True)
                    i.then_inc(sT, 1)
                TE(asum)
                tcnt = C.bump(sT, 1)

                lnpre = C.val("sLN")

                def arc(ve, tt=tt, tcnt=tcnt, lnpre=lnpre):
                    ve.wait_ge(sT, tcnt)
                    if lnpre:
                        ve.wait_ge(sLN, lnpre)
                    ve.drain()
                    ve.reciprocal(sc1[0:1, :], pb[6][0:1, 0:512]).then_inc(sV, 1)
                VE(arc)
                vcnt2 = C.bump(sV, 1)
                C.c["vt_ps_free"] = vcnt2


                def aship(sy, tt=tt, vcnt2=vcnt2):
                    sy.wait_ge(sV, vcnt2)
                    sy.dma_start(out=rowd.ap()[3, tt * 512:tt * 512 + 512],
                                 in_=sc1[0:1, :]).then_inc(sLN, 16)
                SY(aship)
                C.bump("sLN", 16)

            wcnt = C.val("sLN")

            vgate2 = C.val(sV)

            def abc(sy, wcnt=wcnt, vgate2=vgate2):
                sy.wait_ge(sLN, wcnt)
                if vgate2:
                    sy.wait_ge(sV, vgate2)
                sy.dma_start(out=rowb[0:8, :],
                             in_=rowd.ap()[3:4, 0:TOK].to_broadcast([8, TOK])).then_inc(sLN, 16)
            SY(abc)
            wcnt2 = C.bump("sLN", 16)

            def anrm(ve, wcnt2=wcnt2):
                ve.wait_ge(sLN, wcnt2)
                ve.tensor_tensor(hbuf[0:8, 0:1024], hbuf[0:8, 0:1024], rowb[0:8, :],
                                 OP.mult).then_inc(sV, 1)
            VE(anrm)
            vcnt3 = C.bump(sV, 1)

            def aout(sy, vcnt3=vcnt3):
                sy.wait_ge(sV, vcnt3)
                sy.dma_start(out=alpd.ap()[:, :], in_=hbuf[0:8, 0:1024]).then_inc(sLN, 16)
            SY(aout)
            alpw = C.bump("sLN", 16)
            C.c["alpha_bank_free"] = vcnt3
            C.c["ln_banks_free"] = vcnt3

            for ph in range(2):
                vx_wb = [split_weights(v2w, 2 * ph, 16),
                         split_weights(v2w, 2 * ph + 1, 16)]
                for ci in range(2):
                    c = 2 * ph + ci
                    vbpre = C.val("alpb_rd")

                    def abr(sy, c=c, alpw=alpw, vbpre=vbpre):
                        sy.wait_ge(sLN, alpw)
                        if vbpre:
                            sy.wait_ge(sV, vbpre)
                        sy.dma_start(out=rowb[0:64, :],
                                     in_=alpd.ap()[2 * c:2 * c + 1, :].to_broadcast([64, TOK])).then_inc(sLN, 16)
                        sy.dma_start(out=rowb[64:128, :],
                                     in_=alpd.ap()[2 * c + 1:2 * c + 2, :].to_broadcast([64, TOK])).then_inc(sLN, 16)
                    SY(abr)
                    abw = C.bump("sLN", 32)

                    for tt in range(2):
                        bank = pb[ci]
                        bfree = C.val("vx_bank_free%d" % ci)

                        def mmvx(te, ci=ci, tt=tt, wb=vx_wb[ci],
                                 bank=bank, bfree=bfree,
                                 wsp=[C.val("wsplit_dc%d" % d) for d in range(16)],
                                 lnv=[C.val("lnsp_v%d_%d" % (d, tt)) for d in range(16)],
                                 lng=[C.val("lnsp_g%d_%d" % (d, tt)) for d in range(16)]):
                            if bfree:
                                te.wait_ge(sV, bfree)
                            for dc in range(16):
                                te.wait_ge(sV, max(wsp[dc][0], lnv[dc]))
                                te.wait_ge(sGP, max(wsp[dc][1], lng[dc]))
                                w0 = wb + dc * 128
                                i = mm3(te, bank[:, 0:512],
                                        wsh[:, w0:w0 + 128], wsl[:, w0:w0 + 128],
                                        AH(dc * TOK + tt * 512, 512),
                                        AL(dc * TOK + tt * 512, 512),
                                        start=(dc == 0), stop=(dc == 15))
                            i.then_inc(sT, 1)
                        TE(mmvx)
                        tcnt = C.bump(sT, 1)

                        def evx(ve, ci=ci, tt=tt, bank=bank, tcnt=tcnt, abw=abw,
                                gpre=C.val("sGP")):
                            ve.wait_ge(sT, tcnt)
                            ve.wait_ge(sLN, abw)
                            if gpre:
                                ve.wait_ge(sGP, gpre)
                            ve.drain()
                            ve.tensor_tensor(sc4[:], bank[:, 0:512],
                                             rowb[:, tt * 512:tt * 512 + 512],
                                             OP.mult).then_inc(sV, 1)
                        VE(evx)
                        vcnt = C.bump(sV, 1)
                        C.c["vx_bank_free%d" % ci] = vcnt

                        def vhi(ve, ci=ci, tt=tt, vcnt=vcnt):
                            ve.wait_ge(sV, vcnt)
                            ve.tensor_copy(
                                vxh[:, ci * TOK + tt * 512:ci * TOK + tt * 512 + 512],
                                sc4[:]).then_inc(sV, 1)
                        VE(vhi)
                        vcnt2 = C.bump(sV, 1)

                        def vlo(gp, ci=ci, tt=tt, vcnt2=vcnt2):
                            gp.wait_ge(sV, vcnt2)
                            gp.tensor_tensor(
                                vxl[:, ci * TOK + tt * 512:ci * TOK + tt * 512 + 512],
                                sc4[:],
                                vxh[:, ci * TOK + tt * 512:ci * TOK + tt * 512 + 512],
                                OP.subtract).then_inc(sGP, 1)
                        GE(vlo)
                        C.bump("sGP", 1)
                    C.c["alpb_rd"] = C.val(sV)
                    C.c["wsh_rd%d" % (vx_wb[ci] // 2048)] = C.val(sT)
                vx_done = (C.val(sV), C.val("sGP"))

                for oc in range(16):
                    u_wb = split_weights(u2w, oc, 4)
                    wsplit = C.val("wsplit_dc3")
                    for tt in range(2):
                        bfree = C.val("oproj_bank_free")

                        def mmu(te, oc=oc, tt=tt, ph=ph, wsplit=wsplit,
                                vx_done=vx_done, bfree=bfree, u_wb=u_wb):
                            te.wait_ge(sV, wsplit[0])
                            te.wait_ge(sGP, wsplit[1])
                            te.wait_ge(sV, vx_done[0])
                            te.wait_ge(sGP, vx_done[1])
                            if bfree:
                                te.wait_ge(sV, bfree)
                            for ci in range(2):
                                i = mm3(te, pb[7][:, 0:512],
                                        wsh[:, u_wb + (2 * ph + ci) * 128:u_wb + (2 * ph + ci) * 128 + 128],
                                        wsl[:, u_wb + (2 * ph + ci) * 128:u_wb + (2 * ph + ci) * 128 + 128],
                                        vxh[:, ci * TOK + tt * 512:ci * TOK + tt * 512 + 512],
                                        vxl[:, ci * TOK + tt * 512:ci * TOK + tt * 512 + 512],
                                        start=(ci == 0), stop=(ci == 1))
                            i.then_inc(sT, 1)
                        TE(mmu)
                        tcnt = C.bump(sT, 1)

                        def uadd(ve, oc=oc, tt=tt, tcnt=tcnt):
                            ve.wait_ge(sT, tcnt)
                            ve.tensor_tensor(xr[:, oc * TOK + tt * 512:oc * TOK + tt * 512 + 512],
                                             xr[:, oc * TOK + tt * 512:oc * TOK + tt * 512 + 512],
                                             pb[7][:, 0:512], OP.add).then_inc(sV, 1)
                        VE(uadd)
                        C.c["oproj_bank_free"] = C.bump(sV, 1)
                        if ph == 1 and tt == 1:
                            C.c["res_dc%d" % oc] = C.val(sV)
                    C.c["wsh_rd%d" % (u_wb // 2048)] = C.val(sT)
            C.c["ln_gate"] = C.val(sV)
            C.c["proj_banks_free"] = C.val(sV)

        for dc in range(16):
            def ld(sy, dc=dc):
                sy.dma_start(out=xr[:, dc * TOK:(dc + 1) * TOK],
                             in_=x0[dc, :, :]).then_inc(sW, 16)
            SY(ld)
            C.bump(sW, 16)

        def ldid(sy):
            sy.dma_start(out=idsb[:], in_=ident[:, :]).then_inc(sW, 16)
        SY(ldid)
        C.bump(sW, 16)
        wcnt0 = C.val(sW)

        def mkones(ve, wcnt0=wcnt0):
            ve.wait_ge(sW, wcnt0)
            ve.memset(ones[:], 1.0).then_inc(sV, 1)
        VE(mkones)
        C.c["ln_gate"] = C.bump(sV, 1)

        if n_sub >= 1:
            attn_a()
        if n_sub >= 2:
            ffn()
        if n_sub >= 3:
            attn_b()
        if n_sub >= 4:
            ffn()

        vfin, afin, tfin = C.val(sV), C.val(sA), C.val(sT)

        def store(sy, vfin=vfin, afin=afin, tfin=tfin):
            sy.wait_ge(sV, vfin)
            sy.wait_ge(sA, afin)
            sy.wait_ge(sT, tfin)
            for dc in range(16):
                sy.dma_start(out=y[dc, :, :],
                             in_=xr[:, dc * TOK:(dc + 1) * TOK]).then_inc(sOUT, 16)
        SY(store)
        wfin = C.bump("sOUT", 16 * 16)

        def fin(gp, wfin=wfin):
            gp.wait_ge(sOUT, wfin)
        GE(fin)

    return nc


def kernel(**inputs):
    cst = _host_consts()
    xin = np.asarray(inputs["L0_attn_a_x"], dtype=np.float32)

    def core_inputs(core):
        b, p = core // 2, core % 2
        d = {}
        xo = np.ascontiguousarray(xin[b, p * TOK:(p + 1) * TOK, :].T)
        d["x0"] = np.ascontiguousarray(xo.reshape(16, 128, TOK))
        for k in ("cosq2", "sinq2", "cosk2", "sink2", "ident"):
            d[k] = cst[k]
        d["mask_a"] = cst["mask_a"]
        d["mask_c"] = cst["mask_c"]
        hs = slice(p * 1024, (p + 1) * 1024)
        rs = slice(p * 512, (p + 1) * 512)
        d["wq_a"] = _chunked(np.asarray(inputs["L0_attn_a_w_q"])[:, hs])
        d["wdkv"] = _chunked(np.asarray(inputs["L0_attn_a_w_dkv"]))
        d["wuk"] = _chunked(np.asarray(inputs["L0_attn_a_w_uk"])[:, hs])
        d["wuv"] = _chunked(np.asarray(inputs["L0_attn_a_w_uv"])[:, hs])
        d["wqp"] = _chunked(np.asarray(inputs["L0_attn_a_w_q_pos"])[:, rs])
        d["wkp"] = _chunked(np.asarray(inputs["L0_attn_a_w_k_pos"])[:, rs])
        d["wo_a"] = _chunked(np.asarray(inputs["L0_attn_a_w_o"])[hs, :])
        d["wq_b"] = _chunked(np.ascontiguousarray(np.asarray(inputs["L0_attn_b_w_q"])[hs, :].T))
        d["wk_b"] = _chunked(np.ascontiguousarray(np.asarray(inputs["L0_attn_b_w_k"])[hs, :].T))
        d["wv_b"] = _chunked(np.ascontiguousarray(np.asarray(inputs["L0_attn_b_w_v"])[hs, :].T))
        d["wo_b"] = _chunked(np.ascontiguousarray(np.asarray(inputs["L0_attn_b_w_o"])[:, hs].T))
        d["w1"] = _chunked(np.asarray(inputs["L0_ffn_W1"]))
        d["b1"] = np.asarray(inputs["L0_ffn_b1"], dtype=np.float32).reshape(16, 128, 1)
        d["w2"] = np.asarray(inputs["L0_ffn_W2"], dtype=np.float32).reshape(16, 128, 8)
        d["b2"] = np.asarray(inputs["L0_ffn_b2"], dtype=np.float32).reshape(8, 1)
        V = np.asarray(inputs["L0_ffn_V"])
        U = np.asarray(inputs["L0_ffn_U"])
        V2 = np.ascontiguousarray(np.transpose(V, (2, 0, 1)).reshape(2048, 512))
        U2 = np.ascontiguousarray(np.transpose(U, (0, 2, 1)).reshape(512, 2048))
        d["v2w"] = _chunked(V2)
        d["u2w"] = _chunked(U2)
        return {k: (np.ascontiguousarray(v) if k in ("mask_a", "mask_c")
                    else np.ascontiguousarray(v, dtype=np.float32))
                for k, v in d.items()}

    nc = build_program(n_sub=N_SUB, dbg=DBG)
    in_maps = [core_inputs(c) for c in range(8)]
    res = run_bass_kernel_spmd(nc, in_maps, core_ids=list(range(8)))
    out = np.empty((B, T, D), dtype=np.float32)
    for c in range(8):
        b, p = c // 2, c % 2
        yc = np.asarray(res.results[c]["y"]).reshape(2048, TOK)
        out[b, p * TOK:(p + 1) * TOK, :] = yc.T
    return out


# version 09
# speedup vs baseline: 1.1513x; 1.1513x over previous
import numpy as np
import concourse.bass as bass
import concourse.mybir as mybir
from concourse.bass_utils import run_bass_kernel_spmd

F32 = mybir.dt.float32
BF16 = mybir.dt.bfloat16
AF = mybir.ActivationFunctionType
OP = mybir.AluOpType

B, T, D = 4, 2048, 2048
H, DH, DR, DL = 16, 128, 64, 512
NHH = H // 2
TOK = T // 2
SCALE = float(1.0 / np.sqrt(DH))
NEG = -1.0e30

N_SUB = 4
DBG = {"ln": True, "ckv": True, "gather": True, "pairs": 4, "heads": 2,
       "rope": True, "attn": True, "oproj": True, "attn_parts": 5}


def _host_consts():
    BASE, MAX_SEQ, ORIG_MAX = 10000.0, 131072, 4096
    BETA_FAST, BETA_SLOW = 32.0, 1.0
    SCALE_ROPE = 1.0 / np.sqrt(DR)
    base_inv = 1.0 / BASE ** (np.arange(0, DR, 2, dtype=np.float64) / DR)
    scale = MAX_SEQ / ORIG_MAX
    freqs = np.arange(DR // 2, dtype=np.float64)
    ramp = np.clip((freqs - BETA_SLOW) / (BETA_FAST - BETA_SLOW), 0.0, 1.0)
    inv = base_inv * (1 - ramp) + base_inv / scale * ramp
    yarn = 0.1 * np.log(MAX_SEQ / ORIG_MAX) + 1.0
    t = np.arange(T, dtype=np.float64)
    fr = t[None, :] * inv[:, None]
    emb = np.concatenate([fr, fr], axis=0)
    cos, sin = np.cos(emb), np.sin(emb)
    cosq = (cos * yarn * SCALE_ROPE).astype(np.float32)
    sinq = (sin * yarn * SCALE_ROPE).astype(np.float32)
    cosk = (cos * yarn).astype(np.float32)
    sink = (sin * yarn).astype(np.float32)
    def fold(s):
        f = s.copy()
        f[0:32] = -f[0:32]
        return f
    cosq2 = np.concatenate([cosq, cosq], axis=0)
    sinq2 = np.concatenate([fold(sinq), fold(sinq)], axis=0)
    cosk2 = np.concatenate([cosk, cosk], axis=0)
    sink2 = np.concatenate([fold(sink), fold(sink)], axis=0)

    kk = np.arange(T)[:, None]
    qq = np.arange(T)[None, :]
    dist = qq - kk
    um = (((dist >= -256) & (dist <= 256)) | (kk % 64 == 0) | (kk < 128)) & (kk <= qq)
    import ml_dtypes
    mask_a = np.where(um, np.float32(0.0), np.float32(NEG)).astype(ml_dtypes.bfloat16)
    kk5 = np.arange(512)[:, None]
    qq5 = np.arange(512)[None, :]
    mask_c = np.where(kk5 <= qq5, np.float32(0.0), np.float32(NEG)).astype(ml_dtypes.bfloat16)
    ident = np.eye(128, dtype=np.float32)
    return dict(cosq2=cosq2, sinq2=sinq2, cosk2=cosk2, sink2=sink2,
                mask_a=mask_a, mask_c=mask_c, ident=ident)


def _chunked(w):
    Din, Dout = w.shape
    n_dc, n_oc = Din // 128, Dout // 128
    return np.ascontiguousarray(
        w.reshape(n_dc, 128, n_oc, 128).transpose(2, 0, 1, 3))


class Ctr:
    def __init__(self):
        self.c = {}

    def bump(self, key, n=1):
        self.c[key] = self.c.get(key, 0) + n
        return self.c[key]

    def val(self, key):
        return self.c.get(key, 0)


def build_program(n_sub=4, dbg=None):
    dbg = dbg or DBG
    nc = bass.Bass("TRN2", target_bir_lowering=False)

    def inp(name, shape):
        return nc.dram_tensor(name, list(shape), F32, kind="ExternalInput").ap()

    x0 = inp("x0", (16, 128, TOK))
    cosq2 = inp("cosq2", (128, T))
    sinq2 = inp("sinq2", (128, T))
    cosk2 = inp("cosk2", (128, T))
    sink2 = inp("sink2", (128, T))
    mask_a = nc.dram_tensor("mask_a", [T, T], BF16, kind="ExternalInput").ap()
    mask_c = nc.dram_tensor("mask_c", [512, 512], BF16, kind="ExternalInput").ap()
    ident = inp("ident", (128, 128))
    wq_a = inp("wq_a", (NHH, 16, 128, 128))
    wdkv = inp("wdkv", (4, 16, 128, 128))
    wuk = inp("wuk", (NHH, 4, 128, 128))
    wuv = inp("wuv", (NHH, 4, 128, 128))
    wqp = inp("wqp", (4, 16, 128, 128))
    wkp = inp("wkp", (4, 16, 128, 128))
    wo_a = inp("wo_a", (16, 8, 128, 128))
    wq_b = inp("wq_b", (NHH, 16, 128, 128))
    wk_b = inp("wk_b", (NHH, 16, 128, 128))
    wv_b = inp("wv_b", (NHH, 16, 128, 128))
    wo_b = inp("wo_b", (16, 8, 128, 128))
    w1 = inp("w1", (16, 16, 128, 128))
    b1 = inp("b1", (16, 128, 1))
    w2 = inp("w2", (16, 128, 8))
    b2 = inp("b2", (8, 1))
    v2w = inp("v2w", (4, 16, 128, 128))
    u2w = inp("u2w", (16, 4, 128, 128))

    y = nc.dram_tensor("y", [16, 128, TOK], F32, kind="ExternalOutput").ap()

    g1_in = nc.dram_tensor("g1_in", [4096, TOK], BF16)
    g1_out = nc.dram_tensor("g1_out", [2, 4096, TOK], BF16)
    g1c_in = nc.dram_tensor("g1c_in", [1024, TOK], BF16)
    g1c_out = nc.dram_tensor("g1c_out", [2, 1024, TOK], BF16)
    g2_in = nc.dram_tensor("g2_in", [4096, TOK], BF16)
    g2_out = nc.dram_tensor("g2_out", [2, 4096, TOK], BF16)
    rs_ins = [nc.dram_tensor(f"rs_in{i}", [2, 512, TOK], F32) for i in range(4)]
    rs_outs = [nc.dram_tensor(f"rs_out{i}", [512, TOK], F32) for i in range(4)]
    opc = nc.dram_tensor("opc", [2, NHH * 128, T], BF16)
    rowd = nc.dram_tensor("rowd", [8, T], F32)
    alpd = nc.dram_tensor("alpd", [8, TOK], F32)

    gidx = {0: (g1_in, g1_out), 1: (g2_in, g2_out), 2: (g1c_in, g1c_out)}
    C = Ctr()

    from contextlib import ExitStack
    with ExitStack() as ex:
        def sbuf(name, shape, dt=F32):
            return ex.enter_context(nc.sbuf_tensor(name, list(shape), dt))

        xr = sbuf("xr", (128, 16 * TOK))
        arena = sbuf("arena", (128, 16 * TOK))
        def AH(c0, n):
            return arena[:, c0 // 2:(c0 + n) // 2].bitcast(BF16)

        def AL(c0, n):
            return arena[:, 8192 + c0 // 2:8192 + (c0 + n) // 2].bitcast(BF16)

        pq0 = arena[:, 0 * T:1 * T]
        pq1 = arena[:, 1 * T:2 * T]
        pk0 = arena[:, 2 * T:3 * T]
        pk1 = arena[:, 3 * T:4 * T]
        pv0 = arena[:, 4 * T:5 * T]
        pv1 = arena[:, 5 * T:6 * T]
        pqr = arena[:, 6 * T:7 * T]
        pkr = arena[:, 7 * T:8 * T]
        wsh = sbuf("wsh", (128, 2 * 16 * 128), BF16)
        wsl = sbuf("wsl", (128, 2 * 16 * 128), BF16)
        wf32 = sbuf("wf32", (128, 2 * 512), F32)
        xs_hl = sbuf("xs_hl", (128, 3 * 1024), BF16)
        expb = sbuf("expb", (128, 3 * 512), F32)
        mskb = sbuf("mskb", (128, 3 * 512), BF16)
        tabA = sbuf("tabA", (128, 512))
        tabB = sbuf("tabB", (128, 512))
        rotb = sbuf("rotb", (128, 512))
        sc1 = sbuf("sc1", (128, 512))
        sc2 = sbuf("sc2", (128, 512))
        sc3 = sbuf("sc3", (128, 512))
        sc4 = sbuf("sc4", (128, 512))
        rsb = sbuf("rsb", (128, 1024))
        obh = sbuf("obh", (128, 512), BF16)
        obl = sbuf("obl", (128, 512), BF16)
        rowb = sbuf("rowb", (128, 1024))
        ones = sbuf("ones", (128, 8))
        idsb = sbuf("idsb", (128, 128))
        stat = sbuf("stat", (128, TOK))
        dmy = sbuf("dmy", (1, 8))
        hbuf = sbuf("hbuf", (128, 2 * 512))
        w2sb = sbuf("w2sb", (128, 16 * 8))
        b1sb = sbuf("b1sb", (128, 16))
        b2sb = sbuf("b2sb", (8, 1))
        vxh = sbuf("vxh", (128, 2 * TOK), BF16)
        vxl = sbuf("vxl", (128, 2 * TOK), BF16)

        pb = [ex.enter_context(nc.psum_tensor(f"pb{i}", [128, 512], F32))
              for i in range(8)]

        sW = ex.enter_context(nc.semaphore("sW"))
        sG = ex.enter_context(nc.semaphore("sG"))
        sT = ex.enter_context(nc.semaphore("sT"))
        sV = ex.enter_context(nc.semaphore("sV"))
        sA = ex.enter_context(nc.semaphore("sA"))
        sCC = ex.enter_context(nc.semaphore("sCC"))
        sF = [ex.enter_context(nc.semaphore("sF0")),
              ex.enter_context(nc.semaphore("sF1"))]
        sX = [ex.enter_context(nc.semaphore("sX0")),
              ex.enter_context(nc.semaphore("sX1")),
              ex.enter_context(nc.semaphore("sX2"))]
        sM = [ex.enter_context(nc.semaphore("sM0")),
              ex.enter_context(nc.semaphore("sM1")),
              ex.enter_context(nc.semaphore("sM2"))]
        sLN = ex.enter_context(nc.semaphore("sLN"))
        sR = ex.enter_context(nc.semaphore("sR"))
        sRS = ex.enter_context(nc.semaphore("sRS"))
        sRES = ex.enter_context(nc.semaphore("sRES"))
        sOUT = ex.enter_context(nc.semaphore("sOUT"))
        sB = ex.enter_context(nc.semaphore("sB"))
        sGP = ex.enter_context(nc.semaphore("sGP"))
        sOP = ex.enter_context(nc.semaphore("sOP"))

        block = ex.enter_context(nc.Block())
        TE, VE, AE, GE, SY = (block.tensor, block.vector, block.scalar,
                              block.gpsimd, block.sync)

        REG = {"cur": 0}

        def split_weights(wdram, oc, ndc):
            REG["cur"] ^= 1
            wbase = REG["cur"] * 2048
            te_pre = C.val("wsh_rd%d" % REG["cur"])
            for dc0 in range(0, ndc, 4):
                g = min(4, ndc - dc0)
                buf = (dc0 // 4) % 2
                vfree = C.val("wf_freeV%d" % buf)
                gfree = C.val("wf_freeG%d" % buf)
                first = (dc0 == 0)

                def ld(sy, wdram=wdram, oc=oc, dc0=dc0, g=g, buf=buf,
                       vfree=vfree, gfree=gfree, first=first, te_pre=te_pre):
                    if first and te_pre:
                        sy.wait_ge(sT, te_pre)
                    if vfree:
                        sy.wait_ge(sV, vfree)
                    if gfree:
                        sy.wait_ge(sGP, gfree)
                    src_ap = wdram[oc, dc0:dc0 + g, :, :].rearrange("j p c -> p j c")
                    dst_ap = wf32[:, buf * 512:buf * 512 + g * 128].rearrange(
                        "p (j c) -> p j c", j=g)
                    sy.dma_start(out=dst_ap, in_=src_ap).then_inc(sF[buf], 16)
                SY(ld)
                wcnt = C.bump("sF%d" % buf, 16)
                dst = wbase + dc0 * 128

                def hi(ve, buf=buf, dst=dst, wcnt=wcnt, g=g):
                    ve.wait_ge(sF[buf], wcnt)
                    ve.tensor_copy(wsh[:, dst:dst + g * 128],
                                   wf32[:, buf * 512:buf * 512 + g * 128]
                                   ).then_inc(sV, 1)
                VE(hi)
                vcnt = C.bump(sV, 1)
                C.c["wf_freeV%d" % buf] = vcnt

                def lo(gp, buf=buf, dst=dst, vcnt=vcnt, g=g):
                    gp.wait_ge(sV, vcnt)
                    gp.tensor_tensor(
                        wsl[:, dst:dst + g * 128],
                        wf32[:, buf * 512:buf * 512 + g * 128],
                        wsh[:, dst:dst + g * 128],
                        OP.subtract).then_inc(sGP, 1)
                GE(lo)
                gcnt = C.bump("sGP", 1)
                C.c["wf_freeG%d" % buf] = gcnt
                for dc in range(dc0, dc0 + g):
                    C.c["wsplit_dc%d" % dc] = (C.val(sV), C.val("sGP"))
            return wbase

        def mm3(te, psum_ap, wh, wl, rh, rl, start, stop):
            te.matmul(psum_ap, wh, rh, start=start, stop=False,
                      skip_group_check=True)
            te.matmul(psum_ap, wh, rl, start=False, stop=False,
                      skip_group_check=True)
            return te.matmul(psum_ap, wl, rh, start=False, stop=stop,
                             skip_group_check=True)

        def split_to(src_ap, gate_v, ship_sem):
            sem, spre = ship_sem

            def hi(ve, gate_v=gate_v, spre=spre, sem=sem):
                ve.wait_ge(sV, gate_v)
                if spre:
                    ve.wait_ge(sem, spre)
                ve.drain()
                ve.tensor_copy(obh[:], src_ap)
                ve.drain()
                ve.scalar_tensor_tensor(obl[:], src_ap, 1.0, obh[:],
                                        OP.mult, OP.subtract).then_inc(sV, 1)
            VE(hi)
            return C.bump(sV, 1)

        def ln_phase(gdst):
            for blk in range(2):
                t0 = blk * 512
                for dc in range(16):
                    sqb = (sc1, sc2)[dc % 2]
                    tpre = C.val("lnsq_free%d" % (dc % 2))
                    gate_dc = C.val("res_dc%d" % dc) or C.val("ln_gate")

                    def sq(ve, dc=dc, t0=t0, sqb=sqb, tpre=tpre,
                           gate=gate_dc):
                        if tpre:
                            ve.wait_ge(sT, tpre)
                        if gate:
                            ve.wait_ge(sV, gate)
                        ve.drain()
                        ve.tensor_tensor(sqb[:],
                                         xr[:, dc * TOK + t0:dc * TOK + t0 + 512],
                                         xr[:, dc * TOK + t0:dc * TOK + t0 + 512],
                                         OP.mult).then_inc(sV, 1)
                    VE(sq)
                    acnt = C.bump(sV, 1)

                    def mmst(te, dc=dc, t0=t0, sqb=sqb, acnt=acnt,
                             gate=gate_dc):
                        if gate:
                            te.wait_ge(sV, gate)
                        if dc == 0:
                            te.wait_ge(sV, C.val("ln_banks_free"))
                        te.matmul(pb[4][0:1, 0:512], ones[:, 0:1],
                                  xr[:, dc * TOK + t0:dc * TOK + t0 + 512],
                                  start=(dc == 0), stop=False, skip_group_check=True)
                        te.wait_ge(sV, acnt)
                        i = te.matmul(pb[5][0:1, 0:512], ones[:, 0:1], sqb[:],
                                      start=(dc == 0), stop=(dc == 15),
                                      skip_group_check=True)
                        i.then_inc(sT, 1)
                    TE(mmst)
                    tcnt = C.bump(sT, 1)
                    C.c["lnsq_free%d" % (dc % 2)] = tcnt
                tcnt = C.val(sT)
                wpre0 = C.val("sLN")

                def st(ve, blk=blk, tcnt=tcnt, wpre0=wpre0):
                    ve.wait_ge(sT, tcnt)
                    if wpre0:
                        ve.wait_ge(sLN, wpre0)
                    ve.drain()
                    ve.tensor_scalar(stat[0:1, 0:512], pb[4][0:1, 0:512],
                                     1.0 / 2048.0, 0.0, OP.mult, OP.add)
                    ve.tensor_scalar(stat[0:1, 512:1024], pb[5][0:1, 0:512],
                                     1.0 / 2048.0, 1e-5, OP.mult, OP.add).then_inc(sV, 1)
                VE(st)
                vcnt = C.bump(sV, 1)
                C.c["ln_banks_free"] = vcnt

                def shipm(sy, blk=blk, vcnt=vcnt):
                    sy.wait_ge(sV, vcnt)
                    sy.dma_start(out=rowd.ap()[0, blk * 512:blk * 512 + 512],
                                 in_=stat[0:1, 0:512]).then_inc(sLN, 16)
                SY(shipm)
                wcnt = C.bump("sLN", 16)

                def st2(ve, blk=blk, wcnt=wcnt):
                    ve.wait_ge(sLN, wcnt)
                    ve.tensor_tensor(stat[0:1, 0:512], stat[0:1, 0:512],
                                     stat[0:1, 0:512], OP.mult)
                    ve.drain()
                    ve.tensor_tensor(stat[0:1, 512:1024], stat[0:1, 512:1024],
                                     stat[0:1, 0:512], OP.subtract).then_inc(sV, 1)
                VE(st2)
                vcnt2 = C.bump(sV, 1)

                def rst(ae, blk=blk, vcnt2=vcnt2):
                    ae.wait_ge(sV, vcnt2)
                    ae.activation(out=stat[0:1, 512:1024], in_=stat[0:1, 512:1024],
                                  func=AF.Sqrt).then_inc(sA, 1)
                AE(rst)
                acnt = C.bump(sA, 1)

                def rcp(ve, blk=blk, acnt=acnt):
                    ve.wait_ge(sA, acnt)
                    ve.reciprocal(stat[0:1, 512:1024],
                                  stat[0:1, 512:1024]).then_inc(sV, 1)
                VE(rcp)
                vcnt3 = C.bump(sV, 1)

                def shipr(sy, blk=blk, vcnt3=vcnt3):
                    sy.wait_ge(sV, vcnt3)
                    sy.dma_start(out=rowd.ap()[1, blk * 512:blk * 512 + 512],
                                 in_=stat[0:1, 512:1024]).then_inc(sLN, 16)
                SY(shipr)
                C.bump("sLN", 16)

            wcnt0 = C.val("sLN")

            vgate = C.val(sV)

            def bcast(sy, wcnt0=wcnt0, vgate=vgate):
                sy.wait_ge(sLN, wcnt0)
                if vgate:
                    sy.wait_ge(sV, vgate)
                sy.dma_start(out=rowb[:, 0:512],
                             in_=rowd.ap()[0:1, 0:512].to_broadcast([128, 512])).then_inc(sLN, 16)
                sy.dma_start(out=rowb[:, 512:1024],
                             in_=rowd.ap()[1:2, 0:512].to_broadcast([128, 512])).then_inc(sLN, 16)
            SY(bcast)
            wcnt = C.bump("sLN", 32)

            for half in range(2):
                if half == 1:
                    vpre_h = C.val(sV)

                    def bcast2(sy, vpre_h=vpre_h):
                        sy.wait_ge(sV, vpre_h)
                        sy.dma_start(out=rowb[:, 0:512],
                                     in_=rowd.ap()[0:1, 512:1024].to_broadcast([128, 512])).then_inc(sLN, 16)
                        sy.dma_start(out=rowb[:, 512:1024],
                                     in_=rowd.ap()[1:2, 512:1024].to_broadcast([128, 512])).then_inc(sLN, 16)
                    SY(bcast2)
                    wcnt = C.bump("sLN", 32)
                for dc in range(16):
                    def app(ve, dc=dc, wcnt=wcnt, half=half,
                            gpre=C.val("sGP")):
                        if dc == 0:
                            ve.wait_ge(sLN, wcnt)
                        if gpre:
                            ve.wait_ge(sGP, gpre)
                        ve.drain()
                        c0 = dc * TOK + half * 512
                        ve.tensor_tensor(sc1[:], xr[:, c0:c0 + 512],
                                         rowb[:, 0:512], OP.subtract)
                        ve.drain()
                        ve.tensor_tensor(sc1[:], sc1[:],
                                         rowb[:, 512:1024], OP.mult).then_inc(sV, 1)
                    VE(app)
                    vcnt = C.bump(sV, 1)

                    def hi(ve, dc=dc, half=half, vcnt=vcnt):
                        ve.wait_ge(sV, vcnt)
                        ve.tensor_copy(AH(dc * TOK + half * 512, 512),
                                       sc1[:]).then_inc(sV, 1)
                    VE(hi)
                    vcnt2 = C.bump(sV, 1)

                    def lo(gp, dc=dc, half=half, vcnt2=vcnt2):
                        gp.wait_ge(sV, vcnt2)
                        gp.tensor_tensor(AL(dc * TOK + half * 512, 512),
                                         sc1[:],
                                         AH(dc * TOK + half * 512, 512),
                                         OP.subtract).then_inc(sGP, 1)
                    GE(lo)
                    C.bump("sGP", 1)
                    C.c["lnsp_v%d_%d" % (dc, half)] = vcnt2
                    C.c["lnsp_g%d_%d" % (dc, half)] = C.val("sGP")
            vsplit = C.val(sV)
            C.c["ln_split_doneG"] = C.val("sGP")
            if gdst is not None:
                gbuf, _ = gidx[gdst]
                for dc in range(16):
                    def shp(gp, dc=dc, vsplit=vsplit, gbuf=gbuf):
                        if dc == 0:
                            gp.wait_ge(sV, vsplit)
                            gp.wait_ge(sGP, C.val("sGP"))
                        gp.dma_start(out=gbuf.ap()[dc * 256:dc * 256 + 128, :],
                                     in_=AH(dc * TOK, TOK)).then_inc(sG, 16)
                        gp.dma_start(out=gbuf.ap()[dc * 256 + 128:dc * 256 + 256, :],
                                     in_=AL(dc * TOK, TOK)).then_inc(sG, 16)
                    GE(shp)
                    C.bump(sG, 32)
            C.c["ln_split_done"] = C.val(sV)
            C.c["ln_split_doneG"] = C.val("sGP")

        def gather(gdst):
            gcnt = C.val(sG)

            def coll(gp, gcnt=gcnt, gdst=gdst):
                gp.wait_ge(sG, gcnt)
                gi, go = gidx[gdst]
                gp.collective_compute(
                    "AllGather", OP.bypass,
                    replica_groups=[[0, 1], [2, 3], [4, 5], [6, 7]],
                    ins=[gi.ap()],
                    outs=[go.ap()],
                ).then_inc(sCC, 1)
            GE(coll)
            return C.bump(sCC, 1)

        def stream_proj(gdst, ndc, row0, outs, cc_need):
            _, gout = gidx[gdst]
            wsp = [C.val("wsplit_dc%d" % d) for d in range(ndc)]
            for tt in range(4):
                rank, col0 = tt // 2, (tt % 2) * 512
                banks = [pb[(tt % 2) * 2 + k] for k in range(len(outs))]
                bfree = [C.val("proj_bank_free%d" % ((tt % 2) * 2 + k))
                         for k in range(len(outs))]
                for dc in range(ndc):
                    buf = dc % 3
                    tfree = C.val("xs_free%d" % buf)

                    def ldx(sy, rank=rank, col0=col0, dc=dc, buf=buf, tfree=tfree,
                            gout=gout, row0=row0, cc_need=cc_need):
                        if tfree:
                            sy.wait_ge(sT, tfree)
                        sy.wait_ge(sCC, cc_need)
                        r0 = row0 + dc * 256
                        src_ap = gout.ap()[rank, r0:r0 + 256,
                                           col0:col0 + 512].rearrange(
                            "(h p) c -> p h c", p=128)
                        dst_ap = xs_hl[:, buf * 1024:buf * 1024 + 1024].rearrange(
                            "p (h c) -> p h c", h=2)
                        sy.dma_start(out=dst_ap, in_=src_ap).then_inc(sX[buf], 16)
                    SY(ldx)
                    wcnt = C.bump("sX%d" % buf, 16)

                    def mmp(te, dc=dc, buf=buf, wcnt=wcnt, ndc=ndc, banks=banks,
                            wsp=wsp, bfree=bfree, outs=outs):
                        te.wait_ge(sX[buf], wcnt)
                        te.wait_ge(sV, wsp[dc][0])
                        te.wait_ge(sGP, wsp[dc][1])
                        for k, (dsto, mode, wbase) in enumerate(outs):
                            if dc == 0 and bfree[k]:
                                te.wait_ge(sV, bfree[k])
                            w0 = wbase + dc * 128
                            i = mm3(te, banks[k][:, 0:512],
                                    wsh[:, w0:w0 + 128], wsl[:, w0:w0 + 128],
                                    xs_hl[:, buf * 1024:buf * 1024 + 512],
                                    xs_hl[:, buf * 1024 + 512:buf * 1024 + 1024],
                                    start=(dc == 0), stop=(dc == ndc - 1))
                        i.then_inc(sT, 1)
                    TE(mmp)
                    tcnt = C.bump(sT, 1)
                    C.c["xs_free%d" % buf] = tcnt
                tfin = C.val(sT)

                for k, (dsto, mode, wbase) in enumerate(outs):
                    bidx = (tt % 2) * 2 + k
                    if mode == "vt":
                        def ev0(ve, tfin=tfin, bank=banks[k]):
                            ve.wait_ge(sT, tfin)
                            ve.drain()
                            ve.tensor_copy(sc4[:], bank[:, 0:512]).then_inc(sV, 1)
                        VE(ev0)
                        vcnt = C.bump(sV, 1)
                        C.c["proj_bank_free%d" % bidx] = vcnt

                        def tr(te, vcnt=vcnt):
                            te.wait_ge(sV, vcnt)
                            te.wait_ge(sV, C.val("vt_ps_free"))
                            for j in range(4):
                                i = te.matmul(pb[6][:, j * 128:(j + 1) * 128],
                                              sc4[:, j * 128:(j + 1) * 128], idsb[:],
                                              is_transpose=True, start=(j == 0),
                                              stop=(j == 3), skip_group_check=True)
                            i.then_inc(sT, 1)
                        TE(tr)
                        tcnt2 = C.bump(sT, 1)

                        def ev1(ve, tt=tt, dsto=dsto, tcnt2=tcnt2):
                            ve.wait_ge(sT, tcnt2)
                            for j in range(4):
                                kt = tt * 4 + j
                                i = ve.tensor_copy(
                                    arena[:, dsto * T + kt * 128:dsto * T + (kt + 1) * 128],
                                    pb[6][:, j * 128:(j + 1) * 128])
                            i.then_inc(sV, 1)
                        VE(ev1)
                        C.c["vt_ps_free"] = C.bump(sV, 1)
                    else:
                        scl = SCALE if mode == "qs" else 1.0

                        def ev(ve, tt=tt, dsto=dsto, scl=scl, tfin=tfin,
                               bank=banks[k]):
                            ve.wait_ge(sT, tfin)
                            ve.drain()
                            ve.tensor_scalar(
                                arena[:, dsto * T + tt * 512:dsto * T + tt * 512 + 512],
                                bank[:, 0:512], scl, 0.0, OP.mult,
                                OP.add).then_inc(sV, 1)
                        VE(ev)
                        C.c["proj_bank_free%d" % bidx] = C.bump(sV, 1)
            C.c["proj_banks_free"] = C.val(sV)
            for dsto, mode, wbase in outs:
                C.c["wsh_rd%d" % (wbase // 2048)] = C.val(sT)

        def rope_pair():
            for (tabc, tabs, dof) in ((cosq2, sinq2, 6), (cosk2, sink2, 7)):
                apre = C.val("proj_banks_free")
                for cb in range(4):
                    vpre2 = C.val("rope_tab_rd")
                    c0 = dof * T + cb * 512

                    def ldt(sy, tabc=tabc, tabs=tabs, cb=cb, vpre2=vpre2, c0=c0,
                            apre=apre):
                        if vpre2:
                            sy.wait_ge(sV, vpre2)
                        if apre:
                            sy.wait_ge(sV, apre)
                        sy.dma_start(out=tabA[:],
                                     in_=tabc[:, cb * 512:(cb + 1) * 512]).then_inc(sR, 16)
                        sy.dma_start(out=tabB[:],
                                     in_=tabs[:, cb * 512:(cb + 1) * 512]).then_inc(sR, 16)
                        sy.dma_start(out=rotb[0:32, :],
                                     in_=arena[32:64, c0:c0 + 512]).then_inc(sR, 16)
                        sy.dma_start(out=rotb[32:64, :],
                                     in_=arena[0:32, c0:c0 + 512]).then_inc(sR, 16)
                        sy.dma_start(out=rotb[64:96, :],
                                     in_=arena[96:128, c0:c0 + 512]).then_inc(sR, 16)
                        sy.dma_start(out=rotb[96:128, :],
                                     in_=arena[64:96, c0:c0 + 512]).then_inc(sR, 16)
                    SY(ldt)
                    wcnt = C.bump("sR", 96)

                    def rp(ve, cb=cb, wcnt=wcnt, c0=c0):
                        ve.wait_ge(sR, wcnt)
                        ve.tensor_tensor(arena[:, c0:c0 + 512], arena[:, c0:c0 + 512],
                                         tabA[:], OP.mult)
                        ve.tensor_tensor(rotb[:], rotb[:], tabB[:], OP.mult)
                        ve.drain()
                        ve.tensor_tensor(arena[:, c0:c0 + 512], arena[:, c0:c0 + 512],
                                         rotb[:], OP.add).then_inc(sV, 1)
                    VE(rp)
                    C.bump(sV, 1)
                    C.c["rope_tab_rd"] = C.val(sV)

        def attention_head(sub, h, hh):
            qo = hh
            ko = 2 + hh
            vo = 4 + hh
            rr0 = hh * 64
            gate_pb = C.val("proj_banks_free")
            gate_rp = C.val("rope_tab_rd")
            for J in range(4):
                ntiles = 4 * J + 4
                for i in range(ntiles):
                    sc_bank = pb[i % 2]
                    diag = (sub == 1) and (i >= 4 * J)
                    use_mask = ((sub == 0) or diag) and dbg["attn_parts"] >= 2
                    mw = None
                    mb = i % 3
                    if use_mask:
                        vfree = C.val("msk_free%d" % mb)

                        def ldm(sy, i=i, J=J, mb=mb, vfree=vfree, sub=sub):
                            if vfree:
                                sy.wait_ge(sV, vfree)
                            if sub == 0:
                                src = mask_a[i * 128:(i + 1) * 128, J * 512:J * 512 + 512]
                            else:
                                ii = i - 4 * J
                                src = mask_c[ii * 128:(ii + 1) * 128, 0:512]
                            sy.dma_start(out=mskb[:, mb * 512:mb * 512 + 512],
                                         in_=src).then_inc(sM[mb], 16)
                        AE(ldm)
                        mw = C.bump("sM%d" % mb, 16)

                    apre = C.val("exp_done_t%d" % (i % 2))

                    def mmsc(te, i=i, J=J, sc_bank=sc_bank, qo=qo, ko=ko,
                             rr0=rr0, apre=apre, sub=sub, gate_pb=gate_pb,
                             gate_rp=gate_rp):
                        if apre:
                            te.wait_ge(sA, apre)
                        if i == 0 and J == 0:
                            if gate_pb:
                                te.wait_ge(sV, gate_pb)
                            if gate_rp:
                                te.wait_ge(sV, gate_rp)
                        ins = te.matmul(sc_bank[:, 0:512],
                                        arena[:, ko * T + i * 128:ko * T + (i + 1) * 128],
                                        arena[:, qo * T + J * 512:qo * T + J * 512 + 512],
                                        start=True,
                                        stop=(sub == 1), skip_group_check=True)
                        if sub == 0:
                            ins = te.matmul(sc_bank[:, 0:512],
                                            arena[rr0:rr0 + 64, 7 * T + i * 128:7 * T + (i + 1) * 128],
                                            arena[rr0:rr0 + 64, 6 * T + J * 512:6 * T + J * 512 + 512],
                                            start=False, stop=True, skip_group_check=True)
                        ins.then_inc(sT, 1)
                    TE(mmsc)
                    tcnt = C.bump(sT, 1)

                    if use_mask:
                        def mska(ve, sc_bank=sc_bank, mb=mb, tcnt=tcnt, mw=mw):
                            ve.wait_ge(sT, tcnt)
                            ve.wait_ge(sM[mb], mw)
                            ve.tensor_tensor(sc_bank[:, 0:512], sc_bank[:, 0:512],
                                             mskb[:, mb * 512:mb * 512 + 512],
                                             OP.add).then_inc(sV, 1)
                        VE(mska)
                        vcnt = C.bump(sV, 1)
                        C.c["msk_free%d" % mb] = vcnt
                        dep = ("V", vcnt)
                    else:
                        dep = ("T", tcnt)

                    if dbg["attn_parts"] < 3:
                        C.c["exp_done_t%d" % (i % 2)] = dep[1] if dep[0] == "V" else 0
                        if dep[0] != "V":
                            def drain(ve, sc_bank=sc_bank, tcnt=tcnt):
                                ve.wait_ge(sT, tcnt)
                                ve.tensor_copy(sc4[:], sc_bank[:, 0:512]).then_inc(sV, 1)
                            VE(drain)
                            C.c["exp_done_t%d" % (i % 2)] = 0
                            C.c["drain_v"] = C.bump(sV, 1)
                        continue
                    eb = i % 3
                    epre = C.val("exp_rd_%d" % eb)

                    def expf(ae, sc_bank=sc_bank, eb=eb, dep=dep, epre=epre):
                        if dep[0] == "V":
                            ae.wait_ge(sV, dep[1])
                        else:
                            ae.wait_ge(sT, dep[1])
                        if epre:
                            ae.wait_ge(sT, epre)
                        ae.activation(out=expb[:, eb * 512:eb * 512 + 512],
                                      in_=sc_bank[:, 0:512], func=AF.Exp).then_inc(sA, 1)
                    AE(expf)
                    acnt = C.bump(sA, 1)
                    C.c["exp_done_t%d" % (i % 2)] = acnt
                    C.c["exp_rd"] = acnt
                    if dbg["attn_parts"] < 4:
                        continue

                    def mmav(te, i=i, eb=eb, vo=vo, acnt=acnt, J=J,
                             avfree=C.val("avrs_free%d" % (J % 2))):
                        te.wait_ge(sA, acnt)
                        if i == 0 and avfree:
                            te.wait_ge(sV, avfree)
                        av_b, rs_b = (pb[2], pb[3]) if J % 2 == 0 else (pb[4], pb[5])
                        te.matmul(av_b[:, 0:512],
                                  arena[:, vo * T + i * 128:vo * T + (i + 1) * 128],
                                  expb[:, eb * 512:eb * 512 + 512],
                                  start=(i == 0), stop=False, skip_group_check=True)
                        i2 = te.matmul(rs_b[0:1, 0:512], ones[:, 0:1],
                                       expb[:, eb * 512:eb * 512 + 512],
                                       start=(i == 0), stop=False, skip_group_check=True)
                        i2.then_inc(sT, 1)
                    TE(mmav)
                    tcnt2 = C.bump(sT, 1)
                    C.c["exp_rd_%d" % eb] = tcnt2

                if dbg["attn_parts"] < 5:
                    continue
                tfin = C.val(sT)
                wship = C.val("sRS")

                def rsum(ve, tfin=tfin, wship=wship, J=J):
                    if wship:
                        ve.wait_ge(sRS, wship)
                    ve.wait_ge(sT, tfin)
                    ve.drain()
                    rs_b = pb[3] if J % 2 == 0 else pb[5]
                    ve.reciprocal(sc1[0:1, :], rs_b[0:1, 0:512]).then_inc(sV, 1)
                VE(rsum)
                vcnt = C.bump(sV, 1)

                def shiprs(sy, vcnt=vcnt):
                    sy.wait_ge(sV, vcnt)
                    sy.dma_start(out=rowd.ap()[2, 0:512], in_=sc1[0:1, :]).then_inc(sRS, 16)
                AE(shiprs)
                wcnt = C.bump("sRS", 16)
                gpre = C.val(sG)

                def bcrs(sy, wcnt=wcnt, gpre=gpre):
                    sy.wait_ge(sRS, wcnt)
                    if gpre:
                        sy.wait_ge(sG, gpre)
                    sy.dma_start(out=sc3[:],
                                 in_=rowd.ap()[2:3, 0:512].to_broadcast([128, 512])).then_inc(sRS, 16)
                AE(bcrs)
                wcnt2 = C.bump("sRS", 16)

                def norm(ve, wcnt2=wcnt2, gpre=C.val("sGP"), J=J):
                    ve.wait_ge(sRS, wcnt2)
                    if gpre:
                        ve.wait_ge(sGP, gpre)
                    ve.drain()
                    av_b = pb[2] if J % 2 == 0 else pb[4]
                    ve.tensor_tensor(sc4[:], av_b[:, 0:512], sc3[:],
                                     OP.mult).then_inc(sV, 1)
                VE(norm)
                vcnt2 = C.bump(sV, 1)
                C.c["avrs_free%d" % (J % 2)] = vcnt2
                C.c["avrs_free"] = vcnt2
                C.c["rs_ship"] = wcnt2

                ocnt = split_to(sc4[:], vcnt2, (sOP, C.val("sOP")))

                def oshp(sy, ocnt=ocnt, h=h, J=J):
                    sy.wait_ge(sV, ocnt)
                    sy.dma_start(out=opc.ap()[0, h * 128:(h + 1) * 128,
                                              J * 512:J * 512 + 512],
                                 in_=obh[:]).then_inc(sOP, 16)
                    sy.dma_start(out=opc.ap()[1, h * 128:(h + 1) * 128,
                                              J * 512:J * 512 + 512],
                                 in_=obl[:]).then_inc(sOP, 16)
                AE(oshp)
                C.bump("sOP", 32)

        def oproj_rs(sub):
            wo = wo_a if sub == 0 else wo_b
            gpre = C.val("sOP")
            for oc in range(16):
                wb = split_weights(wo, oc, 8)
                for tt in range(4):
                    for dc in range(8):
                        buf = dc % 3
                        tfree = C.val("xs_free%d" % buf)
                        wsplit = C.val("wsplit_dc%d" % dc)

                        def ldo(sy, dc=dc, buf=buf, tfree=tfree, tt=tt, gpre=gpre):
                            if tfree:
                                sy.wait_ge(sT, tfree)
                            sy.wait_ge(sOP, gpre)
                            src_ap = opc.ap()[:, dc * 128:(dc + 1) * 128,
                                              tt * 512:tt * 512 + 512].rearrange(
                                "h p c -> p h c")
                            dst_ap = xs_hl[:, buf * 1024:buf * 1024 + 1024].rearrange(
                                "p (h c) -> p h c", h=2)
                            sy.dma_start(out=dst_ap, in_=src_ap).then_inc(sX[buf], 16)
                        SY(ldo)
                        wcnt = C.bump("sX%d" % buf, 16)

                        def mmo(te, dc=dc, buf=buf, wcnt=wcnt, wsplit=wsplit,
                                wb=wb):
                            te.wait_ge(sX[buf], wcnt)
                            te.wait_ge(sV, wsplit[0])
                            te.wait_ge(sGP, wsplit[1])
                            if dc == 0:
                                te.wait_ge(sV, C.val("oproj_bank_free"))
                            i = mm3(te, pb[7][:, 0:512],
                                    wsh[:, wb + dc * 128:wb + dc * 128 + 128],
                                    wsl[:, wb + dc * 128:wb + dc * 128 + 128],
                                    xs_hl[:, buf * 1024:buf * 1024 + 512],
                                    xs_hl[:, buf * 1024 + 512:buf * 1024 + 1024],
                                    start=(dc == 0), stop=(dc == 7))
                            i.then_inc(sT, 1)
                        TE(mmo)
                        tcnt = C.bump(sT, 1)
                        C.c["xs_free%d" % buf] = tcnt

                    gfree = C.val("oev_shipped")

                    def evo(ve, tcnt=tcnt, gfree=gfree):
                        ve.wait_ge(sT, tcnt)
                        if gfree:
                            ve.wait_ge(sG, gfree)
                        ve.drain()
                        ve.tensor_copy(sc1[:], pb[7][:, 0:512]).then_inc(sV, 1)
                    VE(evo)
                    vcnt = C.bump(sV, 1)
                    C.c["oproj_bank_free"] = vcnt

                    def sho(gp, vcnt=vcnt, oc=oc, tt=tt):
                        gp.wait_ge(sV, vcnt)
                        gp.dma_start(out=rs_ins[oc // 4].ap()[tt // 2, (oc % 4) * 128:(oc % 4 + 1) * 128,
                                                              (tt % 2) * 512:(tt % 2) * 512 + 512],
                                     in_=sc1[:]).then_inc(sG, 16)
                    GE(sho)
                    C.c["oev_shipped"] = C.bump(sG, 16)
                C.c["wsh_rd%d" % (wb // 2048)] = C.val(sT)
                if oc % 4 == 3:
                    gq = C.val(sG)

                    def collq(gp, gq=gq, q=oc // 4):
                        gp.wait_ge(sG, gq)
                        gp.collective_compute(
                            "ReduceScatter", OP.add,
                            replica_groups=[[0, 1], [2, 3], [4, 5], [6, 7]],
                            ins=[rs_ins[q].ap()],
                            outs=[rs_outs[q].ap()],
                        ).then_inc(sCC, 1)
                    GE(collq)
                    C.c["rs_cc%d" % (oc // 4)] = C.bump(sCC, 1)

            for dc in range(16):
                vfree = C.val("res_free")
                cc2 = C.val("rs_cc%d" % (dc // 4))

                def ldr(sy, dc=dc, cc2=cc2, vfree=vfree):
                    if vfree:
                        sy.wait_ge(sV, vfree)
                    sy.wait_ge(sCC, cc2)
                    sy.dma_start(out=rsb[:, 0:512],
                                 in_=rs_outs[dc // 4].ap()[(dc % 4) * 128:(dc % 4 + 1) * 128,
                                                           0:512]).then_inc(sRES, 16)
                    sy.dma_start(out=rsb[:, 512:1024],
                                 in_=rs_outs[dc // 4].ap()[(dc % 4) * 128:(dc % 4 + 1) * 128,
                                                           512:1024]).then_inc(sRES, 16)
                SY(ldr)
                wcnt = C.bump("sRES", 32)

                def addr_(ve, dc=dc, wcnt=wcnt):
                    ve.wait_ge(sRES, wcnt)
                    ve.drain()
                    ve.tensor_tensor(xr[:, dc * TOK:dc * TOK + 512],
                                     xr[:, dc * TOK:dc * TOK + 512], rsb[:, 0:512], OP.add)
                    ve.tensor_tensor(xr[:, dc * TOK + 512:dc * TOK + 1024],
                                     xr[:, dc * TOK + 512:dc * TOK + 1024], rsb[:, 512:1024],
                                     OP.add).then_inc(sV, 1)
                VE(addr_)
                C.c["res_free"] = C.bump(sV, 1)
                C.c["res_dc%d" % dc] = C.val(sV)
            C.c["ln_gate"] = C.val(sV)

        def attn_a():
            if not dbg["ln"]:
                return
            ln_phase(gdst=0)
            cc = gather(0)
            for oc in range(4):
                wb = split_weights(wdkv, oc, 16)
                for tt in range(2):
                    bank = pb[2 + (oc % 2)]
                    bfree = C.val("ckv_bank_free%d" % (oc % 2))
                    lnv = [C.val("lnsp_v%d_%d" % (d, tt)) for d in range(16)]
                    lng = [C.val("lnsp_g%d_%d" % (d, tt)) for d in range(16)]
                    wsp = [C.val("wsplit_dc%d" % d) for d in range(16)]

                    def mmckv(te, tt=tt, wb=wb, bank=bank, bfree=bfree,
                              lnv=lnv, lng=lng, wsp=wsp):
                        if bfree:
                            te.wait_ge(sV, bfree)
                        for dc in range(16):
                            te.wait_ge(sV, max(wsp[dc][0], lnv[dc]))
                            te.wait_ge(sGP, max(wsp[dc][1], lng[dc]))
                            w0 = wb + dc * 128
                            i = mm3(te, bank[:, 0:512],
                                    wsh[:, w0:w0 + 128], wsl[:, w0:w0 + 128],
                                    AH(dc * TOK + tt * 512, 512),
                                    AL(dc * TOK + tt * 512, 512),
                                    start=(dc == 0), stop=(dc == 15))
                        i.then_inc(sT, 1)
                    TE(mmckv)
                    tcnt = C.bump(sT, 1)

                    def evc(ve, bank=bank, tcnt=tcnt, gpre=C.val("sGP")):
                        ve.wait_ge(sT, tcnt)
                        if gpre:
                            ve.wait_ge(sGP, gpre)
                        ve.drain()
                        ve.tensor_copy(sc4[:], bank[:, 0:512]).then_inc(sV, 1)
                    VE(evc)
                    vcnt = C.bump(sV, 1)
                    C.c["ckv_bank_free%d" % (oc % 2)] = vcnt

                    ocnt = split_to(sc4[:], vcnt, (sG, C.val(sG)))

                    def cshp(gp, ocnt=ocnt, oc=oc, tt=tt):
                        gp.wait_ge(sV, ocnt)
                        gp.dma_start(out=g1c_in.ap()[oc * 256:oc * 256 + 128,
                                                     tt * 512:tt * 512 + 512],
                                     in_=obh[:]).then_inc(sG, 16)
                        gp.dma_start(out=g1c_in.ap()[oc * 256 + 128:oc * 256 + 256,
                                                     tt * 512:tt * 512 + 512],
                                     in_=obl[:]).then_inc(sG, 16)
                    GE(cshp)
                    C.bump(sG, 32)
                C.c["wsh_rd%d" % (wb // 2048)] = C.val(sT)

            wq_next = [split_weights(wq_a, 0, 16), split_weights(wq_a, 1, 16)]
            ccc = gather(2)
            np_ = min(NHH // 2, dbg["pairs"])
            for hp in range(np_):
                stream_proj(gdst=0, ndc=16, row0=0,
                            outs=[(0, "qs", wq_next[0]), (1, "qs", wq_next[1])],
                            cc_need=cc)
                wb1 = split_weights(wqp, hp, 16)
                wb2 = split_weights(wkp, hp, 16)
                stream_proj(gdst=0, ndc=16, row0=0,
                            outs=[(6, "n", wb1), (7, "n", wb2)], cc_need=cc)
                wb1 = split_weights(wuk, 2 * hp, 4)
                wb2 = split_weights(wuv, 2 * hp, 4)
                stream_proj(gdst=2, ndc=4, row0=0,
                            outs=[(2, "n", wb1), (4, "vt", wb2)], cc_need=ccc)
                wb1 = split_weights(wuk, 2 * hp + 1, 4)
                wb2 = split_weights(wuv, 2 * hp + 1, 4)
                stream_proj(gdst=2, ndc=4, row0=0,
                            outs=[(3, "n", wb1), (5, "vt", wb2)], cc_need=ccc)
                if dbg["rope"]:
                    rope_pair()
                if hp + 1 < np_:
                    wq_next = [split_weights(wq_a, 2 * hp + 2, 16),
                               split_weights(wq_a, 2 * hp + 3, 16)]
                if dbg["attn"]:
                    for hh in range(dbg["heads"]):
                        attention_head(0, 2 * hp + hh, hh)
            C.c["ln_banks_free"] = C.val(sV)
            if dbg["oproj"]:
                oproj_rs(0)

        def attn_b():
            ln_phase(gdst=1)
            wq_next = [split_weights(wq_b, 0, 16), split_weights(wq_b, 1, 16)]
            cc = gather(1)
            for hp in range(NHH // 2):
                stream_proj(gdst=1, ndc=16, row0=0,
                            outs=[(0, "qs", wq_next[0]), (1, "qs", wq_next[1])],
                            cc_need=cc)
                wb1 = split_weights(wk_b, 2 * hp, 16)
                wb2 = split_weights(wk_b, 2 * hp + 1, 16)
                stream_proj(gdst=1, ndc=16, row0=0,
                            outs=[(2, "n", wb1), (3, "n", wb2)], cc_need=cc)
                wb1 = split_weights(wv_b, 2 * hp, 16)
                wb2 = split_weights(wv_b, 2 * hp + 1, 16)
                stream_proj(gdst=1, ndc=16, row0=0,
                            outs=[(4, "vt", wb1), (5, "vt", wb2)], cc_need=cc)
                if hp + 1 < NHH // 2:
                    wq_next = [split_weights(wq_b, 2 * hp + 2, 16),
                               split_weights(wq_b, 2 * hp + 3, 16)]
                attention_head(1, 2 * hp, 0)
                attention_head(1, 2 * hp + 1, 1)
            C.c["ln_banks_free"] = C.val(sV)
            oproj_rs(1)

        def ffn():
            ln_phase(gdst=None)
            vpre = C.val("ln_split_done")

            def ldw2(sy):
                sy.wait_ge(sT, C.val("w2sb_rd"))
                for oc in range(16):
                    sy.dma_start(out=w2sb[:, oc * 8:(oc + 1) * 8],
                                 in_=w2[oc, :, :]).then_inc(sB, 16)
                    sy.dma_start(out=b1sb[:, oc:oc + 1],
                                 in_=b1[oc, :, :]).then_inc(sB, 16)
                sy.dma_start(out=b2sb[:, :], in_=b2[:, :]).then_inc(sB, 16)
            SY(ldw2)
            w2w = C.bump("sB", 16 * 33)

            for oc in range(16):
                wb = split_weights(w1, oc, 16)
                for tt in range(2):
                    if True:
                        slot = oc % 2
                        ocp = oc // 2
                        hfree = C.val("h_bank_free%d" % slot)

                        def mmh(te, tt=tt, slot=slot, oc=oc, wb=wb,
                                hfree=hfree,
                                wsp=[C.val("wsplit_dc%d" % d) for d in range(16)],
                                lnv=[C.val("lnsp_v%d_%d" % (d, tt)) for d in range(16)],
                                lng=[C.val("lnsp_g%d_%d" % (d, tt)) for d in range(16)]):
                            if hfree:
                                te.wait_ge(sA, hfree)
                            if oc == 0 and tt == 0:
                                te.wait_ge(sV, C.val("ln_banks_free"))
                                te.wait_ge(sV, C.val("proj_banks_free"))
                                te.wait_ge(sA, C.val("exp_rd"))
                                te.wait_ge(sV, C.val("avrs_free"))
                            for dc in range(16):
                                te.wait_ge(sV, max(wsp[dc][0], lnv[dc]))
                                te.wait_ge(sGP, max(wsp[dc][1], lng[dc]))
                                w0 = wb + dc * 128
                                i = mm3(te, pb[slot][:, 0:512],
                                        wsh[:, w0:w0 + 128], wsl[:, w0:w0 + 128],
                                        AH(dc * TOK + tt * 512, (dc * TOK + tt * 512 + 512) - (dc * TOK + tt * 512)),
                                        AL(dc * TOK + tt * 512, (dc * TOK + tt * 512 + 512) - (dc * TOK + tt * 512)),
                                        start=(dc == 0), stop=(dc == 15))
                            i.then_inc(sT, 1)
                        TE(mmh)
                        tcnt = C.bump(sT, 1)

                        def rel(ae, slot=slot, oc=oc, tcnt=tcnt, w2w=w2w):
                            ae.wait_ge(sT, tcnt)
                            ae.wait_ge(sT, C.val("w2_rd%d" % slot))
                            ae.wait_ge(sB, w2w)
                            ae.activation(out=hbuf[:, slot * 512:slot * 512 + 512],
                                          in_=pb[slot][:, 0:512], func=AF.Relu,
                                          bias=b1sb[:, oc:oc + 1], scale=1.0).then_inc(sA, 1)
                        AE(rel)
                        acnt = C.bump(sA, 1)
                        C.c["h_bank_free%d" % slot] = acnt

                        def mm2(te, slot=slot, oc=oc, tt=tt, acnt=acnt,
                                w2w=w2w):
                            te.wait_ge(sA, acnt)
                            if oc == 0:
                                te.wait_ge(sV, C.val("alpha_bank_free"))
                                te.wait_ge(sV, C.val("ln_banks_free"))
                            te.wait_ge(sB, w2w)
                            i = te.matmul(pb[4 + tt][0:8, 0:512],
                                          w2sb[:, oc * 8:(oc + 1) * 8],
                                          hbuf[:, slot * 512:slot * 512 + 512],
                                          start=(oc == 0),
                                          stop=(oc == 15),
                                          skip_group_check=True)
                            i.then_inc(sT, 1)
                        TE(mm2)
                        tcnt2 = C.bump(sT, 1)
                        C.c["w2_rd%d" % slot] = tcnt2
                C.c["wsh_rd%d" % (wb // 2048)] = C.val(sT)
            C.c["w2sb_rd"] = C.val(sT)
            C.c["wsh_rd"] = C.val(sT)

            tfin = C.val(sT)
            for tt in range(2):
                def aev(ve, tt=tt, tfin=tfin):
                    ve.wait_ge(sT, tfin)
                    ve.tensor_scalar(hbuf[0:8, tt * 512:tt * 512 + 512],
                                     pb[4 + tt][0:8, 0:512], b2sb[:, :], 0.0,
                                     OP.add, OP.add).then_inc(sV, 1)
                VE(aev)
                vcnt = C.bump(sV, 1)

                def aex(ae, tt=tt, vcnt=vcnt):
                    ae.wait_ge(sV, vcnt)
                    ae.activation(out=hbuf[0:8, tt * 512:tt * 512 + 512],
                                  in_=hbuf[0:8, tt * 512:tt * 512 + 512],
                                  func=AF.Exp).then_inc(sA, 1)
                AE(aex)
                acnt = C.bump(sA, 1)

                vt_free = C.val("vt_ps_free")

                def asum(te, tt=tt, acnt=acnt, vt_free=vt_free):
                    te.wait_ge(sA, acnt)
                    if vt_free:
                        te.wait_ge(sV, vt_free)
                    i = te.matmul(pb[6][0:1, 0:512],
                                  ones[0:8, 0:1], hbuf[0:8, tt * 512:tt * 512 + 512],
                                  start=True, stop=True,
                                  skip_group_check=True)
                    i.then_inc(sT, 1)
                TE(asum)
                tcnt = C.bump(sT, 1)

                lnpre = C.val("sLN")

                def arc(ve, tt=tt, tcnt=tcnt, lnpre=lnpre):
                    ve.wait_ge(sT, tcnt)
                    if lnpre:
                        ve.wait_ge(sLN, lnpre)
                    ve.drain()
                    ve.reciprocal(sc1[0:1, :], pb[6][0:1, 0:512]).then_inc(sV, 1)
                VE(arc)
                vcnt2 = C.bump(sV, 1)
                C.c["vt_ps_free"] = vcnt2


                def aship(sy, tt=tt, vcnt2=vcnt2):
                    sy.wait_ge(sV, vcnt2)
                    sy.dma_start(out=rowd.ap()[3, tt * 512:tt * 512 + 512],
                                 in_=sc1[0:1, :]).then_inc(sLN, 16)
                SY(aship)
                C.bump("sLN", 16)

            wcnt = C.val("sLN")

            vgate2 = C.val(sV)

            def abc(sy, wcnt=wcnt, vgate2=vgate2):
                sy.wait_ge(sLN, wcnt)
                if vgate2:
                    sy.wait_ge(sV, vgate2)
                sy.dma_start(out=rowb[0:8, :],
                             in_=rowd.ap()[3:4, 0:TOK].to_broadcast([8, TOK])).then_inc(sLN, 16)
            SY(abc)
            wcnt2 = C.bump("sLN", 16)

            def anrm(ve, wcnt2=wcnt2):
                ve.wait_ge(sLN, wcnt2)
                ve.tensor_tensor(hbuf[0:8, 0:1024], hbuf[0:8, 0:1024], rowb[0:8, :],
                                 OP.mult).then_inc(sV, 1)
            VE(anrm)
            vcnt3 = C.bump(sV, 1)

            def aout(sy, vcnt3=vcnt3):
                sy.wait_ge(sV, vcnt3)
                sy.dma_start(out=alpd.ap()[:, :], in_=hbuf[0:8, 0:1024]).then_inc(sLN, 16)
            SY(aout)
            alpw = C.bump("sLN", 16)
            C.c["alpha_bank_free"] = vcnt3
            C.c["ln_banks_free"] = vcnt3

            for ph in range(2):
                vx_wb = [split_weights(v2w, 2 * ph, 16),
                         split_weights(v2w, 2 * ph + 1, 16)]
                for ci in range(2):
                    c = 2 * ph + ci
                    vbpre = C.val("alpb_rd")

                    def abr(sy, c=c, alpw=alpw, vbpre=vbpre):
                        sy.wait_ge(sLN, alpw)
                        if vbpre:
                            sy.wait_ge(sV, vbpre)
                        sy.dma_start(out=rowb[0:64, :],
                                     in_=alpd.ap()[2 * c:2 * c + 1, :].to_broadcast([64, TOK])).then_inc(sLN, 16)
                        sy.dma_start(out=rowb[64:128, :],
                                     in_=alpd.ap()[2 * c + 1:2 * c + 2, :].to_broadcast([64, TOK])).then_inc(sLN, 16)
                    SY(abr)
                    abw = C.bump("sLN", 32)

                    for tt in range(2):
                        bank = pb[ci]
                        bfree = C.val("vx_bank_free%d" % ci)

                        def mmvx(te, ci=ci, tt=tt, wb=vx_wb[ci],
                                 bank=bank, bfree=bfree,
                                 wsp=[C.val("wsplit_dc%d" % d) for d in range(16)],
                                 lnv=[C.val("lnsp_v%d_%d" % (d, tt)) for d in range(16)],
                                 lng=[C.val("lnsp_g%d_%d" % (d, tt)) for d in range(16)]):
                            if bfree:
                                te.wait_ge(sV, bfree)
                            for dc in range(16):
                                te.wait_ge(sV, max(wsp[dc][0], lnv[dc]))
                                te.wait_ge(sGP, max(wsp[dc][1], lng[dc]))
                                w0 = wb + dc * 128
                                i = mm3(te, bank[:, 0:512],
                                        wsh[:, w0:w0 + 128], wsl[:, w0:w0 + 128],
                                        AH(dc * TOK + tt * 512, 512),
                                        AL(dc * TOK + tt * 512, 512),
                                        start=(dc == 0), stop=(dc == 15))
                            i.then_inc(sT, 1)
                        TE(mmvx)
                        tcnt = C.bump(sT, 1)

                        def evx(ve, ci=ci, tt=tt, bank=bank, tcnt=tcnt, abw=abw,
                                gpre=C.val("sGP")):
                            ve.wait_ge(sT, tcnt)
                            ve.wait_ge(sLN, abw)
                            if gpre:
                                ve.wait_ge(sGP, gpre)
                            ve.drain()
                            ve.tensor_tensor(sc4[:], bank[:, 0:512],
                                             rowb[:, tt * 512:tt * 512 + 512],
                                             OP.mult).then_inc(sV, 1)
                        VE(evx)
                        vcnt = C.bump(sV, 1)
                        C.c["vx_bank_free%d" % ci] = vcnt

                        def vhi(ve, ci=ci, tt=tt, vcnt=vcnt):
                            ve.wait_ge(sV, vcnt)
                            ve.tensor_copy(
                                vxh[:, ci * TOK + tt * 512:ci * TOK + tt * 512 + 512],
                                sc4[:]).then_inc(sV, 1)
                        VE(vhi)
                        vcnt2 = C.bump(sV, 1)

                        def vlo(gp, ci=ci, tt=tt, vcnt2=vcnt2):
                            gp.wait_ge(sV, vcnt2)
                            gp.tensor_tensor(
                                vxl[:, ci * TOK + tt * 512:ci * TOK + tt * 512 + 512],
                                sc4[:],
                                vxh[:, ci * TOK + tt * 512:ci * TOK + tt * 512 + 512],
                                OP.subtract).then_inc(sGP, 1)
                        GE(vlo)
                        C.bump("sGP", 1)
                    C.c["alpb_rd"] = C.val(sV)
                    C.c["wsh_rd%d" % (vx_wb[ci] // 2048)] = C.val(sT)
                vx_done = (C.val(sV), C.val("sGP"))

                for oc in range(16):
                    u_wb = split_weights(u2w, oc, 4)
                    wsplit = C.val("wsplit_dc3")
                    for tt in range(2):
                        bfree = C.val("oproj_bank_free")

                        def mmu(te, oc=oc, tt=tt, ph=ph, wsplit=wsplit,
                                vx_done=vx_done, bfree=bfree, u_wb=u_wb):
                            te.wait_ge(sV, wsplit[0])
                            te.wait_ge(sGP, wsplit[1])
                            te.wait_ge(sV, vx_done[0])
                            te.wait_ge(sGP, vx_done[1])
                            if bfree:
                                te.wait_ge(sV, bfree)
                            for ci in range(2):
                                i = mm3(te, pb[7][:, 0:512],
                                        wsh[:, u_wb + (2 * ph + ci) * 128:u_wb + (2 * ph + ci) * 128 + 128],
                                        wsl[:, u_wb + (2 * ph + ci) * 128:u_wb + (2 * ph + ci) * 128 + 128],
                                        vxh[:, ci * TOK + tt * 512:ci * TOK + tt * 512 + 512],
                                        vxl[:, ci * TOK + tt * 512:ci * TOK + tt * 512 + 512],
                                        start=(ci == 0), stop=(ci == 1))
                            i.then_inc(sT, 1)
                        TE(mmu)
                        tcnt = C.bump(sT, 1)

                        def uadd(ve, oc=oc, tt=tt, tcnt=tcnt):
                            ve.wait_ge(sT, tcnt)
                            ve.tensor_tensor(xr[:, oc * TOK + tt * 512:oc * TOK + tt * 512 + 512],
                                             xr[:, oc * TOK + tt * 512:oc * TOK + tt * 512 + 512],
                                             pb[7][:, 0:512], OP.add).then_inc(sV, 1)
                        VE(uadd)
                        C.c["oproj_bank_free"] = C.bump(sV, 1)
                        if ph == 1 and tt == 1:
                            C.c["res_dc%d" % oc] = C.val(sV)
                    C.c["wsh_rd%d" % (u_wb // 2048)] = C.val(sT)
            C.c["ln_gate"] = C.val(sV)
            C.c["proj_banks_free"] = C.val(sV)

        for dc in range(16):
            def ld(sy, dc=dc):
                sy.dma_start(out=xr[:, dc * TOK:(dc + 1) * TOK],
                             in_=x0[dc, :, :]).then_inc(sW, 16)
            SY(ld)
            C.bump(sW, 16)

        def ldid(sy):
            sy.dma_start(out=idsb[:], in_=ident[:, :]).then_inc(sW, 16)
        SY(ldid)
        C.bump(sW, 16)
        wcnt0 = C.val(sW)

        def mkones(ve, wcnt0=wcnt0):
            ve.wait_ge(sW, wcnt0)
            ve.memset(ones[:], 1.0).then_inc(sV, 1)
        VE(mkones)
        C.c["ln_gate"] = C.bump(sV, 1)

        if n_sub >= 1:
            attn_a()
        if n_sub >= 2:
            ffn()
        if n_sub >= 3:
            attn_b()
        if n_sub >= 4:
            ffn()

        vfin, afin, tfin = C.val(sV), C.val(sA), C.val(sT)

        def store(sy, vfin=vfin, afin=afin, tfin=tfin):
            sy.wait_ge(sV, vfin)
            sy.wait_ge(sA, afin)
            sy.wait_ge(sT, tfin)
            for dc in range(16):
                sy.dma_start(out=y[dc, :, :],
                             in_=xr[:, dc * TOK:(dc + 1) * TOK]).then_inc(sOUT, 16)
        SY(store)
        wfin = C.bump("sOUT", 16 * 16)

        def fin(gp, wfin=wfin):
            gp.wait_ge(sOUT, wfin)
        GE(fin)

    return nc


def kernel(**inputs):
    cst = _host_consts()
    xin = np.asarray(inputs["L0_attn_a_x"], dtype=np.float32)

    def core_inputs(core):
        b, p = core // 2, core % 2
        d = {}
        xo = np.ascontiguousarray(xin[b, p * TOK:(p + 1) * TOK, :].T)
        d["x0"] = np.ascontiguousarray(xo.reshape(16, 128, TOK))
        for k in ("cosq2", "sinq2", "cosk2", "sink2", "ident"):
            d[k] = cst[k]
        d["mask_a"] = cst["mask_a"]
        d["mask_c"] = cst["mask_c"]
        hs = slice(p * 1024, (p + 1) * 1024)
        rs = slice(p * 512, (p + 1) * 512)
        d["wq_a"] = _chunked(np.asarray(inputs["L0_attn_a_w_q"])[:, hs])
        d["wdkv"] = _chunked(np.asarray(inputs["L0_attn_a_w_dkv"]))
        d["wuk"] = _chunked(np.asarray(inputs["L0_attn_a_w_uk"])[:, hs])
        d["wuv"] = _chunked(np.asarray(inputs["L0_attn_a_w_uv"])[:, hs])
        d["wqp"] = _chunked(np.asarray(inputs["L0_attn_a_w_q_pos"])[:, rs])
        d["wkp"] = _chunked(np.asarray(inputs["L0_attn_a_w_k_pos"])[:, rs])
        d["wo_a"] = _chunked(np.asarray(inputs["L0_attn_a_w_o"])[hs, :])
        d["wq_b"] = _chunked(np.ascontiguousarray(np.asarray(inputs["L0_attn_b_w_q"])[hs, :].T))
        d["wk_b"] = _chunked(np.ascontiguousarray(np.asarray(inputs["L0_attn_b_w_k"])[hs, :].T))
        d["wv_b"] = _chunked(np.ascontiguousarray(np.asarray(inputs["L0_attn_b_w_v"])[hs, :].T))
        d["wo_b"] = _chunked(np.ascontiguousarray(np.asarray(inputs["L0_attn_b_w_o"])[:, hs].T))
        d["w1"] = _chunked(np.asarray(inputs["L0_ffn_W1"]))
        d["b1"] = np.asarray(inputs["L0_ffn_b1"], dtype=np.float32).reshape(16, 128, 1)
        d["w2"] = np.asarray(inputs["L0_ffn_W2"], dtype=np.float32).reshape(16, 128, 8)
        d["b2"] = np.asarray(inputs["L0_ffn_b2"], dtype=np.float32).reshape(8, 1)
        V = np.asarray(inputs["L0_ffn_V"])
        U = np.asarray(inputs["L0_ffn_U"])
        V2 = np.ascontiguousarray(np.transpose(V, (2, 0, 1)).reshape(2048, 512))
        U2 = np.ascontiguousarray(np.transpose(U, (0, 2, 1)).reshape(512, 2048))
        d["v2w"] = _chunked(V2)
        d["u2w"] = _chunked(U2)
        return {k: (np.ascontiguousarray(v) if k in ("mask_a", "mask_c")
                    else np.ascontiguousarray(v, dtype=np.float32))
                for k, v in d.items()}

    nc = build_program(n_sub=N_SUB, dbg=DBG)
    in_maps = [core_inputs(c) for c in range(8)]
    res = run_bass_kernel_spmd(nc, in_maps, core_ids=list(range(8)))
    out = np.empty((B, T, D), dtype=np.float32)
    for c in range(8):
        b, p = c // 2, c % 2
        yc = np.asarray(res.results[c]["y"]).reshape(2048, TOK)
        out[b, p * TOK:(p + 1) * TOK, :] = yc.T
    return out


# version 11
# speedup vs baseline: 1.2110x; 1.0519x over previous
import numpy as np
import concourse.bass as bass
import concourse.mybir as mybir
from concourse.bass_utils import run_bass_kernel_spmd

F32 = mybir.dt.float32
BF16 = mybir.dt.bfloat16
AF = mybir.ActivationFunctionType
OP = mybir.AluOpType

B, T, D = 4, 2048, 2048
H, DH, DR, DL = 16, 128, 64, 512
NHH = H // 2
TOK = T // 2
SCALE = float(1.0 / np.sqrt(DH))
NEG = -1.0e30

N_SUB = 4
DBG = {"ln": True, "ckv": True, "gather": True, "pairs": 4, "heads": 2,
       "rope": True, "attn": True, "oproj": True, "attn_parts": 5}


def _host_consts():
    BASE, MAX_SEQ, ORIG_MAX = 10000.0, 131072, 4096
    BETA_FAST, BETA_SLOW = 32.0, 1.0
    SCALE_ROPE = 1.0 / np.sqrt(DR)
    base_inv = 1.0 / BASE ** (np.arange(0, DR, 2, dtype=np.float64) / DR)
    scale = MAX_SEQ / ORIG_MAX
    freqs = np.arange(DR // 2, dtype=np.float64)
    ramp = np.clip((freqs - BETA_SLOW) / (BETA_FAST - BETA_SLOW), 0.0, 1.0)
    inv = base_inv * (1 - ramp) + base_inv / scale * ramp
    yarn = 0.1 * np.log(MAX_SEQ / ORIG_MAX) + 1.0
    t = np.arange(T, dtype=np.float64)
    fr = t[None, :] * inv[:, None]
    emb = np.concatenate([fr, fr], axis=0)
    cos, sin = np.cos(emb), np.sin(emb)
    cosq = (cos * yarn * SCALE_ROPE).astype(np.float32)
    sinq = (sin * yarn * SCALE_ROPE).astype(np.float32)
    cosk = (cos * yarn).astype(np.float32)
    sink = (sin * yarn).astype(np.float32)
    def fold(s):
        f = s.copy()
        f[0:32] = -f[0:32]
        return f
    cosq2 = np.concatenate([cosq, cosq], axis=0)
    sinq2 = np.concatenate([fold(sinq), fold(sinq)], axis=0)
    cosk2 = np.concatenate([cosk, cosk], axis=0)
    sink2 = np.concatenate([fold(sink), fold(sink)], axis=0)

    kk = np.arange(T)[:, None]
    qq = np.arange(T)[None, :]
    dist = qq - kk
    um = (((dist >= -256) & (dist <= 256)) | (kk % 64 == 0) | (kk < 128)) & (kk <= qq)
    import ml_dtypes
    mask_a = np.where(um, np.float32(0.0), np.float32(NEG)).astype(ml_dtypes.bfloat16)
    kk5 = np.arange(512)[:, None]
    qq5 = np.arange(512)[None, :]
    mask_c = np.where(kk5 <= qq5, np.float32(0.0), np.float32(NEG)).astype(ml_dtypes.bfloat16)
    ident = np.eye(128, dtype=np.float32)
    return dict(cosq2=cosq2, sinq2=sinq2, cosk2=cosk2, sink2=sink2,
                mask_a=mask_a, mask_c=mask_c, ident=ident)


def _chunked(w):
    Din, Dout = w.shape
    n_dc, n_oc = Din // 128, Dout // 128
    return np.ascontiguousarray(
        w.reshape(n_dc, 128, n_oc, 128).transpose(2, 0, 1, 3))


class Ctr:
    def __init__(self):
        self.c = {}

    def bump(self, key, n=1):
        self.c[key] = self.c.get(key, 0) + n
        return self.c[key]

    def val(self, key):
        return self.c.get(key, 0)


def build_program(n_sub=4, dbg=None):
    dbg = dbg or DBG
    nc = bass.Bass("TRN2", target_bir_lowering=False)

    def inp(name, shape):
        return nc.dram_tensor(name, list(shape), F32, kind="ExternalInput").ap()

    x0 = inp("x0", (16, 128, TOK))
    cosq2 = inp("cosq2", (128, T))
    sinq2 = inp("sinq2", (128, T))
    cosk2 = inp("cosk2", (128, T))
    sink2 = inp("sink2", (128, T))
    mask_a = nc.dram_tensor("mask_a", [T, T], BF16, kind="ExternalInput").ap()
    mask_c = nc.dram_tensor("mask_c", [512, 512], BF16, kind="ExternalInput").ap()
    ident = inp("ident", (128, 128))
    wq_a = inp("wq_a", (NHH, 16, 128, 128))
    wdkv = inp("wdkv", (4, 16, 128, 128))
    wuk = inp("wuk", (NHH, 4, 128, 128))
    wuv = inp("wuv", (NHH, 4, 128, 128))
    wqp = inp("wqp", (4, 16, 128, 128))
    wkp = inp("wkp", (4, 16, 128, 128))
    wo_a = inp("wo_a", (16, 8, 128, 128))
    wq_b = inp("wq_b", (NHH, 16, 128, 128))
    wk_b = inp("wk_b", (NHH, 16, 128, 128))
    wv_b = inp("wv_b", (NHH, 16, 128, 128))
    wo_b = inp("wo_b", (16, 8, 128, 128))
    w1 = inp("w1", (16, 16, 128, 128))
    b1 = inp("b1", (16, 128, 1))
    w2 = inp("w2", (16, 128, 8))
    b2 = inp("b2", (8, 1))
    v2w = inp("v2w", (4, 16, 128, 128))
    u2w = inp("u2w", (16, 4, 128, 128))

    y = nc.dram_tensor("y", [16, 128, TOK], F32, kind="ExternalOutput").ap()

    g1_in = nc.dram_tensor("g1_in", [4096, TOK], BF16)
    g1_out = nc.dram_tensor("g1_out", [2, 4096, TOK], BF16)
    g1c_in = nc.dram_tensor("g1c_in", [1024, TOK], BF16)
    g1c_out = nc.dram_tensor("g1c_out", [2, 1024, TOK], BF16)
    g2_in = nc.dram_tensor("g2_in", [4096, TOK], BF16)
    g2_out = nc.dram_tensor("g2_out", [2, 4096, TOK], BF16)
    rs_ins = [nc.dram_tensor(f"rs_in{i}", [2, 512, TOK], F32) for i in range(4)]
    rs_outs = [nc.dram_tensor(f"rs_out{i}", [512, TOK], F32) for i in range(4)]
    opc = nc.dram_tensor("opc", [2, NHH * 128, T], BF16)
    rowd = nc.dram_tensor("rowd", [8, T], F32)
    alpd = nc.dram_tensor("alpd", [8, TOK], F32)

    gidx = {0: (g1_in, g1_out), 1: (g2_in, g2_out), 2: (g1c_in, g1c_out)}
    C = Ctr()

    from contextlib import ExitStack
    with ExitStack() as ex:
        def sbuf(name, shape, dt=F32):
            return ex.enter_context(nc.sbuf_tensor(name, list(shape), dt))

        xr = sbuf("xr", (128, 16 * TOK))
        arena = sbuf("arena", (128, 16 * TOK))
        def AH(c0, n):
            return arena[:, c0 // 2:(c0 + n) // 2].bitcast(BF16)

        def AL(c0, n):
            return arena[:, 8192 + c0 // 2:8192 + (c0 + n) // 2].bitcast(BF16)

        pq0 = arena[:, 0 * T:1 * T]
        pq1 = arena[:, 1 * T:2 * T]
        pk0 = arena[:, 2 * T:3 * T]
        pk1 = arena[:, 3 * T:4 * T]
        pv0 = arena[:, 4 * T:5 * T]
        pv1 = arena[:, 5 * T:6 * T]
        pqr = arena[:, 6 * T:7 * T]
        pkr = arena[:, 7 * T:8 * T]
        wsh = sbuf("wsh", (128, 2 * 16 * 128), BF16)
        wsl = sbuf("wsl", (128, 2 * 16 * 128), BF16)
        wf32 = sbuf("wf32", (128, 2 * 512), F32)
        xs_hl = sbuf("xs_hl", (128, 3 * 1024), BF16)
        expb = sbuf("expb", (128, 3 * 512), F32)
        mskb = sbuf("mskb", (128, 3 * 512), BF16)
        tabA = sbuf("tabA", (128, 512))
        tabB = sbuf("tabB", (128, 512))
        rotb = sbuf("rotb", (128, 512))
        sc1 = sbuf("sc1", (128, 512))
        sc2 = sbuf("sc2", (128, 512))
        sc3 = sbuf("sc3", (128, 512))
        sc4 = sbuf("sc4", (128, 512))
        eacc = None
        rsb = sbuf("rsb", (128, 1024))
        obh = sbuf("obh", (128, 512), BF16)
        obl = sbuf("obl", (128, 512), BF16)
        rowb = sbuf("rowb", (128, 1024))
        ones = sbuf("ones", (128, 8))
        idsb = sbuf("idsb", (128, 128))
        stat = sbuf("stat", (128, TOK))
        dmy = sbuf("dmy", (1, 8))
        hbuf = sbuf("hbuf", (128, 2 * 512))
        w2sb = sbuf("w2sb", (128, 16 * 8))
        b1sb = sbuf("b1sb", (128, 16))
        b2sb = sbuf("b2sb", (8, 1))
        vxh = sbuf("vxh", (128, 2 * TOK), BF16)
        eacc = vxh[:, 0:1024].bitcast(F32)
        vxl = sbuf("vxl", (128, 2 * TOK), BF16)

        pb = [ex.enter_context(nc.psum_tensor(f"pb{i}", [128, 512], F32))
              for i in range(8)]

        sW = ex.enter_context(nc.semaphore("sW"))
        sG = ex.enter_context(nc.semaphore("sG"))
        sT = ex.enter_context(nc.semaphore("sT"))
        sV = ex.enter_context(nc.semaphore("sV"))
        sA = ex.enter_context(nc.semaphore("sA"))
        sCC = ex.enter_context(nc.semaphore("sCC"))
        sF = [ex.enter_context(nc.semaphore("sF0")),
              ex.enter_context(nc.semaphore("sF1"))]
        sX = [ex.enter_context(nc.semaphore("sX0")),
              ex.enter_context(nc.semaphore("sX1")),
              ex.enter_context(nc.semaphore("sX2"))]
        sM = [ex.enter_context(nc.semaphore("sM0")),
              ex.enter_context(nc.semaphore("sM1")),
              ex.enter_context(nc.semaphore("sM2"))]
        sLN = ex.enter_context(nc.semaphore("sLN"))
        sR = ex.enter_context(nc.semaphore("sR"))
        sRS = ex.enter_context(nc.semaphore("sRS"))
        sRES = ex.enter_context(nc.semaphore("sRES"))
        sOUT = ex.enter_context(nc.semaphore("sOUT"))
        sB = ex.enter_context(nc.semaphore("sB"))
        sGP = ex.enter_context(nc.semaphore("sGP"))
        sOP = ex.enter_context(nc.semaphore("sOP"))

        block = ex.enter_context(nc.Block())
        TE, VE, AE, GE, SY = (block.tensor, block.vector, block.scalar,
                              block.gpsimd, block.sync)

        REG = {"cur": 0}

        def split_weights(wdram, oc, ndc):
            REG["cur"] ^= 1
            wbase = REG["cur"] * 2048
            te_pre = C.val("wsh_rd%d" % REG["cur"])
            for dc0 in range(0, ndc, 4):
                g = min(4, ndc - dc0)
                buf = (dc0 // 4) % 2
                vfree = C.val("wf_freeV%d" % buf)
                gfree = C.val("wf_freeG%d" % buf)
                first = (dc0 == 0)

                def ld(sy, wdram=wdram, oc=oc, dc0=dc0, g=g, buf=buf,
                       vfree=vfree, gfree=gfree, first=first, te_pre=te_pre):
                    if first and te_pre:
                        sy.wait_ge(sT, te_pre)
                    if vfree:
                        sy.wait_ge(sV, vfree)
                    if gfree:
                        sy.wait_ge(sGP, gfree)
                    src_ap = wdram[oc, dc0:dc0 + g, :, :].rearrange("j p c -> p j c")
                    dst_ap = wf32[:, buf * 512:buf * 512 + g * 128].rearrange(
                        "p (j c) -> p j c", j=g)
                    sy.dma_start(out=dst_ap, in_=src_ap).then_inc(sF[buf], 16)
                SY(ld)
                wcnt = C.bump("sF%d" % buf, 16)
                dst = wbase + dc0 * 128

                def hi(ve, buf=buf, dst=dst, wcnt=wcnt, g=g):
                    ve.wait_ge(sF[buf], wcnt)
                    ve.tensor_copy(wsh[:, dst:dst + g * 128],
                                   wf32[:, buf * 512:buf * 512 + g * 128]
                                   ).then_inc(sV, 1)
                VE(hi)
                vcnt = C.bump(sV, 1)
                C.c["wf_freeV%d" % buf] = vcnt

                def lo(gp, buf=buf, dst=dst, vcnt=vcnt, g=g):
                    gp.wait_ge(sV, vcnt)
                    gp.tensor_tensor(
                        wsl[:, dst:dst + g * 128],
                        wf32[:, buf * 512:buf * 512 + g * 128],
                        wsh[:, dst:dst + g * 128],
                        OP.subtract).then_inc(sGP, 1)
                GE(lo)
                gcnt = C.bump("sGP", 1)
                C.c["wf_freeG%d" % buf] = gcnt
                for dc in range(dc0, dc0 + g):
                    C.c["wsplit_dc%d" % dc] = (C.val(sV), C.val("sGP"))
            return wbase

        def mm3(te, psum_ap, wh, wl, rh, rl, start, stop):
            te.matmul(psum_ap, wh, rh, start=start, stop=False,
                      skip_group_check=True)
            te.matmul(psum_ap, wh, rl, start=False, stop=False,
                      skip_group_check=True)
            return te.matmul(psum_ap, wl, rh, start=False, stop=stop,
                             skip_group_check=True)

        def split_to(src_ap, gate_v, ship_sem):
            sem, spre = ship_sem

            def hi(ve, gate_v=gate_v, spre=spre, sem=sem):
                ve.wait_ge(sV, gate_v)
                if spre:
                    ve.wait_ge(sem, spre)
                ve.drain()
                ve.tensor_copy(obh[:], src_ap)
                ve.drain()
                ve.scalar_tensor_tensor(obl[:], src_ap, 1.0, obh[:],
                                        OP.mult, OP.subtract).then_inc(sV, 1)
            VE(hi)
            return C.bump(sV, 1)

        def ln_phase(gdst):
            for blk in range(2):
                t0 = blk * 512
                for dc in range(16):
                    sqb = (sc1, sc2)[dc % 2]
                    tpre = C.val("lnsq_free%d" % (dc % 2))
                    gate_dc = C.val("res_dc%d" % dc) or C.val("ln_gate")

                    def sq(ve, dc=dc, t0=t0, sqb=sqb, tpre=tpre,
                           gate=gate_dc):
                        if tpre:
                            ve.wait_ge(sT, tpre)
                        if gate:
                            ve.wait_ge(sV, gate)
                        ve.drain()
                        ve.tensor_tensor(sqb[:],
                                         xr[:, dc * TOK + t0:dc * TOK + t0 + 512],
                                         xr[:, dc * TOK + t0:dc * TOK + t0 + 512],
                                         OP.mult).then_inc(sV, 1)
                    VE(sq)
                    acnt = C.bump(sV, 1)

                    def mmst(te, dc=dc, t0=t0, sqb=sqb, acnt=acnt,
                             gate=gate_dc):
                        if gate:
                            te.wait_ge(sV, gate)
                        if dc == 0:
                            te.wait_ge(sV, C.val("ln_banks_free"))
                        te.matmul(pb[4][0:1, 0:512], ones[:, 0:1],
                                  xr[:, dc * TOK + t0:dc * TOK + t0 + 512],
                                  start=(dc == 0), stop=False, skip_group_check=True)
                        te.wait_ge(sV, acnt)
                        i = te.matmul(pb[5][0:1, 0:512], ones[:, 0:1], sqb[:],
                                      start=(dc == 0), stop=(dc == 15),
                                      skip_group_check=True)
                        i.then_inc(sT, 1)
                    TE(mmst)
                    tcnt = C.bump(sT, 1)
                    C.c["lnsq_free%d" % (dc % 2)] = tcnt
                tcnt = C.val(sT)
                wpre0 = C.val("sLN")

                def st(ve, blk=blk, tcnt=tcnt, wpre0=wpre0):
                    ve.wait_ge(sT, tcnt)
                    if wpre0:
                        ve.wait_ge(sLN, wpre0)
                    ve.drain()
                    ve.tensor_scalar(stat[0:1, 0:512], pb[4][0:1, 0:512],
                                     1.0 / 2048.0, 0.0, OP.mult, OP.add)
                    ve.tensor_scalar(stat[0:1, 512:1024], pb[5][0:1, 0:512],
                                     1.0 / 2048.0, 1e-5, OP.mult, OP.add).then_inc(sV, 1)
                VE(st)
                vcnt = C.bump(sV, 1)
                C.c["ln_banks_free"] = vcnt

                def shipm(sy, blk=blk, vcnt=vcnt):
                    sy.wait_ge(sV, vcnt)
                    sy.dma_start(out=rowd.ap()[0, blk * 512:blk * 512 + 512],
                                 in_=stat[0:1, 0:512]).then_inc(sLN, 16)
                SY(shipm)
                wcnt = C.bump("sLN", 16)

                def st2(ve, blk=blk, wcnt=wcnt):
                    ve.wait_ge(sLN, wcnt)
                    ve.tensor_tensor(stat[0:1, 0:512], stat[0:1, 0:512],
                                     stat[0:1, 0:512], OP.mult)
                    ve.drain()
                    ve.tensor_tensor(stat[0:1, 512:1024], stat[0:1, 512:1024],
                                     stat[0:1, 0:512], OP.subtract).then_inc(sV, 1)
                VE(st2)
                vcnt2 = C.bump(sV, 1)

                def rst(ae, blk=blk, vcnt2=vcnt2):
                    ae.wait_ge(sV, vcnt2)
                    ae.activation(out=stat[0:1, 512:1024], in_=stat[0:1, 512:1024],
                                  func=AF.Sqrt).then_inc(sA, 1)
                AE(rst)
                acnt = C.bump(sA, 1)

                def rcp(ve, blk=blk, acnt=acnt):
                    ve.wait_ge(sA, acnt)
                    ve.reciprocal(stat[0:1, 512:1024],
                                  stat[0:1, 512:1024]).then_inc(sV, 1)
                VE(rcp)
                vcnt3 = C.bump(sV, 1)

                def shipr(sy, blk=blk, vcnt3=vcnt3):
                    sy.wait_ge(sV, vcnt3)
                    sy.dma_start(out=rowd.ap()[1, blk * 512:blk * 512 + 512],
                                 in_=stat[0:1, 512:1024]).then_inc(sLN, 16)
                SY(shipr)
                C.bump("sLN", 16)

            wcnt0 = C.val("sLN")

            vgate = C.val(sV)

            def bcast(sy, wcnt0=wcnt0, vgate=vgate):
                sy.wait_ge(sLN, wcnt0)
                if vgate:
                    sy.wait_ge(sV, vgate)
                sy.dma_start(out=rowb[:, 0:512],
                             in_=rowd.ap()[0:1, 0:512].to_broadcast([128, 512])).then_inc(sLN, 16)
                sy.dma_start(out=rowb[:, 512:1024],
                             in_=rowd.ap()[1:2, 0:512].to_broadcast([128, 512])).then_inc(sLN, 16)
            SY(bcast)
            wcnt = C.bump("sLN", 32)

            for half in range(2):
                if half == 1:
                    vpre_h = C.val(sV)

                    def bcast2(sy, vpre_h=vpre_h):
                        sy.wait_ge(sV, vpre_h)
                        sy.dma_start(out=rowb[:, 0:512],
                                     in_=rowd.ap()[0:1, 512:1024].to_broadcast([128, 512])).then_inc(sLN, 16)
                        sy.dma_start(out=rowb[:, 512:1024],
                                     in_=rowd.ap()[1:2, 512:1024].to_broadcast([128, 512])).then_inc(sLN, 16)
                    SY(bcast2)
                    wcnt = C.bump("sLN", 32)
                for dc in range(16):
                    def app(ve, dc=dc, wcnt=wcnt, half=half,
                            gpre=C.val("sGP")):
                        if dc == 0:
                            ve.wait_ge(sLN, wcnt)
                        if gpre:
                            ve.wait_ge(sGP, gpre)
                        ve.drain()
                        c0 = dc * TOK + half * 512
                        ve.tensor_tensor(sc1[:], xr[:, c0:c0 + 512],
                                         rowb[:, 0:512], OP.subtract)
                        ve.drain()
                        ve.tensor_tensor(sc1[:], sc1[:],
                                         rowb[:, 512:1024], OP.mult).then_inc(sV, 1)
                    VE(app)
                    vcnt = C.bump(sV, 1)

                    def hi(ve, dc=dc, half=half, vcnt=vcnt):
                        ve.wait_ge(sV, vcnt)
                        ve.tensor_copy(AH(dc * TOK + half * 512, 512),
                                       sc1[:]).then_inc(sV, 1)
                    VE(hi)
                    vcnt2 = C.bump(sV, 1)

                    def lo(gp, dc=dc, half=half, vcnt2=vcnt2):
                        gp.wait_ge(sV, vcnt2)
                        gp.tensor_tensor(AL(dc * TOK + half * 512, 512),
                                         sc1[:],
                                         AH(dc * TOK + half * 512, 512),
                                         OP.subtract).then_inc(sGP, 1)
                    GE(lo)
                    C.bump("sGP", 1)
                    C.c["lnsp_v%d_%d" % (dc, half)] = vcnt2
                    C.c["lnsp_g%d_%d" % (dc, half)] = C.val("sGP")
            vsplit = C.val(sV)
            C.c["ln_split_doneG"] = C.val("sGP")
            if gdst is not None:
                gbuf, _ = gidx[gdst]
                for dc in range(16):
                    def shp(gp, dc=dc, vsplit=vsplit, gbuf=gbuf):
                        if dc == 0:
                            gp.wait_ge(sV, vsplit)
                            gp.wait_ge(sGP, C.val("sGP"))
                        gp.dma_start(out=gbuf.ap()[dc * 256:dc * 256 + 128, :],
                                     in_=AH(dc * TOK, TOK)).then_inc(sG, 16)
                        gp.dma_start(out=gbuf.ap()[dc * 256 + 128:dc * 256 + 256, :],
                                     in_=AL(dc * TOK, TOK)).then_inc(sG, 16)
                    GE(shp)
                    C.bump(sG, 32)
            C.c["ln_split_done"] = C.val(sV)
            C.c["ln_split_doneG"] = C.val("sGP")

        def gather(gdst):
            gcnt = C.val(sG)

            def coll(gp, gcnt=gcnt, gdst=gdst):
                gp.wait_ge(sG, gcnt)
                gi, go = gidx[gdst]
                gp.collective_compute(
                    "AllGather", OP.bypass,
                    replica_groups=[[0, 1], [2, 3], [4, 5], [6, 7]],
                    ins=[gi.ap()],
                    outs=[go.ap()],
                ).then_inc(sCC, 1)
            GE(coll)
            return C.bump(sCC, 1)

        def stream_proj(gdst, ndc, row0, outs, cc_need):
            _, gout = gidx[gdst]
            wsp = [C.val("wsplit_dc%d" % d) for d in range(ndc)]
            for tt in range(4):
                rank, col0 = tt // 2, (tt % 2) * 512
                banks = [pb[(tt % 2) * 2 + k] for k in range(len(outs))]
                bfree = [C.val("proj_bank_free%d" % ((tt % 2) * 2 + k))
                         for k in range(len(outs))]
                for dc in range(ndc):
                    buf = dc % 3
                    tfree = C.val("xs_free%d" % buf)

                    def ldx(sy, rank=rank, col0=col0, dc=dc, buf=buf, tfree=tfree,
                            gout=gout, row0=row0, cc_need=cc_need):
                        if tfree:
                            sy.wait_ge(sT, tfree)
                        sy.wait_ge(sCC, cc_need)
                        r0 = row0 + dc * 256
                        src_ap = gout.ap()[rank, r0:r0 + 256,
                                           col0:col0 + 512].rearrange(
                            "(h p) c -> p h c", p=128)
                        dst_ap = xs_hl[:, buf * 1024:buf * 1024 + 1024].rearrange(
                            "p (h c) -> p h c", h=2)
                        sy.dma_start(out=dst_ap, in_=src_ap).then_inc(sX[buf], 16)
                    SY(ldx)
                    wcnt = C.bump("sX%d" % buf, 16)

                    def mmp(te, dc=dc, buf=buf, wcnt=wcnt, ndc=ndc, banks=banks,
                            wsp=wsp, bfree=bfree, outs=outs):
                        te.wait_ge(sX[buf], wcnt)
                        te.wait_ge(sV, wsp[dc][0])
                        te.wait_ge(sGP, wsp[dc][1])
                        for k, (dsto, mode, wbase) in enumerate(outs):
                            if dc == 0 and bfree[k]:
                                te.wait_ge(sV, bfree[k])
                            w0 = wbase + dc * 128
                            i = mm3(te, banks[k][:, 0:512],
                                    wsh[:, w0:w0 + 128], wsl[:, w0:w0 + 128],
                                    xs_hl[:, buf * 1024:buf * 1024 + 512],
                                    xs_hl[:, buf * 1024 + 512:buf * 1024 + 1024],
                                    start=(dc == 0), stop=(dc == ndc - 1))
                        i.then_inc(sT, 1)
                    TE(mmp)
                    tcnt = C.bump(sT, 1)
                    C.c["xs_free%d" % buf] = tcnt
                tfin = C.val(sT)

                for k, (dsto, mode, wbase) in enumerate(outs):
                    bidx = (tt % 2) * 2 + k
                    if mode == "vt":
                        def ev0(ve, tfin=tfin, bank=banks[k]):
                            ve.wait_ge(sT, tfin)
                            ve.drain()
                            ve.tensor_copy(sc4[:], bank[:, 0:512]).then_inc(sV, 1)
                        VE(ev0)
                        vcnt = C.bump(sV, 1)
                        C.c["proj_bank_free%d" % bidx] = vcnt

                        def tr(te, vcnt=vcnt):
                            te.wait_ge(sV, vcnt)
                            te.wait_ge(sV, C.val("vt_ps_free"))
                            for j in range(4):
                                i = te.matmul(pb[6][:, j * 128:(j + 1) * 128],
                                              sc4[:, j * 128:(j + 1) * 128], idsb[:],
                                              is_transpose=True, start=(j == 0),
                                              stop=(j == 3), skip_group_check=True)
                            i.then_inc(sT, 1)
                        TE(tr)
                        tcnt2 = C.bump(sT, 1)

                        def ev1(ve, tt=tt, dsto=dsto, tcnt2=tcnt2):
                            ve.wait_ge(sT, tcnt2)
                            for j in range(4):
                                kt = tt * 4 + j
                                i = ve.tensor_copy(
                                    arena[:, dsto * T + kt * 128:dsto * T + (kt + 1) * 128],
                                    pb[6][:, j * 128:(j + 1) * 128])
                            i.then_inc(sV, 1)
                        VE(ev1)
                        C.c["vt_ps_free"] = C.bump(sV, 1)
                    else:
                        scl = SCALE if mode == "qs" else 1.0

                        def ev(ve, tt=tt, dsto=dsto, scl=scl, tfin=tfin,
                               bank=banks[k]):
                            ve.wait_ge(sT, tfin)
                            ve.drain()
                            ve.tensor_scalar(
                                arena[:, dsto * T + tt * 512:dsto * T + tt * 512 + 512],
                                bank[:, 0:512], scl, 0.0, OP.mult,
                                OP.add).then_inc(sV, 1)
                        VE(ev)
                        C.c["proj_bank_free%d" % bidx] = C.bump(sV, 1)
            C.c["proj_banks_free"] = C.val(sV)
            for dsto, mode, wbase in outs:
                C.c["wsh_rd%d" % (wbase // 2048)] = C.val(sT)

        def rope_pair():
            for (tabc, tabs, dof) in ((cosq2, sinq2, 6), (cosk2, sink2, 7)):
                apre = C.val("proj_banks_free")
                for cb in range(4):
                    vpre2 = C.val("rope_tab_rd")
                    c0 = dof * T + cb * 512

                    def ldt(sy, tabc=tabc, tabs=tabs, cb=cb, vpre2=vpre2, c0=c0,
                            apre=apre):
                        if vpre2:
                            sy.wait_ge(sV, vpre2)
                        if apre:
                            sy.wait_ge(sV, apre)
                        sy.dma_start(out=tabA[:],
                                     in_=tabc[:, cb * 512:(cb + 1) * 512]).then_inc(sR, 16)
                        sy.dma_start(out=tabB[:],
                                     in_=tabs[:, cb * 512:(cb + 1) * 512]).then_inc(sR, 16)
                        sy.dma_start(out=rotb[0:32, :],
                                     in_=arena[32:64, c0:c0 + 512]).then_inc(sR, 16)
                        sy.dma_start(out=rotb[32:64, :],
                                     in_=arena[0:32, c0:c0 + 512]).then_inc(sR, 16)
                        sy.dma_start(out=rotb[64:96, :],
                                     in_=arena[96:128, c0:c0 + 512]).then_inc(sR, 16)
                        sy.dma_start(out=rotb[96:128, :],
                                     in_=arena[64:96, c0:c0 + 512]).then_inc(sR, 16)
                    SY(ldt)
                    wcnt = C.bump("sR", 96)

                    def rp(ve, cb=cb, wcnt=wcnt, c0=c0):
                        ve.wait_ge(sR, wcnt)
                        ve.tensor_tensor(arena[:, c0:c0 + 512], arena[:, c0:c0 + 512],
                                         tabA[:], OP.mult)
                        ve.tensor_tensor(rotb[:], rotb[:], tabB[:], OP.mult)
                        ve.drain()
                        ve.tensor_tensor(arena[:, c0:c0 + 512], arena[:, c0:c0 + 512],
                                         rotb[:], OP.add).then_inc(sV, 1)
                    VE(rp)
                    C.bump(sV, 1)
                    C.c["rope_tab_rd"] = C.val(sV)

        def attention_head(sub, h, hh):
            qo = hh
            ko = 2 + hh
            vo = 4 + hh
            rr0 = hh * 64
            gate_pb = C.val("proj_banks_free")
            gate_rp = C.val("rope_tab_rd")
            for J in range(4):
                ntiles = 4 * J + 4
                for i in range(ntiles):
                    sc_bank = pb[i % 2]
                    diag = (sub == 1) and (i >= 4 * J)
                    use_mask = ((sub == 0) or diag) and dbg["attn_parts"] >= 2
                    mw = None
                    mb = i % 3
                    if use_mask:
                        vfree = C.val("msk_free%d" % mb)

                        def ldm(sy, i=i, J=J, mb=mb, vfree=vfree, sub=sub):
                            if vfree:
                                sy.wait_ge(sV, vfree)
                            if sub == 0:
                                src = mask_a[i * 128:(i + 1) * 128, J * 512:J * 512 + 512]
                            else:
                                ii = i - 4 * J
                                src = mask_c[ii * 128:(ii + 1) * 128, 0:512]
                            sy.dma_start(out=mskb[:, mb * 512:mb * 512 + 512],
                                         in_=src).then_inc(sM[mb], 16)
                        AE(ldm)
                        mw = C.bump("sM%d" % mb, 16)

                    apre = C.val("exp_done_t%d" % (i % 2))

                    def mmsc(te, i=i, J=J, sc_bank=sc_bank, qo=qo, ko=ko,
                             rr0=rr0, apre=apre, sub=sub, gate_pb=gate_pb,
                             gate_rp=gate_rp):
                        if apre:
                            te.wait_ge(sA, apre)
                        if i == 0 and J == 0:
                            if gate_pb:
                                te.wait_ge(sV, gate_pb)
                            if gate_rp:
                                te.wait_ge(sV, gate_rp)
                        ins = te.matmul(sc_bank[:, 0:512],
                                        arena[:, ko * T + i * 128:ko * T + (i + 1) * 128],
                                        arena[:, qo * T + J * 512:qo * T + J * 512 + 512],
                                        start=True,
                                        stop=(sub == 1), skip_group_check=True)
                        if sub == 0:
                            ins = te.matmul(sc_bank[:, 0:512],
                                            arena[rr0:rr0 + 64, 7 * T + i * 128:7 * T + (i + 1) * 128],
                                            arena[rr0:rr0 + 64, 6 * T + J * 512:6 * T + J * 512 + 512],
                                            start=False, stop=True, skip_group_check=True)
                        ins.then_inc(sT, 1)
                    TE(mmsc)
                    tcnt = C.bump(sT, 1)

                    if use_mask:
                        def mska(ve, sc_bank=sc_bank, mb=mb, tcnt=tcnt, mw=mw):
                            ve.wait_ge(sT, tcnt)
                            ve.wait_ge(sM[mb], mw)
                            ve.tensor_tensor(sc_bank[:, 0:512], sc_bank[:, 0:512],
                                             mskb[:, mb * 512:mb * 512 + 512],
                                             OP.add).then_inc(sV, 1)
                        VE(mska)
                        vcnt = C.bump(sV, 1)
                        C.c["msk_free%d" % mb] = vcnt
                        dep = ("V", vcnt)
                    else:
                        dep = ("T", tcnt)

                    if dbg["attn_parts"] < 3:
                        C.c["exp_done_t%d" % (i % 2)] = dep[1] if dep[0] == "V" else 0
                        if dep[0] != "V":
                            def drain(ve, sc_bank=sc_bank, tcnt=tcnt):
                                ve.wait_ge(sT, tcnt)
                                ve.tensor_copy(sc4[:], sc_bank[:, 0:512]).then_inc(sV, 1)
                            VE(drain)
                            C.c["exp_done_t%d" % (i % 2)] = 0
                            C.c["drain_v"] = C.bump(sV, 1)
                        continue
                    eb = i % 3
                    epre = C.val("exp_rd_%d" % eb)
                    epreV = C.val("exp_rdv_%d" % eb)

                    def expf(ae, sc_bank=sc_bank, eb=eb, dep=dep, epre=epre,
                             epreV=epreV):
                        if dep[0] == "V":
                            ae.wait_ge(sV, dep[1])
                        else:
                            ae.wait_ge(sT, dep[1])
                        if epre:
                            ae.wait_ge(sT, epre)
                        if epreV:
                            ae.wait_ge(sV, epreV)
                        ae.activation(out=expb[:, eb * 512:eb * 512 + 512],
                                      in_=sc_bank[:, 0:512], func=AF.Exp).then_inc(sA, 1)
                    AE(expf)
                    acnt = C.bump(sA, 1)
                    C.c["exp_done_t%d" % (i % 2)] = acnt
                    C.c["exp_rd"] = acnt
                    if dbg["attn_parts"] < 4:
                        continue

                    def mmav(te, i=i, eb=eb, vo=vo, acnt=acnt, J=J,
                             avfree=C.val("avrs_free%d" % (J % 4))):
                        te.wait_ge(sA, acnt)
                        if i == 0 and avfree:
                            te.wait_ge(sV, avfree)
                        av_b = pb[2 + J]
                        te.matmul(av_b[:, 0:512],
                                  arena[:, vo * T + i * 128:vo * T + (i + 1) * 128],
                                  expb[:, eb * 512:eb * 512 + 512],
                                  start=(i == 0), stop=False,
                                  skip_group_check=True).then_inc(sT, 1)
                    TE(mmav)
                    tcnt2 = C.bump(sT, 1)
                    C.c["exp_rd_%d" % eb] = tcnt2

                    aprev = C.val("acc_prev")
                    efree = C.val("eacc_free") if i == 0 else 0

                    def accf(ve, i=i, eb=eb, acnt=acnt, aprev=aprev,
                             efree=efree):
                        ve.wait_ge(sA, acnt)
                        if aprev:
                            ve.wait_ge(sV, aprev)
                        if i == 0:
                            if efree:
                                ve.wait_ge(sT, efree)
                            ve.tensor_copy(eacc[:],
                                           expb[:, eb * 512:eb * 512 + 512]).then_inc(sV, 1)
                        else:
                            ve.tensor_tensor(eacc[:], eacc[:],
                                             expb[:, eb * 512:eb * 512 + 512],
                                             OP.add).then_inc(sV, 1)
                    VE(accf)
                    vacc = C.bump(sV, 1)
                    C.c["acc_prev"] = vacc
                    C.c["exp_rdv_%d" % eb] = vacc

                if dbg["attn_parts"] < 5:
                    continue
                vlast = C.val("acc_prev")
                rfree = C.val("rsrow_free")

                def mmrs(te, vlast=vlast, rfree=rfree):
                    te.wait_ge(sV, vlast)
                    if rfree:
                        te.wait_ge(sV, rfree)
                    te.matmul(pb[7][0:1, 0:512], ones[:, 0:1], eacc[:],
                              start=True, stop=True,
                              skip_group_check=True).then_inc(sT, 1)
                TE(mmrs)
                trs = C.bump(sT, 1)
                C.c["eacc_free"] = trs
                wship = C.val("sRS")

                def rsum(ve, trs=trs, wship=wship):
                    if wship:
                        ve.wait_ge(sRS, wship)
                    ve.wait_ge(sT, trs)
                    ve.drain()
                    ve.reciprocal(sc1[0:1, :], pb[7][0:1, 0:512]).then_inc(sV, 1)
                VE(rsum)
                vcnt = C.bump(sV, 1)
                C.c["rsrow_free"] = vcnt

                def shiprs(sy, vcnt=vcnt):
                    sy.wait_ge(sV, vcnt)
                    sy.dma_start(out=rowd.ap()[2, 0:512], in_=sc1[0:1, :]).then_inc(sRS, 16)
                AE(shiprs)
                wcnt = C.bump("sRS", 16)
                gpre = C.val(sG)

                def bcrs(sy, wcnt=wcnt, gpre=gpre):
                    sy.wait_ge(sRS, wcnt)
                    if gpre:
                        sy.wait_ge(sG, gpre)
                    sy.dma_start(out=sc3[:],
                                 in_=rowd.ap()[2:3, 0:512].to_broadcast([128, 512])).then_inc(sRS, 16)
                AE(bcrs)
                wcnt2 = C.bump("sRS", 16)

                def norm(ve, wcnt2=wcnt2, gpre=C.val("sGP"), J=J):
                    ve.wait_ge(sRS, wcnt2)
                    if gpre:
                        ve.wait_ge(sGP, gpre)
                    ve.drain()
                    av_b = pb[2 + J]
                    ve.tensor_tensor(sc4[:], av_b[:, 0:512], sc3[:],
                                     OP.mult).then_inc(sV, 1)
                VE(norm)
                vcnt2 = C.bump(sV, 1)
                C.c["avrs_free%d" % (J % 4)] = vcnt2
                C.c["avrs_free"] = vcnt2
                C.c["rs_ship"] = wcnt2

                ocnt = split_to(sc4[:], vcnt2, (sOP, C.val("sOP")))

                def oshp(sy, ocnt=ocnt, h=h, J=J):
                    sy.wait_ge(sV, ocnt)
                    sy.dma_start(out=opc.ap()[0, h * 128:(h + 1) * 128,
                                              J * 512:J * 512 + 512],
                                 in_=obh[:]).then_inc(sOP, 16)
                    sy.dma_start(out=opc.ap()[1, h * 128:(h + 1) * 128,
                                              J * 512:J * 512 + 512],
                                 in_=obl[:]).then_inc(sOP, 16)
                AE(oshp)
                C.bump("sOP", 32)

        def oproj_rs(sub):
            wo = wo_a if sub == 0 else wo_b
            gpre = C.val("sOP")
            for oc in range(16):
                wb = split_weights(wo, oc, 8)
                for tt in range(4):
                    for dc in range(8):
                        buf = dc % 3
                        tfree = C.val("xs_free%d" % buf)
                        wsplit = C.val("wsplit_dc%d" % dc)

                        def ldo(sy, dc=dc, buf=buf, tfree=tfree, tt=tt, gpre=gpre):
                            if tfree:
                                sy.wait_ge(sT, tfree)
                            sy.wait_ge(sOP, gpre)
                            src_ap = opc.ap()[:, dc * 128:(dc + 1) * 128,
                                              tt * 512:tt * 512 + 512].rearrange(
                                "h p c -> p h c")
                            dst_ap = xs_hl[:, buf * 1024:buf * 1024 + 1024].rearrange(
                                "p (h c) -> p h c", h=2)
                            sy.dma_start(out=dst_ap, in_=src_ap).then_inc(sX[buf], 16)
                        SY(ldo)
                        wcnt = C.bump("sX%d" % buf, 16)

                        def mmo(te, dc=dc, buf=buf, wcnt=wcnt, wsplit=wsplit,
                                wb=wb):
                            te.wait_ge(sX[buf], wcnt)
                            te.wait_ge(sV, wsplit[0])
                            te.wait_ge(sGP, wsplit[1])
                            if dc == 0:
                                te.wait_ge(sV, C.val("oproj_bank_free"))
                            i = mm3(te, pb[7][:, 0:512],
                                    wsh[:, wb + dc * 128:wb + dc * 128 + 128],
                                    wsl[:, wb + dc * 128:wb + dc * 128 + 128],
                                    xs_hl[:, buf * 1024:buf * 1024 + 512],
                                    xs_hl[:, buf * 1024 + 512:buf * 1024 + 1024],
                                    start=(dc == 0), stop=(dc == 7))
                            i.then_inc(sT, 1)
                        TE(mmo)
                        tcnt = C.bump(sT, 1)
                        C.c["xs_free%d" % buf] = tcnt

                    gfree = C.val("oev_shipped")

                    def evo(ve, tcnt=tcnt, gfree=gfree):
                        ve.wait_ge(sT, tcnt)
                        if gfree:
                            ve.wait_ge(sG, gfree)
                        ve.drain()
                        ve.tensor_copy(sc1[:], pb[7][:, 0:512]).then_inc(sV, 1)
                    VE(evo)
                    vcnt = C.bump(sV, 1)
                    C.c["oproj_bank_free"] = vcnt

                    def sho(gp, vcnt=vcnt, oc=oc, tt=tt):
                        gp.wait_ge(sV, vcnt)
                        gp.dma_start(out=rs_ins[oc // 4].ap()[tt // 2, (oc % 4) * 128:(oc % 4 + 1) * 128,
                                                              (tt % 2) * 512:(tt % 2) * 512 + 512],
                                     in_=sc1[:]).then_inc(sG, 16)
                    GE(sho)
                    C.c["oev_shipped"] = C.bump(sG, 16)
                C.c["wsh_rd%d" % (wb // 2048)] = C.val(sT)
                if oc % 4 == 3:
                    gq = C.val(sG)

                    def collq(gp, gq=gq, q=oc // 4):
                        gp.wait_ge(sG, gq)
                        gp.collective_compute(
                            "ReduceScatter", OP.add,
                            replica_groups=[[0, 1], [2, 3], [4, 5], [6, 7]],
                            ins=[rs_ins[q].ap()],
                            outs=[rs_outs[q].ap()],
                        ).then_inc(sCC, 1)
                    GE(collq)
                    C.c["rs_cc%d" % (oc // 4)] = C.bump(sCC, 1)

            for dc in range(16):
                vfree = C.val("res_free")
                cc2 = C.val("rs_cc%d" % (dc // 4))

                def ldr(sy, dc=dc, cc2=cc2, vfree=vfree):
                    if vfree:
                        sy.wait_ge(sV, vfree)
                    sy.wait_ge(sCC, cc2)
                    sy.dma_start(out=rsb[:, 0:512],
                                 in_=rs_outs[dc // 4].ap()[(dc % 4) * 128:(dc % 4 + 1) * 128,
                                                           0:512]).then_inc(sRES, 16)
                    sy.dma_start(out=rsb[:, 512:1024],
                                 in_=rs_outs[dc // 4].ap()[(dc % 4) * 128:(dc % 4 + 1) * 128,
                                                           512:1024]).then_inc(sRES, 16)
                SY(ldr)
                wcnt = C.bump("sRES", 32)

                def addr_(ve, dc=dc, wcnt=wcnt):
                    ve.wait_ge(sRES, wcnt)
                    ve.drain()
                    ve.tensor_tensor(xr[:, dc * TOK:dc * TOK + 512],
                                     xr[:, dc * TOK:dc * TOK + 512], rsb[:, 0:512], OP.add)
                    ve.tensor_tensor(xr[:, dc * TOK + 512:dc * TOK + 1024],
                                     xr[:, dc * TOK + 512:dc * TOK + 1024], rsb[:, 512:1024],
                                     OP.add).then_inc(sV, 1)
                VE(addr_)
                C.c["res_free"] = C.bump(sV, 1)
                C.c["res_dc%d" % dc] = C.val(sV)
            C.c["ln_gate"] = C.val(sV)

        def attn_a():
            if not dbg["ln"]:
                return
            ln_phase(gdst=0)
            cc = gather(0)
            for oc in range(4):
                wb = split_weights(wdkv, oc, 16)
                for tt in range(2):
                    bank = pb[2 + (oc % 2)]
                    bfree = C.val("ckv_bank_free%d" % (oc % 2))
                    lnv = [C.val("lnsp_v%d_%d" % (d, tt)) for d in range(16)]
                    lng = [C.val("lnsp_g%d_%d" % (d, tt)) for d in range(16)]
                    wsp = [C.val("wsplit_dc%d" % d) for d in range(16)]

                    def mmckv(te, tt=tt, wb=wb, bank=bank, bfree=bfree,
                              lnv=lnv, lng=lng, wsp=wsp):
                        if bfree:
                            te.wait_ge(sV, bfree)
                        for dc in range(16):
                            te.wait_ge(sV, max(wsp[dc][0], lnv[dc]))
                            te.wait_ge(sGP, max(wsp[dc][1], lng[dc]))
                            w0 = wb + dc * 128
                            i = mm3(te, bank[:, 0:512],
                                    wsh[:, w0:w0 + 128], wsl[:, w0:w0 + 128],
                                    AH(dc * TOK + tt * 512, 512),
                                    AL(dc * TOK + tt * 512, 512),
                                    start=(dc == 0), stop=(dc == 15))
                        i.then_inc(sT, 1)
                    TE(mmckv)
                    tcnt = C.bump(sT, 1)

                    def evc(ve, bank=bank, tcnt=tcnt, gpre=C.val("sGP")):
                        ve.wait_ge(sT, tcnt)
                        if gpre:
                            ve.wait_ge(sGP, gpre)
                        ve.drain()
                        ve.tensor_copy(sc4[:], bank[:, 0:512]).then_inc(sV, 1)
                    VE(evc)
                    vcnt = C.bump(sV, 1)
                    C.c["ckv_bank_free%d" % (oc % 2)] = vcnt

                    ocnt = split_to(sc4[:], vcnt, (sG, C.val(sG)))

                    def cshp(gp, ocnt=ocnt, oc=oc, tt=tt):
                        gp.wait_ge(sV, ocnt)
                        gp.dma_start(out=g1c_in.ap()[oc * 256:oc * 256 + 128,
                                                     tt * 512:tt * 512 + 512],
                                     in_=obh[:]).then_inc(sG, 16)
                        gp.dma_start(out=g1c_in.ap()[oc * 256 + 128:oc * 256 + 256,
                                                     tt * 512:tt * 512 + 512],
                                     in_=obl[:]).then_inc(sG, 16)
                    GE(cshp)
                    C.bump(sG, 32)
                C.c["wsh_rd%d" % (wb // 2048)] = C.val(sT)

            wq_next = [split_weights(wq_a, 0, 16), split_weights(wq_a, 1, 16)]
            ccc = gather(2)
            np_ = min(NHH // 2, dbg["pairs"])
            for hp in range(np_):
                stream_proj(gdst=0, ndc=16, row0=0,
                            outs=[(0, "qs", wq_next[0]), (1, "qs", wq_next[1])],
                            cc_need=cc)
                wb1 = split_weights(wqp, hp, 16)
                wb2 = split_weights(wkp, hp, 16)
                stream_proj(gdst=0, ndc=16, row0=0,
                            outs=[(6, "n", wb1), (7, "n", wb2)], cc_need=cc)
                wb1 = split_weights(wuk, 2 * hp, 4)
                wb2 = split_weights(wuv, 2 * hp, 4)
                stream_proj(gdst=2, ndc=4, row0=0,
                            outs=[(2, "n", wb1), (4, "vt", wb2)], cc_need=ccc)
                wb1 = split_weights(wuk, 2 * hp + 1, 4)
                wb2 = split_weights(wuv, 2 * hp + 1, 4)
                stream_proj(gdst=2, ndc=4, row0=0,
                            outs=[(3, "n", wb1), (5, "vt", wb2)], cc_need=ccc)
                if dbg["rope"]:
                    rope_pair()
                if hp + 1 < np_:
                    wq_next = [split_weights(wq_a, 2 * hp + 2, 16),
                               split_weights(wq_a, 2 * hp + 3, 16)]
                if dbg["attn"]:
                    for hh in range(dbg["heads"]):
                        attention_head(0, 2 * hp + hh, hh)
            C.c["ln_banks_free"] = C.val(sV)
            if dbg["oproj"]:
                oproj_rs(0)

        def attn_b():
            ln_phase(gdst=1)
            wq_next = [split_weights(wq_b, 0, 16), split_weights(wq_b, 1, 16)]
            cc = gather(1)
            for hp in range(NHH // 2):
                stream_proj(gdst=1, ndc=16, row0=0,
                            outs=[(0, "qs", wq_next[0]), (1, "qs", wq_next[1])],
                            cc_need=cc)
                wb1 = split_weights(wk_b, 2 * hp, 16)
                wb2 = split_weights(wk_b, 2 * hp + 1, 16)
                stream_proj(gdst=1, ndc=16, row0=0,
                            outs=[(2, "n", wb1), (3, "n", wb2)], cc_need=cc)
                wb1 = split_weights(wv_b, 2 * hp, 16)
                wb2 = split_weights(wv_b, 2 * hp + 1, 16)
                stream_proj(gdst=1, ndc=16, row0=0,
                            outs=[(4, "vt", wb1), (5, "vt", wb2)], cc_need=cc)
                if hp + 1 < NHH // 2:
                    wq_next = [split_weights(wq_b, 2 * hp + 2, 16),
                               split_weights(wq_b, 2 * hp + 3, 16)]
                attention_head(1, 2 * hp, 0)
                attention_head(1, 2 * hp + 1, 1)
            C.c["ln_banks_free"] = C.val(sV)
            oproj_rs(1)

        def ffn():
            ln_phase(gdst=None)
            vpre = C.val("ln_split_done")

            def ldw2(sy):
                sy.wait_ge(sT, C.val("w2sb_rd"))
                for oc in range(16):
                    sy.dma_start(out=w2sb[:, oc * 8:(oc + 1) * 8],
                                 in_=w2[oc, :, :]).then_inc(sB, 16)
                    sy.dma_start(out=b1sb[:, oc:oc + 1],
                                 in_=b1[oc, :, :]).then_inc(sB, 16)
                sy.dma_start(out=b2sb[:, :], in_=b2[:, :]).then_inc(sB, 16)
            SY(ldw2)
            w2w = C.bump("sB", 16 * 33)

            for oc in range(16):
                wb = split_weights(w1, oc, 16)
                for tt in range(2):
                    if True:
                        slot = oc % 2
                        ocp = oc // 2
                        hfree = C.val("h_bank_free%d" % slot)

                        def mmh(te, tt=tt, slot=slot, oc=oc, wb=wb,
                                hfree=hfree,
                                wsp=[C.val("wsplit_dc%d" % d) for d in range(16)],
                                lnv=[C.val("lnsp_v%d_%d" % (d, tt)) for d in range(16)],
                                lng=[C.val("lnsp_g%d_%d" % (d, tt)) for d in range(16)]):
                            if hfree:
                                te.wait_ge(sA, hfree)
                            if oc == 0 and tt == 0:
                                te.wait_ge(sV, C.val("ln_banks_free"))
                                te.wait_ge(sV, C.val("proj_banks_free"))
                                te.wait_ge(sA, C.val("exp_rd"))
                                te.wait_ge(sV, C.val("avrs_free"))
                            for dc in range(16):
                                te.wait_ge(sV, max(wsp[dc][0], lnv[dc]))
                                te.wait_ge(sGP, max(wsp[dc][1], lng[dc]))
                                w0 = wb + dc * 128
                                i = mm3(te, pb[slot][:, 0:512],
                                        wsh[:, w0:w0 + 128], wsl[:, w0:w0 + 128],
                                        AH(dc * TOK + tt * 512, (dc * TOK + tt * 512 + 512) - (dc * TOK + tt * 512)),
                                        AL(dc * TOK + tt * 512, (dc * TOK + tt * 512 + 512) - (dc * TOK + tt * 512)),
                                        start=(dc == 0), stop=(dc == 15))
                            i.then_inc(sT, 1)
                        TE(mmh)
                        tcnt = C.bump(sT, 1)

                        def rel(ae, slot=slot, oc=oc, tcnt=tcnt, w2w=w2w):
                            ae.wait_ge(sT, tcnt)
                            ae.wait_ge(sT, C.val("w2_rd%d" % slot))
                            ae.wait_ge(sB, w2w)
                            ae.activation(out=hbuf[:, slot * 512:slot * 512 + 512],
                                          in_=pb[slot][:, 0:512], func=AF.Relu,
                                          bias=b1sb[:, oc:oc + 1], scale=1.0).then_inc(sA, 1)
                        AE(rel)
                        acnt = C.bump(sA, 1)
                        C.c["h_bank_free%d" % slot] = acnt

                        def mm2(te, slot=slot, oc=oc, tt=tt, acnt=acnt,
                                w2w=w2w):
                            te.wait_ge(sA, acnt)
                            if oc == 0:
                                te.wait_ge(sV, C.val("alpha_bank_free"))
                                te.wait_ge(sV, C.val("ln_banks_free"))
                            te.wait_ge(sB, w2w)
                            i = te.matmul(pb[4 + tt][0:8, 0:512],
                                          w2sb[:, oc * 8:(oc + 1) * 8],
                                          hbuf[:, slot * 512:slot * 512 + 512],
                                          start=(oc == 0),
                                          stop=(oc == 15),
                                          skip_group_check=True)
                            i.then_inc(sT, 1)
                        TE(mm2)
                        tcnt2 = C.bump(sT, 1)
                        C.c["w2_rd%d" % slot] = tcnt2
                C.c["wsh_rd%d" % (wb // 2048)] = C.val(sT)
            C.c["w2sb_rd"] = C.val(sT)
            C.c["wsh_rd"] = C.val(sT)

            tfin = C.val(sT)
            for tt in range(2):
                def aev(ve, tt=tt, tfin=tfin):
                    ve.wait_ge(sT, tfin)
                    ve.tensor_scalar(hbuf[0:8, tt * 512:tt * 512 + 512],
                                     pb[4 + tt][0:8, 0:512], b2sb[:, :], 0.0,
                                     OP.add, OP.add).then_inc(sV, 1)
                VE(aev)
                vcnt = C.bump(sV, 1)

                def aex(ae, tt=tt, vcnt=vcnt):
                    ae.wait_ge(sV, vcnt)
                    ae.activation(out=hbuf[0:8, tt * 512:tt * 512 + 512],
                                  in_=hbuf[0:8, tt * 512:tt * 512 + 512],
                                  func=AF.Exp).then_inc(sA, 1)
                AE(aex)
                acnt = C.bump(sA, 1)

                vt_free = C.val("vt_ps_free")

                def asum(te, tt=tt, acnt=acnt, vt_free=vt_free):
                    te.wait_ge(sA, acnt)
                    if vt_free:
                        te.wait_ge(sV, vt_free)
                    i = te.matmul(pb[6][0:1, 0:512],
                                  ones[0:8, 0:1], hbuf[0:8, tt * 512:tt * 512 + 512],
                                  start=True, stop=True,
                                  skip_group_check=True)
                    i.then_inc(sT, 1)
                TE(asum)
                tcnt = C.bump(sT, 1)

                lnpre = C.val("sLN")

                def arc(ve, tt=tt, tcnt=tcnt, lnpre=lnpre):
                    ve.wait_ge(sT, tcnt)
                    if lnpre:
                        ve.wait_ge(sLN, lnpre)
                    ve.drain()
                    ve.reciprocal(sc1[0:1, :], pb[6][0:1, 0:512]).then_inc(sV, 1)
                VE(arc)
                vcnt2 = C.bump(sV, 1)
                C.c["vt_ps_free"] = vcnt2


                def aship(sy, tt=tt, vcnt2=vcnt2):
                    sy.wait_ge(sV, vcnt2)
                    sy.dma_start(out=rowd.ap()[3, tt * 512:tt * 512 + 512],
                                 in_=sc1[0:1, :]).then_inc(sLN, 16)
                SY(aship)
                C.bump("sLN", 16)

            wcnt = C.val("sLN")

            vgate2 = C.val(sV)

            def abc(sy, wcnt=wcnt, vgate2=vgate2):
                sy.wait_ge(sLN, wcnt)
                if vgate2:
                    sy.wait_ge(sV, vgate2)
                sy.dma_start(out=rowb[0:8, :],
                             in_=rowd.ap()[3:4, 0:TOK].to_broadcast([8, TOK])).then_inc(sLN, 16)
            SY(abc)
            wcnt2 = C.bump("sLN", 16)

            def anrm(ve, wcnt2=wcnt2):
                ve.wait_ge(sLN, wcnt2)
                ve.tensor_tensor(hbuf[0:8, 0:1024], hbuf[0:8, 0:1024], rowb[0:8, :],
                                 OP.mult).then_inc(sV, 1)
            VE(anrm)
            vcnt3 = C.bump(sV, 1)

            def aout(sy, vcnt3=vcnt3):
                sy.wait_ge(sV, vcnt3)
                sy.dma_start(out=alpd.ap()[:, :], in_=hbuf[0:8, 0:1024]).then_inc(sLN, 16)
            SY(aout)
            alpw = C.bump("sLN", 16)
            C.c["alpha_bank_free"] = vcnt3
            C.c["ln_banks_free"] = vcnt3

            for ph in range(2):
                vx_wb = [split_weights(v2w, 2 * ph, 16),
                         split_weights(v2w, 2 * ph + 1, 16)]
                for ci in range(2):
                    c = 2 * ph + ci
                    vbpre = C.val("alpb_rd")

                    def abr(sy, c=c, alpw=alpw, vbpre=vbpre):
                        sy.wait_ge(sLN, alpw)
                        if vbpre:
                            sy.wait_ge(sV, vbpre)
                        sy.dma_start(out=rowb[0:64, :],
                                     in_=alpd.ap()[2 * c:2 * c + 1, :].to_broadcast([64, TOK])).then_inc(sLN, 16)
                        sy.dma_start(out=rowb[64:128, :],
                                     in_=alpd.ap()[2 * c + 1:2 * c + 2, :].to_broadcast([64, TOK])).then_inc(sLN, 16)
                    SY(abr)
                    abw = C.bump("sLN", 32)

                    for tt in range(2):
                        bank = pb[ci]
                        bfree = C.val("vx_bank_free%d" % ci)

                        def mmvx(te, ci=ci, tt=tt, wb=vx_wb[ci],
                                 bank=bank, bfree=bfree,
                                 wsp=[C.val("wsplit_dc%d" % d) for d in range(16)],
                                 lnv=[C.val("lnsp_v%d_%d" % (d, tt)) for d in range(16)],
                                 lng=[C.val("lnsp_g%d_%d" % (d, tt)) for d in range(16)]):
                            if bfree:
                                te.wait_ge(sV, bfree)
                            for dc in range(16):
                                te.wait_ge(sV, max(wsp[dc][0], lnv[dc]))
                                te.wait_ge(sGP, max(wsp[dc][1], lng[dc]))
                                w0 = wb + dc * 128
                                i = mm3(te, bank[:, 0:512],
                                        wsh[:, w0:w0 + 128], wsl[:, w0:w0 + 128],
                                        AH(dc * TOK + tt * 512, 512),
                                        AL(dc * TOK + tt * 512, 512),
                                        start=(dc == 0), stop=(dc == 15))
                            i.then_inc(sT, 1)
                        TE(mmvx)
                        tcnt = C.bump(sT, 1)

                        def evx(ve, ci=ci, tt=tt, bank=bank, tcnt=tcnt, abw=abw,
                                gpre=C.val("sGP")):
                            ve.wait_ge(sT, tcnt)
                            ve.wait_ge(sLN, abw)
                            if gpre:
                                ve.wait_ge(sGP, gpre)
                            ve.drain()
                            ve.tensor_tensor(sc4[:], bank[:, 0:512],
                                             rowb[:, tt * 512:tt * 512 + 512],
                                             OP.mult).then_inc(sV, 1)
                        VE(evx)
                        vcnt = C.bump(sV, 1)
                        C.c["vx_bank_free%d" % ci] = vcnt

                        def vhi(ve, ci=ci, tt=tt, vcnt=vcnt):
                            ve.wait_ge(sV, vcnt)
                            ve.tensor_copy(
                                vxh[:, ci * TOK + tt * 512:ci * TOK + tt * 512 + 512],
                                sc4[:]).then_inc(sV, 1)
                        VE(vhi)
                        vcnt2 = C.bump(sV, 1)

                        def vlo(gp, ci=ci, tt=tt, vcnt2=vcnt2):
                            gp.wait_ge(sV, vcnt2)
                            gp.tensor_tensor(
                                vxl[:, ci * TOK + tt * 512:ci * TOK + tt * 512 + 512],
                                sc4[:],
                                vxh[:, ci * TOK + tt * 512:ci * TOK + tt * 512 + 512],
                                OP.subtract).then_inc(sGP, 1)
                        GE(vlo)
                        C.bump("sGP", 1)
                    C.c["alpb_rd"] = C.val(sV)
                    C.c["wsh_rd%d" % (vx_wb[ci] // 2048)] = C.val(sT)
                vx_done = (C.val(sV), C.val("sGP"))

                for oc in range(16):
                    u_wb = split_weights(u2w, oc, 4)
                    wsplit = C.val("wsplit_dc3")
                    for tt in range(2):
                        bfree = C.val("oproj_bank_free")

                        def mmu(te, oc=oc, tt=tt, ph=ph, wsplit=wsplit,
                                vx_done=vx_done, bfree=bfree, u_wb=u_wb):
                            te.wait_ge(sV, wsplit[0])
                            te.wait_ge(sGP, wsplit[1])
                            te.wait_ge(sV, vx_done[0])
                            te.wait_ge(sGP, vx_done[1])
                            if bfree:
                                te.wait_ge(sV, bfree)
                            for ci in range(2):
                                i = mm3(te, pb[7][:, 0:512],
                                        wsh[:, u_wb + (2 * ph + ci) * 128:u_wb + (2 * ph + ci) * 128 + 128],
                                        wsl[:, u_wb + (2 * ph + ci) * 128:u_wb + (2 * ph + ci) * 128 + 128],
                                        vxh[:, ci * TOK + tt * 512:ci * TOK + tt * 512 + 512],
                                        vxl[:, ci * TOK + tt * 512:ci * TOK + tt * 512 + 512],
                                        start=(ci == 0), stop=(ci == 1))
                            i.then_inc(sT, 1)
                        TE(mmu)
                        tcnt = C.bump(sT, 1)

                        def uadd(ve, oc=oc, tt=tt, tcnt=tcnt):
                            ve.wait_ge(sT, tcnt)
                            ve.tensor_tensor(xr[:, oc * TOK + tt * 512:oc * TOK + tt * 512 + 512],
                                             xr[:, oc * TOK + tt * 512:oc * TOK + tt * 512 + 512],
                                             pb[7][:, 0:512], OP.add).then_inc(sV, 1)
                        VE(uadd)
                        C.c["oproj_bank_free"] = C.bump(sV, 1)
                        if ph == 1 and tt == 1:
                            C.c["res_dc%d" % oc] = C.val(sV)
                    C.c["wsh_rd%d" % (u_wb // 2048)] = C.val(sT)
            C.c["ln_gate"] = C.val(sV)
            C.c["proj_banks_free"] = C.val(sV)

        for dc in range(16):
            def ld(sy, dc=dc):
                sy.dma_start(out=xr[:, dc * TOK:(dc + 1) * TOK],
                             in_=x0[dc, :, :]).then_inc(sW, 16)
            SY(ld)
            C.bump(sW, 16)

        def ldid(sy):
            sy.dma_start(out=idsb[:], in_=ident[:, :]).then_inc(sW, 16)
        SY(ldid)
        C.bump(sW, 16)
        wcnt0 = C.val(sW)

        def mkones(ve, wcnt0=wcnt0):
            ve.wait_ge(sW, wcnt0)
            ve.memset(ones[:], 1.0).then_inc(sV, 1)
        VE(mkones)
        C.c["ln_gate"] = C.bump(sV, 1)

        if n_sub >= 1:
            attn_a()
        if n_sub >= 2:
            ffn()
        if n_sub >= 3:
            attn_b()
        if n_sub >= 4:
            ffn()

        vfin, afin, tfin = C.val(sV), C.val(sA), C.val(sT)

        def store(sy, vfin=vfin, afin=afin, tfin=tfin):
            sy.wait_ge(sV, vfin)
            sy.wait_ge(sA, afin)
            sy.wait_ge(sT, tfin)
            for dc in range(16):
                sy.dma_start(out=y[dc, :, :],
                             in_=xr[:, dc * TOK:(dc + 1) * TOK]).then_inc(sOUT, 16)
        SY(store)
        wfin = C.bump("sOUT", 16 * 16)

        def fin(gp, wfin=wfin):
            gp.wait_ge(sOUT, wfin)
        GE(fin)

    return nc


def kernel(**inputs):
    cst = _host_consts()
    xin = np.asarray(inputs["L0_attn_a_x"], dtype=np.float32)

    def core_inputs(core):
        b, p = core // 2, core % 2
        d = {}
        xo = np.ascontiguousarray(xin[b, p * TOK:(p + 1) * TOK, :].T)
        d["x0"] = np.ascontiguousarray(xo.reshape(16, 128, TOK))
        for k in ("cosq2", "sinq2", "cosk2", "sink2", "ident"):
            d[k] = cst[k]
        d["mask_a"] = cst["mask_a"]
        d["mask_c"] = cst["mask_c"]
        hs = slice(p * 1024, (p + 1) * 1024)
        rs = slice(p * 512, (p + 1) * 512)
        d["wq_a"] = _chunked(np.asarray(inputs["L0_attn_a_w_q"])[:, hs])
        d["wdkv"] = _chunked(np.asarray(inputs["L0_attn_a_w_dkv"]))
        d["wuk"] = _chunked(np.asarray(inputs["L0_attn_a_w_uk"])[:, hs])
        d["wuv"] = _chunked(np.asarray(inputs["L0_attn_a_w_uv"])[:, hs])
        d["wqp"] = _chunked(np.asarray(inputs["L0_attn_a_w_q_pos"])[:, rs])
        d["wkp"] = _chunked(np.asarray(inputs["L0_attn_a_w_k_pos"])[:, rs])
        d["wo_a"] = _chunked(np.asarray(inputs["L0_attn_a_w_o"])[hs, :])
        d["wq_b"] = _chunked(np.ascontiguousarray(np.asarray(inputs["L0_attn_b_w_q"])[hs, :].T))
        d["wk_b"] = _chunked(np.ascontiguousarray(np.asarray(inputs["L0_attn_b_w_k"])[hs, :].T))
        d["wv_b"] = _chunked(np.ascontiguousarray(np.asarray(inputs["L0_attn_b_w_v"])[hs, :].T))
        d["wo_b"] = _chunked(np.ascontiguousarray(np.asarray(inputs["L0_attn_b_w_o"])[:, hs].T))
        d["w1"] = _chunked(np.asarray(inputs["L0_ffn_W1"]))
        d["b1"] = np.asarray(inputs["L0_ffn_b1"], dtype=np.float32).reshape(16, 128, 1)
        d["w2"] = np.asarray(inputs["L0_ffn_W2"], dtype=np.float32).reshape(16, 128, 8)
        d["b2"] = np.asarray(inputs["L0_ffn_b2"], dtype=np.float32).reshape(8, 1)
        V = np.asarray(inputs["L0_ffn_V"])
        U = np.asarray(inputs["L0_ffn_U"])
        V2 = np.ascontiguousarray(np.transpose(V, (2, 0, 1)).reshape(2048, 512))
        U2 = np.ascontiguousarray(np.transpose(U, (0, 2, 1)).reshape(512, 2048))
        d["v2w"] = _chunked(V2)
        d["u2w"] = _chunked(U2)
        return {k: (np.ascontiguousarray(v) if k in ("mask_a", "mask_c")
                    else np.ascontiguousarray(v, dtype=np.float32))
                for k, v in d.items()}

    nc = build_program(n_sub=N_SUB, dbg=DBG)
    in_maps = [core_inputs(c) for c in range(8)]
    res = run_bass_kernel_spmd(nc, in_maps, core_ids=list(range(8)))
    out = np.empty((B, T, D), dtype=np.float32)
    for c in range(8):
        b, p = c // 2, c % 2
        yc = np.asarray(res.results[c]["y"]).reshape(2048, TOK)
        out[b, p * TOK:(p + 1) * TOK, :] = yc.T
    return out


# version 15
# speedup vs baseline: 1.2237x; 1.0105x over previous
import numpy as np
import concourse.bass as bass
import concourse.mybir as mybir
from concourse.bass_utils import run_bass_kernel_spmd

F32 = mybir.dt.float32
BF16 = mybir.dt.bfloat16
AF = mybir.ActivationFunctionType
OP = mybir.AluOpType

B, T, D = 4, 2048, 2048
H, DH, DR, DL = 16, 128, 64, 512
NHH = H // 2
TOK = T // 2
SCALE = float(1.0 / np.sqrt(DH))
NEG = -1.0e30

N_SUB = 4
DBG = {"ln": True, "ckv": True, "gather": True, "pairs": 4, "heads": 2,
       "rope": True, "attn": True, "oproj": True, "attn_parts": 5}


def _host_consts():
    BASE, MAX_SEQ, ORIG_MAX = 10000.0, 131072, 4096
    BETA_FAST, BETA_SLOW = 32.0, 1.0
    SCALE_ROPE = 1.0 / np.sqrt(DR)
    base_inv = 1.0 / BASE ** (np.arange(0, DR, 2, dtype=np.float64) / DR)
    scale = MAX_SEQ / ORIG_MAX
    freqs = np.arange(DR // 2, dtype=np.float64)
    ramp = np.clip((freqs - BETA_SLOW) / (BETA_FAST - BETA_SLOW), 0.0, 1.0)
    inv = base_inv * (1 - ramp) + base_inv / scale * ramp
    yarn = 0.1 * np.log(MAX_SEQ / ORIG_MAX) + 1.0
    t = np.arange(T, dtype=np.float64)
    fr = t[None, :] * inv[:, None]
    emb = np.concatenate([fr, fr], axis=0)
    cos, sin = np.cos(emb), np.sin(emb)
    cosq = (cos * yarn * SCALE_ROPE).astype(np.float32)
    sinq = (sin * yarn * SCALE_ROPE).astype(np.float32)
    cosk = (cos * yarn).astype(np.float32)
    sink = (sin * yarn).astype(np.float32)
    def fold(s):
        f = s.copy()
        f[0:32] = -f[0:32]
        return f
    cosq2 = np.concatenate([cosq, cosq], axis=0)
    sinq2 = np.concatenate([fold(sinq), fold(sinq)], axis=0)
    cosk2 = np.concatenate([cosk, cosk], axis=0)
    sink2 = np.concatenate([fold(sink), fold(sink)], axis=0)

    kk = np.arange(T)[:, None]
    qq = np.arange(T)[None, :]
    dist = qq - kk
    um = (((dist >= -256) & (dist <= 256)) | (kk % 64 == 0) | (kk < 128)) & (kk <= qq)
    import ml_dtypes
    mask_a = np.where(um, np.float32(0.0), np.float32(NEG)).astype(ml_dtypes.bfloat16)
    kk5 = np.arange(512)[:, None]
    qq5 = np.arange(512)[None, :]
    mask_c = np.where(kk5 <= qq5, np.float32(0.0), np.float32(NEG)).astype(ml_dtypes.bfloat16)
    ident = np.eye(128, dtype=np.float32)
    return dict(cosq2=cosq2, sinq2=sinq2, cosk2=cosk2, sink2=sink2,
                mask_a=mask_a, mask_c=mask_c, ident=ident)


def _chunked(w):
    Din, Dout = w.shape
    n_dc, n_oc = Din // 128, Dout // 128
    return np.ascontiguousarray(
        w.reshape(n_dc, 128, n_oc, 128).transpose(2, 1, 0, 3))


class Ctr:
    def __init__(self):
        self.c = {}

    def bump(self, key, n=1):
        self.c[key] = self.c.get(key, 0) + n
        return self.c[key]

    def val(self, key):
        return self.c.get(key, 0)


def build_program(n_sub=4, dbg=None):
    dbg = dbg or DBG
    nc = bass.Bass("TRN2", target_bir_lowering=False)

    def inp(name, shape):
        return nc.dram_tensor(name, list(shape), F32, kind="ExternalInput").ap()

    x0 = inp("x0", (16, 128, TOK))
    cosq2 = inp("cosq2", (128, T))
    sinq2 = inp("sinq2", (128, T))
    cosk2 = inp("cosk2", (128, T))
    sink2 = inp("sink2", (128, T))
    mask_a = nc.dram_tensor("mask_a", [T, T], BF16, kind="ExternalInput").ap()
    mask_c = nc.dram_tensor("mask_c", [512, 512], BF16, kind="ExternalInput").ap()
    ident = inp("ident", (128, 128))
    wq_a = inp("wq_a", (NHH, 128, 16, 128))
    wdkv = inp("wdkv", (4, 128, 16, 128))
    wuk = inp("wuk", (NHH, 128, 4, 128))
    wuv = inp("wuv", (NHH, 128, 4, 128))
    wqp = inp("wqp", (4, 128, 16, 128))
    wkp = inp("wkp", (4, 128, 16, 128))
    wo_a = inp("wo_a", (16, 128, 8, 128))
    wq_b = inp("wq_b", (NHH, 128, 16, 128))
    wk_b = inp("wk_b", (NHH, 128, 16, 128))
    wv_b = inp("wv_b", (NHH, 128, 16, 128))
    wo_b = inp("wo_b", (16, 128, 8, 128))
    w1 = inp("w1", (16, 128, 16, 128))
    b1 = inp("b1", (16, 128, 1))
    w2 = inp("w2", (16, 128, 8))
    b2 = inp("b2", (8, 1))
    v2w = inp("v2w", (4, 128, 16, 128))
    u2w = inp("u2w", (16, 128, 4, 128))

    y = nc.dram_tensor("y", [16, 128, TOK], F32, kind="ExternalOutput").ap()

    g1_in = [nc.dram_tensor("g1_in%d" % q, [1024, TOK], BF16)
             for q in range(4)]
    g1_out = [nc.dram_tensor("g1_out%d" % q, [2, 1024, TOK], BF16)
              for q in range(4)]
    g1c_in = nc.dram_tensor("g1c_in", [1024, TOK], BF16)
    g1c_out = nc.dram_tensor("g1c_out", [2, 1024, TOK], BF16)
    g2_in = [nc.dram_tensor("g2_in%d" % q, [1024, TOK], BF16)
             for q in range(4)]
    g2_out = [nc.dram_tensor("g2_out%d" % q, [2, 1024, TOK], BF16)
              for q in range(4)]
    rs_ins = [nc.dram_tensor(f"rs_in{i}", [2, 512, TOK], F32) for i in range(4)]
    rs_outs = [nc.dram_tensor(f"rs_out{i}", [512, TOK], F32) for i in range(4)]
    opc = nc.dram_tensor("opc", [2, NHH * 128, T], BF16)
    rowd = nc.dram_tensor("rowd", [8, T], F32)
    alpd = nc.dram_tensor("alpd", [8, TOK], F32)

    gidx = {0: (g1_in, g1_out), 1: (g2_in, g2_out), 2: (g1c_in, g1c_out)}
    C = Ctr()

    from contextlib import ExitStack
    with ExitStack() as ex:
        def sbuf(name, shape, dt=F32):
            return ex.enter_context(nc.sbuf_tensor(name, list(shape), dt))

        xr = sbuf("xr", (128, 16 * TOK))
        arena = sbuf("arena", (128, 16 * TOK))
        def AH(c0, n):
            return arena[:, c0 // 2:(c0 + n) // 2].bitcast(BF16)

        def AL(c0, n):
            return arena[:, 8192 + c0 // 2:8192 + (c0 + n) // 2].bitcast(BF16)

        pq0 = arena[:, 0 * T:1 * T]
        pq1 = arena[:, 1 * T:2 * T]
        pk0 = arena[:, 2 * T:3 * T]
        pk1 = arena[:, 3 * T:4 * T]
        pv0 = arena[:, 4 * T:5 * T]
        pv1 = arena[:, 5 * T:6 * T]
        pqr = arena[:, 6 * T:7 * T]
        pkr = arena[:, 7 * T:8 * T]
        wsh = sbuf("wsh", (128, 2 * 16 * 128), BF16)
        wsl = sbuf("wsl", (128, 2 * 16 * 128), BF16)
        wf32 = sbuf("wf32", (128, 4 * 512), F32)
        xs_hl = sbuf("xs_hl", (128, 3 * 1024), BF16)
        expb = sbuf("expb", (128, 3 * 512), F32)
        mskb = sbuf("mskb", (128, 2 * 512), BF16)
        tabA = sbuf("tabA", (128, 512))
        tabB = sbuf("tabB", (128, 512))
        rotb = sbuf("rotb", (128, 512))
        sc1 = sbuf("sc1", (128, 512))
        sc2 = sbuf("sc2", (128, 512))
        sc3 = sbuf("sc3", (128, 512))
        sc4 = sbuf("sc4", (128, 512))
        eacc = None
        rsb = sbuf("rsb", (128, 1024))
        obh = sbuf("obh", (128, 512), BF16)
        obl = sbuf("obl", (128, 512), BF16)
        rowb = sbuf("rowb", (128, 1024))
        ones = sbuf("ones", (128, 8))
        idsb = sbuf("idsb", (128, 128))
        stat = sbuf("stat", (128, TOK))
        dmy = sbuf("dmy", (1, 8))
        hbuf = sbuf("hbuf", (128, 2 * 512))
        w2sb = sbuf("w2sb", (128, 16 * 8))
        b1sb = sbuf("b1sb", (128, 16))
        b2sb = sbuf("b2sb", (8, 1))
        vxh = sbuf("vxh", (128, 2 * TOK), BF16)
        eacc = vxh[:, 0:1024].bitcast(F32)
        vxl = sbuf("vxl", (128, 2 * TOK), BF16)

        pb = [ex.enter_context(nc.psum_tensor(f"pb{i}", [128, 512], F32))
              for i in range(8)]

        sW = ex.enter_context(nc.semaphore("sW"))
        sG = ex.enter_context(nc.semaphore("sG"))
        sT = ex.enter_context(nc.semaphore("sT"))
        sV = ex.enter_context(nc.semaphore("sV"))
        sA = ex.enter_context(nc.semaphore("sA"))
        sCC = ex.enter_context(nc.semaphore("sCC"))
        sF = [ex.enter_context(nc.semaphore("sF0")),
              ex.enter_context(nc.semaphore("sF1")),
              ex.enter_context(nc.semaphore("sF2")),
              ex.enter_context(nc.semaphore("sF3"))]
        sX = [ex.enter_context(nc.semaphore("sX0")),
              ex.enter_context(nc.semaphore("sX1")),
              ex.enter_context(nc.semaphore("sX2"))]
        sGQ = [ex.enter_context(nc.semaphore("sGQ0")),
               ex.enter_context(nc.semaphore("sGQ1")),
               ex.enter_context(nc.semaphore("sGQ2")),
               ex.enter_context(nc.semaphore("sGQ3"))]
        sM = [ex.enter_context(nc.semaphore("sM0")),
              ex.enter_context(nc.semaphore("sM1")),
              ex.enter_context(nc.semaphore("sM2"))]
        sLN = ex.enter_context(nc.semaphore("sLN"))
        sR = ex.enter_context(nc.semaphore("sR"))
        sRS = ex.enter_context(nc.semaphore("sRS"))
        sRES = ex.enter_context(nc.semaphore("sRES"))
        sOUT = ex.enter_context(nc.semaphore("sOUT"))
        sB = ex.enter_context(nc.semaphore("sB"))
        sGP = ex.enter_context(nc.semaphore("sGP"))
        sOP = ex.enter_context(nc.semaphore("sOP"))

        block = ex.enter_context(nc.Block())
        TE, VE, AE, GE, SY = (block.tensor, block.vector, block.scalar,
                              block.gpsimd, block.sync)

        REG = {"cur": 0, "wbuf": 0}

        def split_weights(wdram, oc, ndc):
            REG["cur"] ^= 1
            wbase = REG["cur"] * 2048
            te_pre = C.val("wsh_rd%d" % REG["cur"])
            for dc0 in range(0, ndc, 4):
                g = min(4, ndc - dc0)
                REG["wbuf"] = (REG["wbuf"] + 1) % 4
                buf = REG["wbuf"]
                vfree = C.val("wf_freeV%d" % buf)
                gfree = C.val("wf_freeG%d" % buf)
                first = (dc0 == 0)

                def ld(sy, wdram=wdram, oc=oc, dc0=dc0, g=g, buf=buf,
                       vfree=vfree, gfree=gfree, first=first, te_pre=te_pre):
                    if first and te_pre:
                        sy.wait_ge(sT, te_pre)
                    if vfree:
                        sy.wait_ge(sV, vfree)
                    if gfree:
                        sy.wait_ge(sGP, gfree)
                    src_ap = wdram[oc, :, dc0:dc0 + g, :]
                    dst_ap = wf32[:, buf * 512:buf * 512 + g * 128].rearrange(
                        "p (j c) -> p j c", j=g)
                    sy.dma_start(out=dst_ap, in_=src_ap).then_inc(sF[buf], 16)
                SY(ld)
                wcnt = C.bump("sF%d" % buf, 16)
                dst = wbase + dc0 * 128

                def hi(ve, buf=buf, dst=dst, wcnt=wcnt, g=g):
                    ve.wait_ge(sF[buf], wcnt)
                    ve.tensor_copy(wsh[:, dst:dst + g * 128],
                                   wf32[:, buf * 512:buf * 512 + g * 128]
                                   ).then_inc(sV, 1)
                VE(hi)
                vcnt = C.bump(sV, 1)
                C.c["wf_freeV%d" % buf] = vcnt

                def lo(gp, buf=buf, dst=dst, vcnt=vcnt, g=g):
                    gp.wait_ge(sV, vcnt)
                    gp.tensor_tensor(
                        wsl[:, dst:dst + g * 128],
                        wf32[:, buf * 512:buf * 512 + g * 128],
                        wsh[:, dst:dst + g * 128],
                        OP.subtract).then_inc(sGP, 1)
                GE(lo)
                gcnt = C.bump("sGP", 1)
                C.c["wf_freeG%d" % buf] = gcnt
                for dc in range(dc0, dc0 + g):
                    C.c["wsplit_dc%d" % dc] = (C.val(sV), C.val("sGP"))
            return wbase

        def mm3(te, psum_ap, wh, wl, rh, rl, start, stop):
            te.matmul(psum_ap, wh, rh, start=start, stop=False,
                      skip_group_check=True)
            te.matmul(psum_ap, wh, rl, start=False, stop=False,
                      skip_group_check=True)
            return te.matmul(psum_ap, wl, rh, start=False, stop=stop,
                             skip_group_check=True)

        def split_to(src_ap, gate_v, ship_sem):
            sem, spre = ship_sem

            def hi(ve, gate_v=gate_v, spre=spre, sem=sem):
                ve.wait_ge(sV, gate_v)
                if spre:
                    ve.wait_ge(sem, spre)
                ve.drain()
                ve.tensor_copy(obh[:], src_ap)
                ve.drain()
                ve.scalar_tensor_tensor(obl[:], src_ap, 1.0, obh[:],
                                        OP.mult, OP.subtract).then_inc(sV, 1)
            VE(hi)
            return C.bump(sV, 1)

        def ln_phase(gdst):
            for blk in range(2):
                t0 = blk * 512
                for dc in range(16):
                    sqb = (sc1, sc2)[dc % 2]
                    tpre = C.val("lnsq_free%d" % (dc % 2))
                    gate_dc = C.val("res_dc%d" % dc) or C.val("ln_gate")

                    def sq(ve, dc=dc, t0=t0, sqb=sqb, tpre=tpre,
                           gate=gate_dc):
                        if tpre:
                            ve.wait_ge(sT, tpre)
                        if gate:
                            ve.wait_ge(sV, gate)
                        ve.drain()
                        ve.tensor_tensor(sqb[:],
                                         xr[:, dc * TOK + t0:dc * TOK + t0 + 512],
                                         xr[:, dc * TOK + t0:dc * TOK + t0 + 512],
                                         OP.mult).then_inc(sV, 1)
                    VE(sq)
                    acnt = C.bump(sV, 1)

                    def mmst(te, dc=dc, t0=t0, sqb=sqb, acnt=acnt,
                             gate=gate_dc):
                        if gate:
                            te.wait_ge(sV, gate)
                        if dc == 0:
                            te.wait_ge(sV, C.val("ln_banks_free"))
                        te.matmul(pb[4][0:1, 0:512], ones[:, 0:1],
                                  xr[:, dc * TOK + t0:dc * TOK + t0 + 512],
                                  start=(dc == 0), stop=False, skip_group_check=True)
                        te.wait_ge(sV, acnt)
                        i = te.matmul(pb[5][0:1, 0:512], ones[:, 0:1], sqb[:],
                                      start=(dc == 0), stop=(dc == 15),
                                      skip_group_check=True)
                        i.then_inc(sT, 1)
                    TE(mmst)
                    tcnt = C.bump(sT, 1)
                    C.c["lnsq_free%d" % (dc % 2)] = tcnt
                tcnt = C.val(sT)
                wpre0 = C.val("sLN")

                def st(ve, blk=blk, tcnt=tcnt, wpre0=wpre0):
                    ve.wait_ge(sT, tcnt)
                    if wpre0:
                        ve.wait_ge(sLN, wpre0)
                    ve.drain()
                    ve.tensor_scalar(stat[0:1, 0:512], pb[4][0:1, 0:512],
                                     1.0 / 2048.0, 0.0, OP.mult, OP.add)
                    ve.tensor_scalar(stat[0:1, 512:1024], pb[5][0:1, 0:512],
                                     1.0 / 2048.0, 1e-5, OP.mult, OP.add).then_inc(sV, 1)
                VE(st)
                vcnt = C.bump(sV, 1)
                C.c["ln_banks_free"] = vcnt

                def shipm(sy, blk=blk, vcnt=vcnt):
                    sy.wait_ge(sV, vcnt)
                    sy.dma_start(out=rowd.ap()[0, blk * 512:blk * 512 + 512],
                                 in_=stat[0:1, 0:512]).then_inc(sLN, 16)
                SY(shipm)
                wcnt = C.bump("sLN", 16)

                def st2(ve, blk=blk, wcnt=wcnt):
                    ve.wait_ge(sLN, wcnt)
                    ve.tensor_tensor(stat[0:1, 0:512], stat[0:1, 0:512],
                                     stat[0:1, 0:512], OP.mult)
                    ve.drain()
                    ve.tensor_tensor(stat[0:1, 512:1024], stat[0:1, 512:1024],
                                     stat[0:1, 0:512], OP.subtract).then_inc(sV, 1)
                VE(st2)
                vcnt2 = C.bump(sV, 1)

                def rst(ae, blk=blk, vcnt2=vcnt2):
                    ae.wait_ge(sV, vcnt2)
                    ae.activation(out=stat[0:1, 512:1024], in_=stat[0:1, 512:1024],
                                  func=AF.Sqrt).then_inc(sA, 1)
                AE(rst)
                acnt = C.bump(sA, 1)

                def rcp(ve, blk=blk, acnt=acnt):
                    ve.wait_ge(sA, acnt)
                    ve.reciprocal(stat[0:1, 512:1024],
                                  stat[0:1, 512:1024]).then_inc(sV, 1)
                VE(rcp)
                vcnt3 = C.bump(sV, 1)

                def shipr(sy, blk=blk, vcnt3=vcnt3):
                    sy.wait_ge(sV, vcnt3)
                    sy.dma_start(out=rowd.ap()[1, blk * 512:blk * 512 + 512],
                                 in_=stat[0:1, 512:1024]).then_inc(sLN, 16)
                SY(shipr)
                C.bump("sLN", 16)

            wcnt0 = C.val("sLN")

            vgate = C.val(sV)

            def bcast(sy, wcnt0=wcnt0, vgate=vgate):
                sy.wait_ge(sLN, wcnt0)
                if vgate:
                    sy.wait_ge(sV, vgate)
                sy.dma_start(out=rowb[:, 0:512],
                             in_=rowd.ap()[0:1, 0:512].to_broadcast([128, 512])).then_inc(sLN, 16)
                sy.dma_start(out=rowb[:, 512:1024],
                             in_=rowd.ap()[1:2, 0:512].to_broadcast([128, 512])).then_inc(sLN, 16)
            SY(bcast)
            wcnt = C.bump("sLN", 32)

            for half in range(2):
                if half == 1:
                    vpre_h = C.val(sV)

                    def bcast2(sy, vpre_h=vpre_h):
                        sy.wait_ge(sV, vpre_h)
                        sy.dma_start(out=rowb[:, 0:512],
                                     in_=rowd.ap()[0:1, 512:1024].to_broadcast([128, 512])).then_inc(sLN, 16)
                        sy.dma_start(out=rowb[:, 512:1024],
                                     in_=rowd.ap()[1:2, 512:1024].to_broadcast([128, 512])).then_inc(sLN, 16)
                    SY(bcast2)
                    wcnt = C.bump("sLN", 32)
                for dc in range(16):
                    def app(ve, dc=dc, wcnt=wcnt, half=half,
                            gpre=C.val("sGP")):
                        if dc == 0:
                            ve.wait_ge(sLN, wcnt)
                        if gpre:
                            ve.wait_ge(sGP, gpre)
                        ve.drain()
                        c0 = dc * TOK + half * 512
                        ve.tensor_tensor(sc1[:], xr[:, c0:c0 + 512],
                                         rowb[:, 0:512], OP.subtract)
                        ve.drain()
                        ve.tensor_tensor(sc1[:], sc1[:],
                                         rowb[:, 512:1024], OP.mult).then_inc(sV, 1)
                    VE(app)
                    vcnt = C.bump(sV, 1)

                    def hi(ve, dc=dc, half=half, vcnt=vcnt):
                        ve.wait_ge(sV, vcnt)
                        ve.tensor_copy(AH(dc * TOK + half * 512, 512),
                                       sc1[:]).then_inc(sV, 1)
                    VE(hi)
                    vcnt2 = C.bump(sV, 1)

                    def lo(gp, dc=dc, half=half, vcnt2=vcnt2):
                        gp.wait_ge(sV, vcnt2)
                        gp.tensor_tensor(AL(dc * TOK + half * 512, 512),
                                         sc1[:],
                                         AH(dc * TOK + half * 512, 512),
                                         OP.subtract).then_inc(sGP, 1)
                    GE(lo)
                    C.bump("sGP", 1)
                    C.c["lnsp_v%d_%d" % (dc, half)] = vcnt2
                    C.c["lnsp_g%d_%d" % (dc, half)] = C.val("sGP")
            vsplit = C.val(sV)
            C.c["ln_split_doneG"] = C.val("sGP")
            if gdst is not None:
                gbuf, _ = gidx[gdst]
                for dc in range(16):
                    vdc = C.val("lnsp_v%d_1" % dc)
                    gdc = C.val("lnsp_g%d_1" % dc)

                    def shp(gp, dc=dc, vdc=vdc, gdc=gdc, gbuf=gbuf):
                        gp.wait_ge(sV, vdc)
                        gp.wait_ge(sGP, gdc)
                        b = gbuf[dc // 4].ap()
                        r = (dc % 4) * 256
                        gp.dma_start(out=b[r:r + 128, :],
                                     in_=AH(dc * TOK, TOK)).then_inc(sGQ[dc // 4], 16)
                        gp.dma_start(out=b[r + 128:r + 256, :],
                                     in_=AL(dc * TOK, TOK)).then_inc(sGQ[dc // 4], 16)
                    GE(shp)
                    C.bump("sGQ%d" % (dc // 4), 32)
            C.c["ln_split_done"] = C.val(sV)
            C.c["ln_split_doneG"] = C.val("sGP")

        def gather(gdst):
            gi, go = gidx[gdst]
            if gdst == 2:
                gcnt = C.val(sG)

                def coll(gp, gcnt=gcnt, gi=gi, go=go):
                    gp.wait_ge(sG, gcnt)
                    gp.collective_compute(
                        "AllGather", OP.bypass,
                        replica_groups=[[0, 1], [2, 3], [4, 5], [6, 7]],
                        ins=[gi.ap()],
                        outs=[go.ap()],
                    ).then_inc(sCC, 1)
                GE(coll)
                return C.bump(sCC, 1)
            ccs = []
            for q in range(4):
                gq = C.val("sGQ%d" % q)

                def coll(gp, q=q, gq=gq, gi=gi[q], go=go[q]):
                    gp.wait_ge(sGQ[q], gq)
                    gp.collective_compute(
                        "AllGather", OP.bypass,
                        replica_groups=[[0, 1], [2, 3], [4, 5], [6, 7]],
                        ins=[gi.ap()],
                        outs=[go.ap()],
                    ).then_inc(sCC, 1)
                GE(coll)
                ccs.append(C.bump(sCC, 1))
            return ccs

        def stream_proj(gdst, ndc, row0, outs, cc_need):
            _, gout = gidx[gdst]
            wsp = [C.val("wsplit_dc%d" % d) for d in range(ndc)]
            for tt in range(4):
                rank, col0 = tt // 2, (tt % 2) * 512
                banks = [pb[(tt % 2) * 2 + k] for k in range(len(outs))]
                bfree = [C.val("proj_bank_free%d" % ((tt % 2) * 2 + k))
                         for k in range(len(outs))]
                for dc in range(ndc):
                    buf = dc % 3
                    tfree = C.val("xs_free%d" % buf)

                    if isinstance(cc_need, list):
                        ccd = cc_need[dc // 4]
                        gsel, r0 = gout[dc // 4], row0 + (dc % 4) * 256
                    else:
                        ccd = cc_need
                        gsel, r0 = gout, row0 + dc * 256

                    def ldx(sy, rank=rank, col0=col0, dc=dc, buf=buf, tfree=tfree,
                            gsel=gsel, r0=r0, ccd=ccd):
                        if tfree:
                            sy.wait_ge(sT, tfree)
                        sy.wait_ge(sCC, ccd)
                        src_ap = gsel.ap()[rank, r0:r0 + 256,
                                           col0:col0 + 512].rearrange(
                            "(h p) c -> p h c", p=128)
                        dst_ap = xs_hl[:, buf * 1024:buf * 1024 + 1024].rearrange(
                            "p (h c) -> p h c", h=2)
                        sy.dma_start(out=dst_ap, in_=src_ap).then_inc(sX[buf], 16)
                    SY(ldx)
                    wcnt = C.bump("sX%d" % buf, 16)

                    def mmp(te, dc=dc, buf=buf, wcnt=wcnt, ndc=ndc, banks=banks,
                            wsp=wsp, bfree=bfree, outs=outs):
                        te.wait_ge(sX[buf], wcnt)
                        te.wait_ge(sV, wsp[dc][0])
                        te.wait_ge(sGP, wsp[dc][1])
                        for k, (dsto, mode, wbase) in enumerate(outs):
                            if dc == 0 and bfree[k]:
                                te.wait_ge(sV, bfree[k])
                            w0 = wbase + dc * 128
                            i = mm3(te, banks[k][:, 0:512],
                                    wsh[:, w0:w0 + 128], wsl[:, w0:w0 + 128],
                                    xs_hl[:, buf * 1024:buf * 1024 + 512],
                                    xs_hl[:, buf * 1024 + 512:buf * 1024 + 1024],
                                    start=(dc == 0), stop=(dc == ndc - 1))
                        i.then_inc(sT, 1)
                    TE(mmp)
                    tcnt = C.bump(sT, 1)
                    C.c["xs_free%d" % buf] = tcnt
                tfin = C.val(sT)

                for k, (dsto, mode, wbase) in enumerate(outs):
                    bidx = (tt % 2) * 2 + k
                    if mode == "vt":
                        def ev0(ve, tfin=tfin, bank=banks[k]):
                            ve.wait_ge(sT, tfin)
                            ve.drain()
                            ve.tensor_copy(sc4[:], bank[:, 0:512]).then_inc(sV, 1)
                        VE(ev0)
                        vcnt = C.bump(sV, 1)
                        C.c["proj_bank_free%d" % bidx] = vcnt

                        def tr(te, vcnt=vcnt):
                            te.wait_ge(sV, vcnt)
                            te.wait_ge(sV, C.val("vt_ps_free"))
                            for j in range(4):
                                i = te.matmul(pb[6][:, j * 128:(j + 1) * 128],
                                              sc4[:, j * 128:(j + 1) * 128], idsb[:],
                                              is_transpose=True, start=(j == 0),
                                              stop=(j == 3), skip_group_check=True)
                            i.then_inc(sT, 1)
                        TE(tr)
                        tcnt2 = C.bump(sT, 1)

                        def ev1(ve, tt=tt, dsto=dsto, tcnt2=tcnt2):
                            ve.wait_ge(sT, tcnt2)
                            for j in range(4):
                                kt = tt * 4 + j
                                i = ve.tensor_copy(
                                    arena[:, dsto * T + kt * 128:dsto * T + (kt + 1) * 128],
                                    pb[6][:, j * 128:(j + 1) * 128])
                            i.then_inc(sV, 1)
                        VE(ev1)
                        C.c["vt_ps_free"] = C.bump(sV, 1)
                    else:
                        scl = SCALE if mode == "qs" else 1.0

                        def ev(ve, tt=tt, dsto=dsto, scl=scl, tfin=tfin,
                               bank=banks[k]):
                            ve.wait_ge(sT, tfin)
                            ve.drain()
                            ve.tensor_scalar(
                                arena[:, dsto * T + tt * 512:dsto * T + tt * 512 + 512],
                                bank[:, 0:512], scl, 0.0, OP.mult,
                                OP.add).then_inc(sV, 1)
                        VE(ev)
                        C.c["proj_bank_free%d" % bidx] = C.bump(sV, 1)
            C.c["proj_banks_free"] = C.val(sV)
            for dsto, mode, wbase in outs:
                C.c["wsh_rd%d" % (wbase // 2048)] = C.val(sT)

        def rope_pair():
            apre = C.val("proj_banks_free")
            for cb in range(4):
                for (tabc, tabs, dof) in ((cosq2, sinq2, 6), (cosk2, sink2, 7)):
                    vpre2 = C.val("rope_tab_rd")
                    c0 = dof * T + cb * 512

                    def ldt(sy, tabc=tabc, tabs=tabs, cb=cb, vpre2=vpre2, c0=c0,
                            apre=apre):
                        if vpre2:
                            sy.wait_ge(sV, vpre2)
                        if apre:
                            sy.wait_ge(sV, apre)
                        sy.dma_start(out=tabA[:],
                                     in_=tabc[:, cb * 512:(cb + 1) * 512]).then_inc(sR, 16)
                        sy.dma_start(out=tabB[:],
                                     in_=tabs[:, cb * 512:(cb + 1) * 512]).then_inc(sR, 16)
                        sy.dma_start(out=rotb[0:32, :],
                                     in_=arena[32:64, c0:c0 + 512]).then_inc(sR, 16)
                        sy.dma_start(out=rotb[32:64, :],
                                     in_=arena[0:32, c0:c0 + 512]).then_inc(sR, 16)
                        sy.dma_start(out=rotb[64:96, :],
                                     in_=arena[96:128, c0:c0 + 512]).then_inc(sR, 16)
                        sy.dma_start(out=rotb[96:128, :],
                                     in_=arena[64:96, c0:c0 + 512]).then_inc(sR, 16)
                    SY(ldt)
                    wcnt = C.bump("sR", 96)

                    def rp(ve, cb=cb, wcnt=wcnt, c0=c0):
                        ve.wait_ge(sR, wcnt)
                        ve.tensor_tensor(arena[:, c0:c0 + 512], arena[:, c0:c0 + 512],
                                         tabA[:], OP.mult)
                        ve.tensor_tensor(rotb[:], rotb[:], tabB[:], OP.mult)
                        ve.drain()
                        ve.tensor_tensor(arena[:, c0:c0 + 512], arena[:, c0:c0 + 512],
                                         rotb[:], OP.add).then_inc(sV, 1)
                    VE(rp)
                    C.bump(sV, 1)
                    C.c["rope_tab_rd"] = C.val(sV)
                    C.c["rope_%s%d" % ("q" if dof == 6 else "k", cb)] = C.val(sV)

        def attention_head(sub, h, hh):
            qo = hh
            ko = 2 + hh
            vo = 4 + hh
            rr0 = hh * 64
            gate_pb = C.val("proj_banks_free")
            gate_rp = C.val("rope_tab_rd")
            for J in range(4):
                ntiles = 4 * J + 4
                for i in range(ntiles):
                    sc_bank = pb[i % 2]
                    diag = (sub == 1) and (i >= 4 * J)
                    use_mask = ((sub == 0) or diag) and dbg["attn_parts"] >= 2
                    mw = None
                    mb = i % 2
                    if use_mask:
                        vfree = C.val("msk_free%d" % mb)

                        def ldm(sy, i=i, J=J, mb=mb, vfree=vfree, sub=sub):
                            if vfree:
                                sy.wait_ge(sV, vfree)
                            if sub == 0:
                                src = mask_a[i * 128:(i + 1) * 128, J * 512:J * 512 + 512]
                            else:
                                ii = i - 4 * J
                                src = mask_c[ii * 128:(ii + 1) * 128, 0:512]
                            sy.dma_start(out=mskb[:, mb * 512:mb * 512 + 512],
                                         in_=src).then_inc(sM[mb], 16)
                        AE(ldm)
                        mw = C.bump("sM%d" % mb, 16)

                    apre = C.val("exp_done_t%d" % (i % 2))

                    rgate = 0
                    if sub == 0:
                        rgate = max(C.val("rope_q%d" % J),
                                    C.val("rope_k%d" % (i // 4)))

                    def mmsc(te, i=i, J=J, sc_bank=sc_bank, qo=qo, ko=ko,
                             rr0=rr0, apre=apre, sub=sub, gate_pb=gate_pb,
                             rgate=rgate):
                        if apre:
                            te.wait_ge(sA, apre)
                        if rgate:
                            te.wait_ge(sV, rgate)
                        if i == 0 and J == 0:
                            if gate_pb:
                                te.wait_ge(sV, gate_pb)
                        ins = te.matmul(sc_bank[:, 0:512],
                                        arena[:, ko * T + i * 128:ko * T + (i + 1) * 128],
                                        arena[:, qo * T + J * 512:qo * T + J * 512 + 512],
                                        start=True,
                                        stop=(sub == 1), skip_group_check=True)
                        if sub == 0:
                            ins = te.matmul(sc_bank[:, 0:512],
                                            arena[rr0:rr0 + 64, 7 * T + i * 128:7 * T + (i + 1) * 128],
                                            arena[rr0:rr0 + 64, 6 * T + J * 512:6 * T + J * 512 + 512],
                                            start=False, stop=True, skip_group_check=True)
                        ins.then_inc(sT, 1)
                    TE(mmsc)
                    tcnt = C.bump(sT, 1)

                    if use_mask:
                        def mska(ve, sc_bank=sc_bank, mb=mb, tcnt=tcnt, mw=mw):
                            ve.wait_ge(sT, tcnt)
                            ve.wait_ge(sM[mb], mw)
                            ve.tensor_tensor(sc_bank[:, 0:512], sc_bank[:, 0:512],
                                             mskb[:, mb * 512:mb * 512 + 512],
                                             OP.add).then_inc(sV, 1)
                        VE(mska)
                        vcnt = C.bump(sV, 1)
                        C.c["msk_free%d" % mb] = vcnt
                        dep = ("V", vcnt)
                    else:
                        dep = ("T", tcnt)

                    if dbg["attn_parts"] < 3:
                        C.c["exp_done_t%d" % (i % 2)] = dep[1] if dep[0] == "V" else 0
                        if dep[0] != "V":
                            def drain(ve, sc_bank=sc_bank, tcnt=tcnt):
                                ve.wait_ge(sT, tcnt)
                                ve.tensor_copy(sc4[:], sc_bank[:, 0:512]).then_inc(sV, 1)
                            VE(drain)
                            C.c["exp_done_t%d" % (i % 2)] = 0
                            C.c["drain_v"] = C.bump(sV, 1)
                        continue
                    eb = i % 3
                    epre = C.val("exp_rd_%d" % eb)
                    epreV = C.val("exp_rdv_%d" % eb)

                    def expf(ae, sc_bank=sc_bank, eb=eb, dep=dep, epre=epre,
                             epreV=epreV):
                        if dep[0] == "V":
                            ae.wait_ge(sV, dep[1])
                        else:
                            ae.wait_ge(sT, dep[1])
                        if epre:
                            ae.wait_ge(sT, epre)
                        if epreV:
                            ae.wait_ge(sV, epreV)
                        ae.activation(out=expb[:, eb * 512:eb * 512 + 512],
                                      in_=sc_bank[:, 0:512], func=AF.Exp).then_inc(sA, 1)
                    AE(expf)
                    acnt = C.bump(sA, 1)
                    C.c["exp_done_t%d" % (i % 2)] = acnt
                    C.c["exp_rd"] = acnt
                    if dbg["attn_parts"] < 4:
                        continue

                    def mmav(te, i=i, eb=eb, vo=vo, acnt=acnt, J=J,
                             avfree=C.val("avrs_free%d" % (J % 4))):
                        te.wait_ge(sA, acnt)
                        if i == 0 and avfree:
                            te.wait_ge(sV, avfree)
                        av_b = pb[2 + J]
                        te.matmul(av_b[:, 0:512],
                                  arena[:, vo * T + i * 128:vo * T + (i + 1) * 128],
                                  expb[:, eb * 512:eb * 512 + 512],
                                  start=(i == 0), stop=False,
                                  skip_group_check=True).then_inc(sT, 1)
                    TE(mmav)
                    tcnt2 = C.bump(sT, 1)
                    C.c["exp_rd_%d" % eb] = tcnt2

                    aprev = C.val("acc_prev")
                    efree = C.val("eacc_free") if i == 0 else 0

                    def accf(ve, i=i, eb=eb, acnt=acnt, aprev=aprev,
                             efree=efree):
                        ve.wait_ge(sA, acnt)
                        if aprev:
                            ve.wait_ge(sV, aprev)
                        if i == 0:
                            if efree:
                                ve.wait_ge(sT, efree)
                            ve.tensor_copy(eacc[:],
                                           expb[:, eb * 512:eb * 512 + 512]).then_inc(sV, 1)
                        else:
                            ve.tensor_tensor(eacc[:], eacc[:],
                                             expb[:, eb * 512:eb * 512 + 512],
                                             OP.add).then_inc(sV, 1)
                    VE(accf)
                    vacc = C.bump(sV, 1)
                    C.c["acc_prev"] = vacc
                    C.c["exp_rdv_%d" % eb] = vacc

                if dbg["attn_parts"] < 5:
                    continue
                vlast = C.val("acc_prev")
                rfree = C.val("rsrow_free")

                def mmrs(te, vlast=vlast, rfree=rfree):
                    te.wait_ge(sV, vlast)
                    if rfree:
                        te.wait_ge(sV, rfree)
                    te.matmul(pb[7][0:1, 0:512], ones[:, 0:1], eacc[:],
                              start=True, stop=True,
                              skip_group_check=True).then_inc(sT, 1)
                TE(mmrs)
                trs = C.bump(sT, 1)
                C.c["eacc_free"] = trs
                wship = C.val("sRS")

                def rsum(ve, trs=trs, wship=wship):
                    if wship:
                        ve.wait_ge(sRS, wship)
                    ve.wait_ge(sT, trs)
                    ve.drain()
                    ve.reciprocal(sc1[0:1, :], pb[7][0:1, 0:512]).then_inc(sV, 1)
                VE(rsum)
                vcnt = C.bump(sV, 1)
                C.c["rsrow_free"] = vcnt

                def shiprs(sy, vcnt=vcnt):
                    sy.wait_ge(sV, vcnt)
                    sy.dma_start(out=rowd.ap()[2, 0:512], in_=sc1[0:1, :]).then_inc(sRS, 16)
                AE(shiprs)
                wcnt = C.bump("sRS", 16)
                gpre = C.val(sG)

                def bcrs(sy, wcnt=wcnt, gpre=gpre):
                    sy.wait_ge(sRS, wcnt)
                    if gpre:
                        sy.wait_ge(sG, gpre)
                    sy.dma_start(out=sc3[:],
                                 in_=rowd.ap()[2:3, 0:512].to_broadcast([128, 512])).then_inc(sRS, 16)
                AE(bcrs)
                wcnt2 = C.bump("sRS", 16)

                def norm(ve, wcnt2=wcnt2, gpre=C.val("sGP"), J=J):
                    ve.wait_ge(sRS, wcnt2)
                    if gpre:
                        ve.wait_ge(sGP, gpre)
                    ve.drain()
                    av_b = pb[2 + J]
                    ve.tensor_tensor(sc4[:], av_b[:, 0:512], sc3[:],
                                     OP.mult).then_inc(sV, 1)
                VE(norm)
                vcnt2 = C.bump(sV, 1)
                C.c["avrs_free%d" % (J % 4)] = vcnt2
                C.c["avrs_free"] = vcnt2
                C.c["rs_ship"] = wcnt2

                ocnt = split_to(sc4[:], vcnt2, (sOP, C.val("sOP")))

                def oshp(sy, ocnt=ocnt, h=h, J=J):
                    sy.wait_ge(sV, ocnt)
                    sy.dma_start(out=opc.ap()[0, h * 128:(h + 1) * 128,
                                              J * 512:J * 512 + 512],
                                 in_=obh[:]).then_inc(sOP, 16)
                    sy.dma_start(out=opc.ap()[1, h * 128:(h + 1) * 128,
                                              J * 512:J * 512 + 512],
                                 in_=obl[:]).then_inc(sOP, 16)
                AE(oshp)
                C.bump("sOP", 32)

        def oproj_rs(sub):
            wo = wo_a if sub == 0 else wo_b
            gpre = C.val("sOP")
            for oc in range(16):
                wb = split_weights(wo, oc, 8)
                for tt in range(4):
                    for dc in range(8):
                        buf = dc % 3
                        tfree = C.val("xs_free%d" % buf)
                        wsplit = C.val("wsplit_dc%d" % dc)

                        def ldo(sy, dc=dc, buf=buf, tfree=tfree, tt=tt, gpre=gpre):
                            if tfree:
                                sy.wait_ge(sT, tfree)
                            sy.wait_ge(sOP, gpre)
                            src_ap = opc.ap()[:, dc * 128:(dc + 1) * 128,
                                              tt * 512:tt * 512 + 512].rearrange(
                                "h p c -> p h c")
                            dst_ap = xs_hl[:, buf * 1024:buf * 1024 + 1024].rearrange(
                                "p (h c) -> p h c", h=2)
                            sy.dma_start(out=dst_ap, in_=src_ap).then_inc(sX[buf], 16)
                        SY(ldo)
                        wcnt = C.bump("sX%d" % buf, 16)

                        def mmo(te, dc=dc, buf=buf, wcnt=wcnt, wsplit=wsplit,
                                wb=wb):
                            te.wait_ge(sX[buf], wcnt)
                            te.wait_ge(sV, wsplit[0])
                            te.wait_ge(sGP, wsplit[1])
                            if dc == 0:
                                te.wait_ge(sV, C.val("oproj_bank_free"))
                            i = mm3(te, pb[7][:, 0:512],
                                    wsh[:, wb + dc * 128:wb + dc * 128 + 128],
                                    wsl[:, wb + dc * 128:wb + dc * 128 + 128],
                                    xs_hl[:, buf * 1024:buf * 1024 + 512],
                                    xs_hl[:, buf * 1024 + 512:buf * 1024 + 1024],
                                    start=(dc == 0), stop=(dc == 7))
                            i.then_inc(sT, 1)
                        TE(mmo)
                        tcnt = C.bump(sT, 1)
                        C.c["xs_free%d" % buf] = tcnt

                    gfree = C.val("oev_shipped")

                    def evo(ve, tcnt=tcnt, gfree=gfree):
                        ve.wait_ge(sT, tcnt)
                        if gfree:
                            ve.wait_ge(sG, gfree)
                        ve.drain()
                        ve.tensor_copy(sc1[:], pb[7][:, 0:512]).then_inc(sV, 1)
                    VE(evo)
                    vcnt = C.bump(sV, 1)
                    C.c["oproj_bank_free"] = vcnt

                    def sho(gp, vcnt=vcnt, oc=oc, tt=tt):
                        gp.wait_ge(sV, vcnt)
                        gp.dma_start(out=rs_ins[oc // 4].ap()[tt // 2, (oc % 4) * 128:(oc % 4 + 1) * 128,
                                                              (tt % 2) * 512:(tt % 2) * 512 + 512],
                                     in_=sc1[:]).then_inc(sG, 16)
                    GE(sho)
                    C.c["oev_shipped"] = C.bump(sG, 16)
                C.c["wsh_rd%d" % (wb // 2048)] = C.val(sT)
                if oc % 4 == 3:
                    gq = C.val(sG)

                    def collq(gp, gq=gq, q=oc // 4):
                        gp.wait_ge(sG, gq)
                        gp.collective_compute(
                            "ReduceScatter", OP.add,
                            replica_groups=[[0, 1], [2, 3], [4, 5], [6, 7]],
                            ins=[rs_ins[q].ap()],
                            outs=[rs_outs[q].ap()],
                        ).then_inc(sCC, 1)
                    GE(collq)
                    C.c["rs_cc%d" % (oc // 4)] = C.bump(sCC, 1)

            for dc in range(16):
                vfree = C.val("res_free")
                cc2 = C.val("rs_cc%d" % (dc // 4))

                def ldr(sy, dc=dc, cc2=cc2, vfree=vfree):
                    if vfree:
                        sy.wait_ge(sV, vfree)
                    sy.wait_ge(sCC, cc2)
                    sy.dma_start(out=rsb[:, 0:512],
                                 in_=rs_outs[dc // 4].ap()[(dc % 4) * 128:(dc % 4 + 1) * 128,
                                                           0:512]).then_inc(sRES, 16)
                    sy.dma_start(out=rsb[:, 512:1024],
                                 in_=rs_outs[dc // 4].ap()[(dc % 4) * 128:(dc % 4 + 1) * 128,
                                                           512:1024]).then_inc(sRES, 16)
                SY(ldr)
                wcnt = C.bump("sRES", 32)

                def addr_(ve, dc=dc, wcnt=wcnt):
                    ve.wait_ge(sRES, wcnt)
                    ve.drain()
                    ve.tensor_tensor(xr[:, dc * TOK:dc * TOK + 512],
                                     xr[:, dc * TOK:dc * TOK + 512], rsb[:, 0:512], OP.add)
                    ve.tensor_tensor(xr[:, dc * TOK + 512:dc * TOK + 1024],
                                     xr[:, dc * TOK + 512:dc * TOK + 1024], rsb[:, 512:1024],
                                     OP.add).then_inc(sV, 1)
                VE(addr_)
                C.c["res_free"] = C.bump(sV, 1)
                C.c["res_dc%d" % dc] = C.val(sV)
            C.c["ln_gate"] = C.val(sV)

        def attn_a():
            if not dbg["ln"]:
                return
            ln_phase(gdst=0)
            cc = gather(0)
            for oc in range(4):
                wb = split_weights(wdkv, oc, 16)
                for tt in range(2):
                    bank = pb[2 + (oc % 2)]
                    bfree = C.val("ckv_bank_free%d" % (oc % 2))
                    lnv = [C.val("lnsp_v%d_%d" % (d, tt)) for d in range(16)]
                    lng = [C.val("lnsp_g%d_%d" % (d, tt)) for d in range(16)]
                    wsp = [C.val("wsplit_dc%d" % d) for d in range(16)]

                    def mmckv(te, tt=tt, wb=wb, bank=bank, bfree=bfree,
                              lnv=lnv, lng=lng, wsp=wsp):
                        if bfree:
                            te.wait_ge(sV, bfree)
                        for dc in range(16):
                            te.wait_ge(sV, max(wsp[dc][0], lnv[dc]))
                            te.wait_ge(sGP, max(wsp[dc][1], lng[dc]))
                            w0 = wb + dc * 128
                            i = mm3(te, bank[:, 0:512],
                                    wsh[:, w0:w0 + 128], wsl[:, w0:w0 + 128],
                                    AH(dc * TOK + tt * 512, 512),
                                    AL(dc * TOK + tt * 512, 512),
                                    start=(dc == 0), stop=(dc == 15))
                        i.then_inc(sT, 1)
                    TE(mmckv)
                    tcnt = C.bump(sT, 1)

                    def evc(ve, bank=bank, tcnt=tcnt, gpre=C.val("sGP")):
                        ve.wait_ge(sT, tcnt)
                        if gpre:
                            ve.wait_ge(sGP, gpre)
                        ve.drain()
                        ve.tensor_copy(sc4[:], bank[:, 0:512]).then_inc(sV, 1)
                    VE(evc)
                    vcnt = C.bump(sV, 1)
                    C.c["ckv_bank_free%d" % (oc % 2)] = vcnt

                    ocnt = split_to(sc4[:], vcnt, (sG, C.val(sG)))

                    def cshp(gp, ocnt=ocnt, oc=oc, tt=tt):
                        gp.wait_ge(sV, ocnt)
                        gp.dma_start(out=g1c_in.ap()[oc * 256:oc * 256 + 128,
                                                     tt * 512:tt * 512 + 512],
                                     in_=obh[:]).then_inc(sG, 16)
                        gp.dma_start(out=g1c_in.ap()[oc * 256 + 128:oc * 256 + 256,
                                                     tt * 512:tt * 512 + 512],
                                     in_=obl[:]).then_inc(sG, 16)
                    GE(cshp)
                    C.bump(sG, 32)
                C.c["wsh_rd%d" % (wb // 2048)] = C.val(sT)

            wq_next = [split_weights(wq_a, 0, 16), split_weights(wq_a, 1, 16)]
            ccc = gather(2)
            np_ = min(NHH // 2, dbg["pairs"])
            for hp in range(np_):
                stream_proj(gdst=0, ndc=16, row0=0,
                            outs=[(0, "qs", wq_next[0]), (1, "qs", wq_next[1])],
                            cc_need=cc)
                wb1 = split_weights(wqp, hp, 16)
                wb2 = split_weights(wkp, hp, 16)
                stream_proj(gdst=0, ndc=16, row0=0,
                            outs=[(6, "n", wb1), (7, "n", wb2)], cc_need=cc)
                wb1 = split_weights(wuk, 2 * hp, 4)
                wb2 = split_weights(wuv, 2 * hp, 4)
                stream_proj(gdst=2, ndc=4, row0=0,
                            outs=[(2, "n", wb1), (4, "vt", wb2)], cc_need=ccc)
                wb1 = split_weights(wuk, 2 * hp + 1, 4)
                wb2 = split_weights(wuv, 2 * hp + 1, 4)
                stream_proj(gdst=2, ndc=4, row0=0,
                            outs=[(3, "n", wb1), (5, "vt", wb2)], cc_need=ccc)
                if dbg["rope"]:
                    rope_pair()
                if hp + 1 < np_:
                    wq_next = [split_weights(wq_a, 2 * hp + 2, 16),
                               split_weights(wq_a, 2 * hp + 3, 16)]
                if dbg["attn"]:
                    for hh in range(dbg["heads"]):
                        attention_head(0, 2 * hp + hh, hh)
            C.c["ln_banks_free"] = C.val(sV)
            if dbg["oproj"]:
                oproj_rs(0)

        def attn_b():
            ln_phase(gdst=1)
            wq_next = [split_weights(wq_b, 0, 16), split_weights(wq_b, 1, 16)]
            cc = gather(1)
            for hp in range(NHH // 2):
                stream_proj(gdst=1, ndc=16, row0=0,
                            outs=[(0, "qs", wq_next[0]), (1, "qs", wq_next[1])],
                            cc_need=cc)
                wb1 = split_weights(wk_b, 2 * hp, 16)
                wb2 = split_weights(wk_b, 2 * hp + 1, 16)
                stream_proj(gdst=1, ndc=16, row0=0,
                            outs=[(2, "n", wb1), (3, "n", wb2)], cc_need=cc)
                wb1 = split_weights(wv_b, 2 * hp, 16)
                wb2 = split_weights(wv_b, 2 * hp + 1, 16)
                stream_proj(gdst=1, ndc=16, row0=0,
                            outs=[(4, "vt", wb1), (5, "vt", wb2)], cc_need=cc)
                if hp + 1 < NHH // 2:
                    wq_next = [split_weights(wq_b, 2 * hp + 2, 16),
                               split_weights(wq_b, 2 * hp + 3, 16)]
                attention_head(1, 2 * hp, 0)
                attention_head(1, 2 * hp + 1, 1)
            C.c["ln_banks_free"] = C.val(sV)
            oproj_rs(1)

        def ffn():
            ln_phase(gdst=None)
            vpre = C.val("ln_split_done")

            def ldw2(sy):
                sy.wait_ge(sT, C.val("w2sb_rd"))
                for oc in range(16):
                    sy.dma_start(out=w2sb[:, oc * 8:(oc + 1) * 8],
                                 in_=w2[oc, :, :]).then_inc(sB, 16)
                    sy.dma_start(out=b1sb[:, oc:oc + 1],
                                 in_=b1[oc, :, :]).then_inc(sB, 16)
                sy.dma_start(out=b2sb[:, :], in_=b2[:, :]).then_inc(sB, 16)
            SY(ldw2)
            w2w = C.bump("sB", 16 * 33)

            for oc in range(16):
                wb = split_weights(w1, oc, 16)
                for tt in range(2):
                    if True:
                        slot = oc % 2
                        ocp = oc // 2
                        hfree = C.val("h_bank_free%d" % slot)

                        def mmh(te, tt=tt, slot=slot, oc=oc, wb=wb,
                                hfree=hfree,
                                wsp=[C.val("wsplit_dc%d" % d) for d in range(16)],
                                lnv=[C.val("lnsp_v%d_%d" % (d, tt)) for d in range(16)],
                                lng=[C.val("lnsp_g%d_%d" % (d, tt)) for d in range(16)]):
                            if hfree:
                                te.wait_ge(sA, hfree)
                            if oc == 0 and tt == 0:
                                te.wait_ge(sV, C.val("ln_banks_free"))
                                te.wait_ge(sV, C.val("proj_banks_free"))
                                te.wait_ge(sA, C.val("exp_rd"))
                                te.wait_ge(sV, C.val("avrs_free"))
                            for dc in range(16):
                                te.wait_ge(sV, max(wsp[dc][0], lnv[dc]))
                                te.wait_ge(sGP, max(wsp[dc][1], lng[dc]))
                                w0 = wb + dc * 128
                                i = mm3(te, pb[slot][:, 0:512],
                                        wsh[:, w0:w0 + 128], wsl[:, w0:w0 + 128],
                                        AH(dc * TOK + tt * 512, (dc * TOK + tt * 512 + 512) - (dc * TOK + tt * 512)),
                                        AL(dc * TOK + tt * 512, (dc * TOK + tt * 512 + 512) - (dc * TOK + tt * 512)),
                                        start=(dc == 0), stop=(dc == 15))
                            i.then_inc(sT, 1)
                        TE(mmh)
                        tcnt = C.bump(sT, 1)

                        def rel(ae, slot=slot, oc=oc, tcnt=tcnt, w2w=w2w):
                            ae.wait_ge(sT, tcnt)
                            ae.wait_ge(sT, C.val("w2_rd%d" % slot))
                            ae.wait_ge(sB, w2w)
                            ae.activation(out=hbuf[:, slot * 512:slot * 512 + 512],
                                          in_=pb[slot][:, 0:512], func=AF.Relu,
                                          bias=b1sb[:, oc:oc + 1], scale=1.0).then_inc(sA, 1)
                        AE(rel)
                        acnt = C.bump(sA, 1)
                        C.c["h_bank_free%d" % slot] = acnt

                        def mm2(te, slot=slot, oc=oc, tt=tt, acnt=acnt,
                                w2w=w2w):
                            te.wait_ge(sA, acnt)
                            if oc == 0:
                                te.wait_ge(sV, C.val("alpha_bank_free"))
                                te.wait_ge(sV, C.val("ln_banks_free"))
                            te.wait_ge(sB, w2w)
                            i = te.matmul(pb[4 + tt][0:8, 0:512],
                                          w2sb[:, oc * 8:(oc + 1) * 8],
                                          hbuf[:, slot * 512:slot * 512 + 512],
                                          start=(oc == 0),
                                          stop=(oc == 15),
                                          skip_group_check=True)
                            i.then_inc(sT, 1)
                        TE(mm2)
                        tcnt2 = C.bump(sT, 1)
                        C.c["w2_rd%d" % slot] = tcnt2
                C.c["wsh_rd%d" % (wb // 2048)] = C.val(sT)
            C.c["w2sb_rd"] = C.val(sT)
            C.c["wsh_rd"] = C.val(sT)

            tfin = C.val(sT)
            for tt in range(2):
                def aev(ve, tt=tt, tfin=tfin):
                    ve.wait_ge(sT, tfin)
                    ve.tensor_scalar(hbuf[0:8, tt * 512:tt * 512 + 512],
                                     pb[4 + tt][0:8, 0:512], b2sb[:, :], 0.0,
                                     OP.add, OP.add).then_inc(sV, 1)
                VE(aev)
                vcnt = C.bump(sV, 1)

                def aex(ae, tt=tt, vcnt=vcnt):
                    ae.wait_ge(sV, vcnt)
                    ae.activation(out=hbuf[0:8, tt * 512:tt * 512 + 512],
                                  in_=hbuf[0:8, tt * 512:tt * 512 + 512],
                                  func=AF.Exp).then_inc(sA, 1)
                AE(aex)
                acnt = C.bump(sA, 1)

                vt_free = C.val("vt_ps_free")

                def asum(te, tt=tt, acnt=acnt, vt_free=vt_free):
                    te.wait_ge(sA, acnt)
                    if vt_free:
                        te.wait_ge(sV, vt_free)
                    i = te.matmul(pb[6][0:1, 0:512],
                                  ones[0:8, 0:1], hbuf[0:8, tt * 512:tt * 512 + 512],
                                  start=True, stop=True,
                                  skip_group_check=True)
                    i.then_inc(sT, 1)
                TE(asum)
                tcnt = C.bump(sT, 1)

                lnpre = C.val("sLN")

                def arc(ve, tt=tt, tcnt=tcnt, lnpre=lnpre):
                    ve.wait_ge(sT, tcnt)
                    if lnpre:
                        ve.wait_ge(sLN, lnpre)
                    ve.drain()
                    ve.reciprocal(sc1[0:1, :], pb[6][0:1, 0:512]).then_inc(sV, 1)
                VE(arc)
                vcnt2 = C.bump(sV, 1)
                C.c["vt_ps_free"] = vcnt2


                def aship(sy, tt=tt, vcnt2=vcnt2):
                    sy.wait_ge(sV, vcnt2)
                    sy.dma_start(out=rowd.ap()[3, tt * 512:tt * 512 + 512],
                                 in_=sc1[0:1, :]).then_inc(sLN, 16)
                SY(aship)
                C.bump("sLN", 16)

            wcnt = C.val("sLN")

            vgate2 = C.val(sV)

            def abc(sy, wcnt=wcnt, vgate2=vgate2):
                sy.wait_ge(sLN, wcnt)
                if vgate2:
                    sy.wait_ge(sV, vgate2)
                sy.dma_start(out=rowb[0:8, :],
                             in_=rowd.ap()[3:4, 0:TOK].to_broadcast([8, TOK])).then_inc(sLN, 16)
            SY(abc)
            wcnt2 = C.bump("sLN", 16)

            def anrm(ve, wcnt2=wcnt2):
                ve.wait_ge(sLN, wcnt2)
                ve.tensor_tensor(hbuf[0:8, 0:1024], hbuf[0:8, 0:1024], rowb[0:8, :],
                                 OP.mult).then_inc(sV, 1)
            VE(anrm)
            vcnt3 = C.bump(sV, 1)

            def aout(sy, vcnt3=vcnt3):
                sy.wait_ge(sV, vcnt3)
                sy.dma_start(out=alpd.ap()[:, :], in_=hbuf[0:8, 0:1024]).then_inc(sLN, 16)
            SY(aout)
            alpw = C.bump("sLN", 16)
            C.c["alpha_bank_free"] = vcnt3
            C.c["ln_banks_free"] = vcnt3

            for ph in range(2):
                vx_wb = [split_weights(v2w, 2 * ph, 16),
                         split_weights(v2w, 2 * ph + 1, 16)]
                for ci in range(2):
                    c = 2 * ph + ci
                    vbpre = C.val("alpb_rd")

                    def abr(sy, c=c, alpw=alpw, vbpre=vbpre):
                        sy.wait_ge(sLN, alpw)
                        if vbpre:
                            sy.wait_ge(sV, vbpre)
                        sy.dma_start(out=rowb[0:64, :],
                                     in_=alpd.ap()[2 * c:2 * c + 1, :].to_broadcast([64, TOK])).then_inc(sLN, 16)
                        sy.dma_start(out=rowb[64:128, :],
                                     in_=alpd.ap()[2 * c + 1:2 * c + 2, :].to_broadcast([64, TOK])).then_inc(sLN, 16)
                    SY(abr)
                    abw = C.bump("sLN", 32)

                    for tt in range(2):
                        bank = pb[ci]
                        bfree = C.val("vx_bank_free%d" % ci)

                        def mmvx(te, ci=ci, tt=tt, wb=vx_wb[ci],
                                 bank=bank, bfree=bfree,
                                 wsp=[C.val("wsplit_dc%d" % d) for d in range(16)],
                                 lnv=[C.val("lnsp_v%d_%d" % (d, tt)) for d in range(16)],
                                 lng=[C.val("lnsp_g%d_%d" % (d, tt)) for d in range(16)]):
                            if bfree:
                                te.wait_ge(sV, bfree)
                            for dc in range(16):
                                te.wait_ge(sV, max(wsp[dc][0], lnv[dc]))
                                te.wait_ge(sGP, max(wsp[dc][1], lng[dc]))
                                w0 = wb + dc * 128
                                i = mm3(te, bank[:, 0:512],
                                        wsh[:, w0:w0 + 128], wsl[:, w0:w0 + 128],
                                        AH(dc * TOK + tt * 512, 512),
                                        AL(dc * TOK + tt * 512, 512),
                                        start=(dc == 0), stop=(dc == 15))
                            i.then_inc(sT, 1)
                        TE(mmvx)
                        tcnt = C.bump(sT, 1)

                        def evx(ve, ci=ci, tt=tt, bank=bank, tcnt=tcnt, abw=abw,
                                gpre=C.val("sGP")):
                            ve.wait_ge(sT, tcnt)
                            ve.wait_ge(sLN, abw)
                            if gpre:
                                ve.wait_ge(sGP, gpre)
                            ve.drain()
                            ve.tensor_tensor(sc4[:], bank[:, 0:512],
                                             rowb[:, tt * 512:tt * 512 + 512],
                                             OP.mult).then_inc(sV, 1)
                        VE(evx)
                        vcnt = C.bump(sV, 1)
                        C.c["vx_bank_free%d" % ci] = vcnt

                        def vhi(ve, ci=ci, tt=tt, vcnt=vcnt):
                            ve.wait_ge(sV, vcnt)
                            ve.tensor_copy(
                                vxh[:, ci * TOK + tt * 512:ci * TOK + tt * 512 + 512],
                                sc4[:]).then_inc(sV, 1)
                        VE(vhi)
                        vcnt2 = C.bump(sV, 1)

                        def vlo(gp, ci=ci, tt=tt, vcnt2=vcnt2):
                            gp.wait_ge(sV, vcnt2)
                            gp.tensor_tensor(
                                vxl[:, ci * TOK + tt * 512:ci * TOK + tt * 512 + 512],
                                sc4[:],
                                vxh[:, ci * TOK + tt * 512:ci * TOK + tt * 512 + 512],
                                OP.subtract).then_inc(sGP, 1)
                        GE(vlo)
                        C.bump("sGP", 1)
                    C.c["alpb_rd"] = C.val(sV)
                    C.c["wsh_rd%d" % (vx_wb[ci] // 2048)] = C.val(sT)
                vx_done = (C.val(sV), C.val("sGP"))

                for oc in range(16):
                    u_wb = split_weights(u2w, oc, 4)
                    wsplit = C.val("wsplit_dc3")
                    for tt in range(2):
                        bfree = C.val("oproj_bank_free")

                        def mmu(te, oc=oc, tt=tt, ph=ph, wsplit=wsplit,
                                vx_done=vx_done, bfree=bfree, u_wb=u_wb):
                            te.wait_ge(sV, wsplit[0])
                            te.wait_ge(sGP, wsplit[1])
                            te.wait_ge(sV, vx_done[0])
                            te.wait_ge(sGP, vx_done[1])
                            if bfree:
                                te.wait_ge(sV, bfree)
                            for ci in range(2):
                                i = mm3(te, pb[7][:, 0:512],
                                        wsh[:, u_wb + (2 * ph + ci) * 128:u_wb + (2 * ph + ci) * 128 + 128],
                                        wsl[:, u_wb + (2 * ph + ci) * 128:u_wb + (2 * ph + ci) * 128 + 128],
                                        vxh[:, ci * TOK + tt * 512:ci * TOK + tt * 512 + 512],
                                        vxl[:, ci * TOK + tt * 512:ci * TOK + tt * 512 + 512],
                                        start=(ci == 0), stop=(ci == 1))
                            i.then_inc(sT, 1)
                        TE(mmu)
                        tcnt = C.bump(sT, 1)

                        def uadd(ve, oc=oc, tt=tt, tcnt=tcnt):
                            ve.wait_ge(sT, tcnt)
                            ve.tensor_tensor(xr[:, oc * TOK + tt * 512:oc * TOK + tt * 512 + 512],
                                             xr[:, oc * TOK + tt * 512:oc * TOK + tt * 512 + 512],
                                             pb[7][:, 0:512], OP.add).then_inc(sV, 1)
                        VE(uadd)
                        C.c["oproj_bank_free"] = C.bump(sV, 1)
                        if ph == 1 and tt == 1:
                            C.c["res_dc%d" % oc] = C.val(sV)
                    C.c["wsh_rd%d" % (u_wb // 2048)] = C.val(sT)
            C.c["ln_gate"] = C.val(sV)
            C.c["proj_banks_free"] = C.val(sV)

        for dc in range(16):
            def ld(sy, dc=dc):
                sy.dma_start(out=xr[:, dc * TOK:(dc + 1) * TOK],
                             in_=x0[dc, :, :]).then_inc(sW, 16)
            SY(ld)
            C.bump(sW, 16)

        def ldid(sy):
            sy.dma_start(out=idsb[:], in_=ident[:, :]).then_inc(sW, 16)
        SY(ldid)
        C.bump(sW, 16)
        wcnt0 = C.val(sW)

        def mkones(ve, wcnt0=wcnt0):
            ve.wait_ge(sW, wcnt0)
            ve.memset(ones[:], 1.0).then_inc(sV, 1)
        VE(mkones)
        C.c["ln_gate"] = C.bump(sV, 1)

        if n_sub >= 1:
            attn_a()
        if n_sub >= 2:
            ffn()
        if n_sub >= 3:
            attn_b()
        if n_sub >= 4:
            ffn()

        vfin, afin, tfin = C.val(sV), C.val(sA), C.val(sT)

        def store(sy, vfin=vfin, afin=afin, tfin=tfin,
                  gates=[C.val("res_dc%d" % dc) for dc in range(16)]):
            for dc in range(16):
                if gates[dc]:
                    sy.wait_ge(sV, gates[dc])
                else:
                    sy.wait_ge(sV, vfin)
                    sy.wait_ge(sA, afin)
                    sy.wait_ge(sT, tfin)
                sy.dma_start(out=y[dc, :, :],
                             in_=xr[:, dc * TOK:(dc + 1) * TOK]).then_inc(sOUT, 16)
        SY(store)
        wfin = C.bump("sOUT", 16 * 16)

        def fin(gp, wfin=wfin):
            gp.wait_ge(sOUT, wfin)
        GE(fin)

    return nc


def kernel(**inputs):
    cst = _host_consts()
    xin = np.asarray(inputs["L0_attn_a_x"], dtype=np.float32)

    def core_inputs(core):
        b, p = core // 2, core % 2
        d = {}
        xo = np.ascontiguousarray(xin[b, p * TOK:(p + 1) * TOK, :].T)
        d["x0"] = np.ascontiguousarray(xo.reshape(16, 128, TOK))
        for k in ("cosq2", "sinq2", "cosk2", "sink2", "ident"):
            d[k] = cst[k]
        d["mask_a"] = cst["mask_a"]
        d["mask_c"] = cst["mask_c"]
        hs = slice(p * 1024, (p + 1) * 1024)
        rs = slice(p * 512, (p + 1) * 512)
        d["wq_a"] = _chunked(np.asarray(inputs["L0_attn_a_w_q"])[:, hs])
        d["wdkv"] = _chunked(np.asarray(inputs["L0_attn_a_w_dkv"]))
        d["wuk"] = _chunked(np.asarray(inputs["L0_attn_a_w_uk"])[:, hs])
        d["wuv"] = _chunked(np.asarray(inputs["L0_attn_a_w_uv"])[:, hs])
        d["wqp"] = _chunked(np.asarray(inputs["L0_attn_a_w_q_pos"])[:, rs])
        d["wkp"] = _chunked(np.asarray(inputs["L0_attn_a_w_k_pos"])[:, rs])
        d["wo_a"] = _chunked(np.asarray(inputs["L0_attn_a_w_o"])[hs, :])
        d["wq_b"] = _chunked(np.ascontiguousarray(np.asarray(inputs["L0_attn_b_w_q"])[hs, :].T))
        d["wk_b"] = _chunked(np.ascontiguousarray(np.asarray(inputs["L0_attn_b_w_k"])[hs, :].T))
        d["wv_b"] = _chunked(np.ascontiguousarray(np.asarray(inputs["L0_attn_b_w_v"])[hs, :].T))
        d["wo_b"] = _chunked(np.ascontiguousarray(np.asarray(inputs["L0_attn_b_w_o"])[:, hs].T))
        d["w1"] = _chunked(np.asarray(inputs["L0_ffn_W1"]))
        d["b1"] = np.asarray(inputs["L0_ffn_b1"], dtype=np.float32).reshape(16, 128, 1)
        d["w2"] = np.asarray(inputs["L0_ffn_W2"], dtype=np.float32).reshape(16, 128, 8)
        d["b2"] = np.asarray(inputs["L0_ffn_b2"], dtype=np.float32).reshape(8, 1)
        V = np.asarray(inputs["L0_ffn_V"])
        U = np.asarray(inputs["L0_ffn_U"])
        V2 = np.ascontiguousarray(np.transpose(V, (2, 0, 1)).reshape(2048, 512))
        U2 = np.ascontiguousarray(np.transpose(U, (0, 2, 1)).reshape(512, 2048))
        d["v2w"] = _chunked(V2)
        d["u2w"] = _chunked(U2)
        return {k: (np.ascontiguousarray(v) if k in ("mask_a", "mask_c")
                    else np.ascontiguousarray(v, dtype=np.float32))
                for k, v in d.items()}

    nc = build_program(n_sub=N_SUB, dbg=DBG)
    in_maps = [core_inputs(c) for c in range(8)]
    res = run_bass_kernel_spmd(nc, in_maps, core_ids=list(range(8)))
    out = np.empty((B, T, D), dtype=np.float32)
    for c in range(8):
        b, p = c // 2, c % 2
        yc = np.asarray(res.results[c]["y"]).reshape(2048, TOK)
        out[b, p * TOK:(p + 1) * TOK, :] = yc.T
    return out


# version 16 (final)
# speedup vs baseline: 1.2460x; 1.0183x over previous
import numpy as np
import concourse.bass as bass
import concourse.mybir as mybir
from concourse.bass_utils import run_bass_kernel_spmd

F32 = mybir.dt.float32
BF16 = mybir.dt.bfloat16
AF = mybir.ActivationFunctionType
OP = mybir.AluOpType

B, T, D = 4, 2048, 2048
H, DH, DR, DL = 16, 128, 64, 512
NHH = H // 2
TOK = T // 2
SCALE = float(1.0 / np.sqrt(DH))
NEG = -1.0e30

N_SUB = 4
DBG = {"ln": True, "ckv": True, "gather": True, "pairs": 4, "heads": 2,
       "rope": True, "attn": True, "oproj": True, "attn_parts": 5}


def _host_consts():
    BASE, MAX_SEQ, ORIG_MAX = 10000.0, 131072, 4096
    BETA_FAST, BETA_SLOW = 32.0, 1.0
    SCALE_ROPE = 1.0 / np.sqrt(DR)
    base_inv = 1.0 / BASE ** (np.arange(0, DR, 2, dtype=np.float64) / DR)
    scale = MAX_SEQ / ORIG_MAX
    freqs = np.arange(DR // 2, dtype=np.float64)
    ramp = np.clip((freqs - BETA_SLOW) / (BETA_FAST - BETA_SLOW), 0.0, 1.0)
    inv = base_inv * (1 - ramp) + base_inv / scale * ramp
    yarn = 0.1 * np.log(MAX_SEQ / ORIG_MAX) + 1.0
    t = np.arange(T, dtype=np.float64)
    fr = t[None, :] * inv[:, None]
    emb = np.concatenate([fr, fr], axis=0)
    cos, sin = np.cos(emb), np.sin(emb)
    cosq = (cos * yarn * SCALE_ROPE).astype(np.float32)
    sinq = (sin * yarn * SCALE_ROPE).astype(np.float32)
    cosk = (cos * yarn).astype(np.float32)
    sink = (sin * yarn).astype(np.float32)
    def fold(s):
        f = s.copy()
        f[0:32] = -f[0:32]
        return f
    cosq2 = np.concatenate([cosq, cosq], axis=0)
    sinq2 = np.concatenate([fold(sinq), fold(sinq)], axis=0)
    cosk2 = np.concatenate([cosk, cosk], axis=0)
    sink2 = np.concatenate([fold(sink), fold(sink)], axis=0)

    kk = np.arange(T)[:, None]
    qq = np.arange(T)[None, :]
    dist = qq - kk
    um = (((dist >= -256) & (dist <= 256)) | (kk % 64 == 0) | (kk < 128)) & (kk <= qq)
    import ml_dtypes
    mask_a = np.where(um, np.float32(0.0), np.float32(NEG)).astype(ml_dtypes.bfloat16)
    kk5 = np.arange(512)[:, None]
    qq5 = np.arange(512)[None, :]
    mask_c = np.where(kk5 <= qq5, np.float32(0.0), np.float32(NEG)).astype(ml_dtypes.bfloat16)
    ident = np.eye(128, dtype=np.float32)
    return dict(cosq2=cosq2, sinq2=sinq2, cosk2=cosk2, sink2=sink2,
                mask_a=mask_a, mask_c=mask_c, ident=ident)


def _chunked(w):
    Din, Dout = w.shape
    n_dc, n_oc = Din // 128, Dout // 128
    return np.ascontiguousarray(
        w.reshape(n_dc, 128, n_oc, 128).transpose(2, 1, 0, 3))


class Ctr:
    def __init__(self):
        self.c = {}

    def bump(self, key, n=1):
        self.c[key] = self.c.get(key, 0) + n
        return self.c[key]

    def val(self, key):
        return self.c.get(key, 0)


def build_program(n_sub=4, dbg=None):
    dbg = dbg or DBG
    nc = bass.Bass("TRN2", target_bir_lowering=False)

    def inp(name, shape):
        return nc.dram_tensor(name, list(shape), F32, kind="ExternalInput").ap()

    x0 = inp("x0", (16, 128, TOK))
    cosq2 = inp("cosq2", (128, T))
    sinq2 = inp("sinq2", (128, T))
    cosk2 = inp("cosk2", (128, T))
    sink2 = inp("sink2", (128, T))
    mask_a = nc.dram_tensor("mask_a", [T, T], BF16, kind="ExternalInput").ap()
    mask_c = nc.dram_tensor("mask_c", [512, 512], BF16, kind="ExternalInput").ap()
    ident = inp("ident", (128, 128))
    wq_a = inp("wq_a", (NHH, 128, 16, 128))
    wdkv = inp("wdkv", (4, 128, 16, 128))
    wuk = inp("wuk", (NHH, 128, 4, 128))
    wuv = inp("wuv", (NHH, 128, 4, 128))
    wqp = inp("wqp", (4, 128, 16, 128))
    wkp = inp("wkp", (4, 128, 16, 128))
    wo_a = inp("wo_a", (16, 128, 8, 128))
    wq_b = inp("wq_b", (NHH, 128, 16, 128))
    wk_b = inp("wk_b", (NHH, 128, 16, 128))
    wv_b = inp("wv_b", (NHH, 128, 16, 128))
    wo_b = inp("wo_b", (16, 128, 8, 128))
    w1 = inp("w1", (16, 128, 16, 128))
    b1 = inp("b1", (16, 128, 1))
    w2 = inp("w2", (16, 128, 8))
    b2 = inp("b2", (8, 1))
    v2w = inp("v2w", (4, 128, 16, 128))
    u2w = inp("u2w", (16, 128, 4, 128))

    y = nc.dram_tensor("y", [16, 128, TOK], F32, kind="ExternalOutput").ap()

    g1_in = [nc.dram_tensor("g1_in%d" % q, [1024, TOK], BF16)
             for q in range(4)]
    g1_out = [nc.dram_tensor("g1_out%d" % q, [2, 1024, TOK], BF16)
              for q in range(4)]
    g1c_in = nc.dram_tensor("g1c_in", [1024, TOK], BF16)
    g1c_out = nc.dram_tensor("g1c_out", [2, 1024, TOK], BF16)
    g2_in = [nc.dram_tensor("g2_in%d" % q, [1024, TOK], BF16)
             for q in range(4)]
    g2_out = [nc.dram_tensor("g2_out%d" % q, [2, 1024, TOK], BF16)
              for q in range(4)]
    rs_ins = [nc.dram_tensor(f"rs_in{i}", [2, 512, TOK], F32) for i in range(4)]
    rs_outs = [nc.dram_tensor(f"rs_out{i}", [512, TOK], F32) for i in range(4)]
    opc = nc.dram_tensor("opc", [2, NHH * 128, T], BF16)
    rowd = nc.dram_tensor("rowd", [8, T], F32)
    alpd = nc.dram_tensor("alpd", [8, TOK], F32)

    gidx = {0: (g1_in, g1_out), 1: (g2_in, g2_out), 2: (g1c_in, g1c_out)}
    C = Ctr()

    from contextlib import ExitStack
    with ExitStack() as ex:
        def sbuf(name, shape, dt=F32):
            return ex.enter_context(nc.sbuf_tensor(name, list(shape), dt))

        xr = sbuf("xr", (128, 16 * TOK))
        arena = sbuf("arena", (128, 16 * TOK))
        def AH(c0, n):
            return arena[:, c0 // 2:(c0 + n) // 2].bitcast(BF16)

        def AL(c0, n):
            return arena[:, 8192 + c0 // 2:8192 + (c0 + n) // 2].bitcast(BF16)

        pq0 = arena[:, 0 * T:1 * T]
        pq1 = arena[:, 1 * T:2 * T]
        pk0 = arena[:, 2 * T:3 * T]
        pk1 = arena[:, 3 * T:4 * T]
        pv0 = arena[:, 4 * T:5 * T]
        pv1 = arena[:, 5 * T:6 * T]
        pqr = arena[:, 6 * T:7 * T]
        pkr = arena[:, 7 * T:8 * T]
        wsh = sbuf("wsh", (128, 2 * 16 * 128), BF16)
        wsl = sbuf("wsl", (128, 2 * 16 * 128), BF16)
        wf32 = sbuf("wf32", (128, 4 * 512), F32)
        xs_hl = sbuf("xs_hl", (128, 3 * 1024), BF16)
        expb = sbuf("expb", (128, 3 * 512), F32)
        mskb = sbuf("mskb", (128, 2 * 512), BF16)
        tabA = sbuf("tabA", (128, 512))
        tabB = sbuf("tabB", (128, 512))
        rotb = sbuf("rotb", (128, 512))
        sc1 = sbuf("sc1", (128, 512))
        sc2 = sbuf("sc2", (128, 512))
        sc3 = sbuf("sc3", (128, 512))
        sc4 = sbuf("sc4", (128, 512))
        eacc = None
        rsb = sbuf("rsb", (128, 1024))
        obh = sbuf("obh", (128, 512), BF16)
        obl = sbuf("obl", (128, 512), BF16)
        rowb = sbuf("rowb", (128, 1024))
        ones = sbuf("ones", (128, 8))
        idsb = sbuf("idsb", (128, 128))
        stat = sbuf("stat", (128, TOK))
        dmy = sbuf("dmy", (1, 8))
        hbuf = sbuf("hbuf", (128, 2 * 512))
        w2sb = sbuf("w2sb", (128, 16 * 8))
        b1sb = sbuf("b1sb", (128, 16))
        b2sb = sbuf("b2sb", (8, 1))
        vxh = sbuf("vxh", (128, 2 * TOK), BF16)
        eacc = vxh[:, 0:1024].bitcast(F32)
        vxl = sbuf("vxl", (128, 2 * TOK), BF16)

        pb = [ex.enter_context(nc.psum_tensor(f"pb{i}", [128, 512], F32))
              for i in range(8)]

        sW = ex.enter_context(nc.semaphore("sW"))
        sG = ex.enter_context(nc.semaphore("sG"))
        sT = ex.enter_context(nc.semaphore("sT"))
        sV = ex.enter_context(nc.semaphore("sV"))
        sA = ex.enter_context(nc.semaphore("sA"))
        sCC = ex.enter_context(nc.semaphore("sCC"))
        sF = [ex.enter_context(nc.semaphore("sF0")),
              ex.enter_context(nc.semaphore("sF1")),
              ex.enter_context(nc.semaphore("sF2")),
              ex.enter_context(nc.semaphore("sF3"))]
        sX = [ex.enter_context(nc.semaphore("sX0")),
              ex.enter_context(nc.semaphore("sX1")),
              ex.enter_context(nc.semaphore("sX2"))]
        sGQ = [ex.enter_context(nc.semaphore("sGQ0")),
               ex.enter_context(nc.semaphore("sGQ1")),
               ex.enter_context(nc.semaphore("sGQ2")),
               ex.enter_context(nc.semaphore("sGQ3"))]
        sM = [ex.enter_context(nc.semaphore("sM0")),
              ex.enter_context(nc.semaphore("sM1")),
              ex.enter_context(nc.semaphore("sM2"))]
        sLN = ex.enter_context(nc.semaphore("sLN"))
        sR = ex.enter_context(nc.semaphore("sR"))
        sRS = ex.enter_context(nc.semaphore("sRS"))
        sRES = ex.enter_context(nc.semaphore("sRES"))
        sOUT = ex.enter_context(nc.semaphore("sOUT"))
        sB = ex.enter_context(nc.semaphore("sB"))
        sGP = ex.enter_context(nc.semaphore("sGP"))
        sOP = ex.enter_context(nc.semaphore("sOP"))

        block = ex.enter_context(nc.Block())
        TE, VE, AE, GE, SY = (block.tensor, block.vector, block.scalar,
                              block.gpsimd, block.sync)

        REG = {"cur": 0, "wbuf": 0}

        def split_weights(wdram, oc, ndc):
            REG["cur"] ^= 1
            wbase = REG["cur"] * 2048
            te_pre = C.val("wsh_rd%d" % REG["cur"])
            for dc0 in range(0, ndc, 4):
                g = min(4, ndc - dc0)
                REG["wbuf"] = (REG["wbuf"] + 1) % 4
                buf = REG["wbuf"]
                vfree = C.val("wf_freeV%d" % buf)
                gfree = C.val("wf_freeG%d" % buf)
                first = (dc0 == 0)

                def ld(sy, wdram=wdram, oc=oc, dc0=dc0, g=g, buf=buf,
                       vfree=vfree, gfree=gfree, first=first, te_pre=te_pre):
                    if first and te_pre:
                        sy.wait_ge(sT, te_pre)
                    if vfree:
                        sy.wait_ge(sV, vfree)
                    if gfree:
                        sy.wait_ge(sGP, gfree)
                    src_ap = wdram[oc, :, dc0:dc0 + g, :]
                    dst_ap = wf32[:, buf * 512:buf * 512 + g * 128].rearrange(
                        "p (j c) -> p j c", j=g)
                    sy.dma_start(out=dst_ap, in_=src_ap).then_inc(sF[buf], 16)
                SY(ld)
                wcnt = C.bump("sF%d" % buf, 16)
                dst = wbase + dc0 * 128

                def hi(ve, buf=buf, dst=dst, wcnt=wcnt, g=g):
                    ve.wait_ge(sF[buf], wcnt)
                    ve.tensor_copy(wsh[:, dst:dst + g * 128],
                                   wf32[:, buf * 512:buf * 512 + g * 128]
                                   ).then_inc(sV, 1)
                VE(hi)
                vcnt = C.bump(sV, 1)
                C.c["wf_freeV%d" % buf] = vcnt

                def lo(gp, buf=buf, dst=dst, vcnt=vcnt, g=g):
                    gp.wait_ge(sV, vcnt)
                    gp.tensor_tensor(
                        wsl[:, dst:dst + g * 128],
                        wf32[:, buf * 512:buf * 512 + g * 128],
                        wsh[:, dst:dst + g * 128],
                        OP.subtract).then_inc(sGP, 1)
                GE(lo)
                gcnt = C.bump("sGP", 1)
                C.c["wf_freeG%d" % buf] = gcnt
                for dc in range(dc0, dc0 + g):
                    C.c["wsplit_dc%d" % dc] = (C.val(sV), C.val("sGP"))
            return wbase

        def mm3(te, psum_ap, wh, wl, rh, rl, start, stop):
            te.matmul(psum_ap, wh, rh, start=start, stop=False,
                      skip_group_check=True)
            te.matmul(psum_ap, wh, rl, start=False, stop=False,
                      skip_group_check=True)
            return te.matmul(psum_ap, wl, rh, start=False, stop=stop,
                             skip_group_check=True)

        def split_to(src_ap, gate_v, ship_sem):
            sem, spre = ship_sem

            def hi(ve, gate_v=gate_v, spre=spre, sem=sem):
                ve.wait_ge(sV, gate_v)
                if spre:
                    ve.wait_ge(sem, spre)
                ve.drain()
                ve.tensor_copy(obh[:], src_ap)
                ve.drain()
                ve.scalar_tensor_tensor(obl[:], src_ap, 1.0, obh[:],
                                        OP.mult, OP.subtract).then_inc(sV, 1)
            VE(hi)
            return C.bump(sV, 1)

        def ln_phase(gdst):
            for blk in range(2):
                t0 = blk * 512
                for dc in range(16):
                    sqb = (sc1, sc2)[dc % 2]
                    tpre = C.val("lnsq_free%d" % (dc % 2))
                    gate_dc = C.val("res_dc%d" % dc) or C.val("ln_gate")

                    def sq(ve, dc=dc, t0=t0, sqb=sqb, tpre=tpre,
                           gate=gate_dc):
                        if tpre:
                            ve.wait_ge(sT, tpre)
                        if gate:
                            ve.wait_ge(sV, gate)
                        ve.drain()
                        ve.tensor_tensor(sqb[:],
                                         xr[:, dc * TOK + t0:dc * TOK + t0 + 512],
                                         xr[:, dc * TOK + t0:dc * TOK + t0 + 512],
                                         OP.mult).then_inc(sV, 1)
                    VE(sq)
                    acnt = C.bump(sV, 1)

                    def mmst(te, dc=dc, t0=t0, sqb=sqb, acnt=acnt,
                             gate=gate_dc):
                        if gate:
                            te.wait_ge(sV, gate)
                        if dc == 0:
                            te.wait_ge(sV, C.val("ln_banks_free"))
                        te.matmul(pb[4][0:1, 0:512], ones[:, 0:1],
                                  xr[:, dc * TOK + t0:dc * TOK + t0 + 512],
                                  start=(dc == 0), stop=False, skip_group_check=True)
                        te.wait_ge(sV, acnt)
                        i = te.matmul(pb[5][0:1, 0:512], ones[:, 0:1], sqb[:],
                                      start=(dc == 0), stop=(dc == 15),
                                      skip_group_check=True)
                        i.then_inc(sT, 1)
                    TE(mmst)
                    tcnt = C.bump(sT, 1)
                    C.c["lnsq_free%d" % (dc % 2)] = tcnt
                tcnt = C.val(sT)
                wpre0 = C.val("sLN")

                def st(ve, blk=blk, tcnt=tcnt, wpre0=wpre0):
                    ve.wait_ge(sT, tcnt)
                    if wpre0:
                        ve.wait_ge(sLN, wpre0)
                    ve.drain()
                    ve.tensor_scalar(stat[0:1, 0:512], pb[4][0:1, 0:512],
                                     1.0 / 2048.0, 0.0, OP.mult, OP.add)
                    ve.tensor_scalar(stat[0:1, 512:1024], pb[5][0:1, 0:512],
                                     1.0 / 2048.0, 1e-5, OP.mult, OP.add).then_inc(sV, 1)
                VE(st)
                vcnt = C.bump(sV, 1)
                C.c["ln_banks_free"] = vcnt

                def shipm(sy, blk=blk, vcnt=vcnt):
                    sy.wait_ge(sV, vcnt)
                    sy.dma_start(out=rowd.ap()[0, blk * 512:blk * 512 + 512],
                                 in_=stat[0:1, 0:512]).then_inc(sLN, 16)
                SY(shipm)
                wcnt = C.bump("sLN", 16)

                def st2(ve, blk=blk, wcnt=wcnt):
                    ve.wait_ge(sLN, wcnt)
                    ve.tensor_tensor(stat[0:1, 0:512], stat[0:1, 0:512],
                                     stat[0:1, 0:512], OP.mult)
                    ve.drain()
                    ve.tensor_tensor(stat[0:1, 512:1024], stat[0:1, 512:1024],
                                     stat[0:1, 0:512], OP.subtract).then_inc(sV, 1)
                VE(st2)
                vcnt2 = C.bump(sV, 1)

                def rst(ae, blk=blk, vcnt2=vcnt2):
                    ae.wait_ge(sV, vcnt2)
                    ae.activation(out=stat[0:1, 512:1024], in_=stat[0:1, 512:1024],
                                  func=AF.Sqrt).then_inc(sA, 1)
                AE(rst)
                acnt = C.bump(sA, 1)

                def rcp(ve, blk=blk, acnt=acnt):
                    ve.wait_ge(sA, acnt)
                    ve.reciprocal(stat[0:1, 512:1024],
                                  stat[0:1, 512:1024]).then_inc(sV, 1)
                VE(rcp)
                vcnt3 = C.bump(sV, 1)

                def shipr(sy, blk=blk, vcnt3=vcnt3):
                    sy.wait_ge(sV, vcnt3)
                    sy.dma_start(out=rowd.ap()[1, blk * 512:blk * 512 + 512],
                                 in_=stat[0:1, 512:1024]).then_inc(sLN, 16)
                SY(shipr)
                C.bump("sLN", 16)

            wcnt0 = C.val("sLN")

            vgate = C.val(sV)

            def bcast(sy, wcnt0=wcnt0, vgate=vgate):
                sy.wait_ge(sLN, wcnt0)
                if vgate:
                    sy.wait_ge(sV, vgate)
                sy.dma_start(out=rowb[:, 0:512],
                             in_=rowd.ap()[0:1, 0:512].to_broadcast([128, 512])).then_inc(sLN, 16)
                sy.dma_start(out=rowb[:, 512:1024],
                             in_=rowd.ap()[1:2, 0:512].to_broadcast([128, 512])).then_inc(sLN, 16)
            SY(bcast)
            wcnt = C.bump("sLN", 32)

            for half in range(2):
                if half == 1:
                    vpre_h = C.val(sV)

                    def bcast2(sy, vpre_h=vpre_h):
                        sy.wait_ge(sV, vpre_h)
                        sy.dma_start(out=rowb[:, 0:512],
                                     in_=rowd.ap()[0:1, 512:1024].to_broadcast([128, 512])).then_inc(sLN, 16)
                        sy.dma_start(out=rowb[:, 512:1024],
                                     in_=rowd.ap()[1:2, 512:1024].to_broadcast([128, 512])).then_inc(sLN, 16)
                    SY(bcast2)
                    wcnt = C.bump("sLN", 32)
                for dc in range(16):
                    def app(ve, dc=dc, wcnt=wcnt, half=half,
                            gpre=C.val("sGP")):
                        if dc == 0:
                            ve.wait_ge(sLN, wcnt)
                        if gpre:
                            ve.wait_ge(sGP, gpre)
                        ve.drain()
                        c0 = dc * TOK + half * 512
                        ve.tensor_tensor(sc1[:], xr[:, c0:c0 + 512],
                                         rowb[:, 0:512], OP.subtract)
                        ve.drain()
                        ve.tensor_tensor(sc1[:], sc1[:],
                                         rowb[:, 512:1024], OP.mult).then_inc(sV, 1)
                    VE(app)
                    vcnt = C.bump(sV, 1)

                    def hi(ve, dc=dc, half=half, vcnt=vcnt):
                        ve.wait_ge(sV, vcnt)
                        ve.tensor_copy(AH(dc * TOK + half * 512, 512),
                                       sc1[:]).then_inc(sV, 1)
                    VE(hi)
                    vcnt2 = C.bump(sV, 1)

                    def lo(gp, dc=dc, half=half, vcnt2=vcnt2):
                        gp.wait_ge(sV, vcnt2)
                        gp.tensor_tensor(AL(dc * TOK + half * 512, 512),
                                         sc1[:],
                                         AH(dc * TOK + half * 512, 512),
                                         OP.subtract).then_inc(sGP, 1)
                    GE(lo)
                    C.bump("sGP", 1)
                    C.c["lnsp_v%d_%d" % (dc, half)] = vcnt2
                    C.c["lnsp_g%d_%d" % (dc, half)] = C.val("sGP")
            vsplit = C.val(sV)
            C.c["ln_split_doneG"] = C.val("sGP")
            if gdst is not None:
                gbuf, _ = gidx[gdst]
                for dc in range(16):
                    vdc = C.val("lnsp_v%d_1" % dc)
                    gdc = C.val("lnsp_g%d_1" % dc)

                    def shp(gp, dc=dc, vdc=vdc, gdc=gdc, gbuf=gbuf):
                        gp.wait_ge(sV, vdc)
                        gp.wait_ge(sGP, gdc)
                        b = gbuf[dc // 4].ap()
                        r = (dc % 4) * 256
                        gp.dma_start(out=b[r:r + 128, :],
                                     in_=AH(dc * TOK, TOK)).then_inc(sGQ[dc // 4], 16)
                        gp.dma_start(out=b[r + 128:r + 256, :],
                                     in_=AL(dc * TOK, TOK)).then_inc(sGQ[dc // 4], 16)
                    GE(shp)
                    C.bump("sGQ%d" % (dc // 4), 32)
            C.c["ln_split_done"] = C.val(sV)
            C.c["ln_split_doneG"] = C.val("sGP")

        def gather(gdst):
            gi, go = gidx[gdst]
            if gdst == 2:
                gcnt = C.val(sG)

                def coll(gp, gcnt=gcnt, gi=gi, go=go):
                    gp.wait_ge(sG, gcnt)
                    gp.collective_compute(
                        "AllGather", OP.bypass,
                        replica_groups=[[0, 1], [2, 3], [4, 5], [6, 7]],
                        ins=[gi.ap()],
                        outs=[go.ap()],
                    ).then_inc(sCC, 1)
                GE(coll)
                return C.bump(sCC, 1)
            ccs = []
            for q in range(4):
                gq = C.val("sGQ%d" % q)

                def coll(gp, q=q, gq=gq, gi=gi[q], go=go[q]):
                    gp.wait_ge(sGQ[q], gq)
                    gp.collective_compute(
                        "AllGather", OP.bypass,
                        replica_groups=[[0, 1], [2, 3], [4, 5], [6, 7]],
                        ins=[gi.ap()],
                        outs=[go.ap()],
                    ).then_inc(sCC, 1)
                GE(coll)
                ccs.append(C.bump(sCC, 1))
            return ccs

        def stream_proj(gdst, ndc, row0, outs, cc_need):
            _, gout = gidx[gdst]
            wsp = [C.val("wsplit_dc%d" % d) for d in range(ndc)]
            for tt in range(4):
                rank, col0 = tt // 2, (tt % 2) * 512
                banks = [pb[(tt % 2) * 2 + k] for k in range(len(outs))]
                bfree = [C.val("proj_bank_free%d" % ((tt % 2) * 2 + k))
                         for k in range(len(outs))]
                for dc in range(ndc):
                    buf = dc % 3
                    tfree = C.val("xs_free%d" % buf)

                    if isinstance(cc_need, list):
                        ccd = cc_need[dc // 4]
                        gsel, r0 = gout[dc // 4], row0 + (dc % 4) * 256
                    else:
                        ccd = cc_need
                        gsel, r0 = gout, row0 + dc * 256

                    def ldx(sy, rank=rank, col0=col0, dc=dc, buf=buf, tfree=tfree,
                            gsel=gsel, r0=r0, ccd=ccd):
                        if tfree:
                            sy.wait_ge(sT, tfree)
                        sy.wait_ge(sCC, ccd)
                        src_ap = gsel.ap()[rank, r0:r0 + 256,
                                           col0:col0 + 512].rearrange(
                            "(h p) c -> p h c", p=128)
                        dst_ap = xs_hl[:, buf * 1024:buf * 1024 + 1024].rearrange(
                            "p (h c) -> p h c", h=2)
                        sy.dma_start(out=dst_ap, in_=src_ap).then_inc(sX[buf], 16)
                    SY(ldx)
                    wcnt = C.bump("sX%d" % buf, 16)

                    def mmp(te, dc=dc, buf=buf, wcnt=wcnt, ndc=ndc, banks=banks,
                            wsp=wsp, bfree=bfree, outs=outs):
                        te.wait_ge(sX[buf], wcnt)
                        te.wait_ge(sV, wsp[dc][0])
                        te.wait_ge(sGP, wsp[dc][1])
                        for k, (dsto, mode, wbase) in enumerate(outs):
                            if dc == 0 and bfree[k]:
                                te.wait_ge(sV, bfree[k])
                            w0 = wbase + dc * 128
                            i = mm3(te, banks[k][:, 0:512],
                                    wsh[:, w0:w0 + 128], wsl[:, w0:w0 + 128],
                                    xs_hl[:, buf * 1024:buf * 1024 + 512],
                                    xs_hl[:, buf * 1024 + 512:buf * 1024 + 1024],
                                    start=(dc == 0), stop=(dc == ndc - 1))
                        i.then_inc(sT, 1)
                    TE(mmp)
                    tcnt = C.bump(sT, 1)
                    C.c["xs_free%d" % buf] = tcnt
                tfin = C.val(sT)

                for k, (dsto, mode, wbase) in enumerate(outs):
                    bidx = (tt % 2) * 2 + k
                    if mode == "vt":
                        def ev0(ve, tfin=tfin, bank=banks[k]):
                            ve.wait_ge(sT, tfin)
                            ve.drain()
                            ve.tensor_copy(sc4[:], bank[:, 0:512]).then_inc(sV, 1)
                        VE(ev0)
                        vcnt = C.bump(sV, 1)
                        C.c["proj_bank_free%d" % bidx] = vcnt

                        def tr(te, vcnt=vcnt):
                            te.wait_ge(sV, vcnt)
                            te.wait_ge(sV, C.val("vt_ps_free"))
                            for j in range(4):
                                i = te.matmul(pb[6][:, j * 128:(j + 1) * 128],
                                              sc4[:, j * 128:(j + 1) * 128], idsb[:],
                                              is_transpose=True, start=(j == 0),
                                              stop=(j == 3), skip_group_check=True)
                            i.then_inc(sT, 1)
                        TE(tr)
                        tcnt2 = C.bump(sT, 1)

                        def ev1(ve, tt=tt, dsto=dsto, tcnt2=tcnt2):
                            ve.wait_ge(sT, tcnt2)
                            for j in range(4):
                                kt = tt * 4 + j
                                i = ve.tensor_copy(
                                    arena[:, dsto * T + kt * 128:dsto * T + (kt + 1) * 128],
                                    pb[6][:, j * 128:(j + 1) * 128])
                            i.then_inc(sV, 1)
                        VE(ev1)
                        C.c["vt_ps_free"] = C.bump(sV, 1)
                    else:
                        scl = SCALE if mode == "qs" else 1.0

                        def ev(ve, tt=tt, dsto=dsto, scl=scl, tfin=tfin,
                               bank=banks[k]):
                            ve.wait_ge(sT, tfin)
                            ve.drain()
                            ve.tensor_scalar(
                                arena[:, dsto * T + tt * 512:dsto * T + tt * 512 + 512],
                                bank[:, 0:512], scl, 0.0, OP.mult,
                                OP.add).then_inc(sV, 1)
                        VE(ev)
                        C.c["proj_bank_free%d" % bidx] = C.bump(sV, 1)
            C.c["proj_banks_free"] = C.val(sV)
            for dsto, mode, wbase in outs:
                C.c["wsh_rd%d" % (wbase // 2048)] = C.val(sT)

        def rope_pair():
            for (tabc, tabs, dof) in ((cosq2, sinq2, 6), (cosk2, sink2, 7)):
                apre = C.val("proj_banks_free")
                for cb in range(4):
                    vpre2 = C.val("rope_tab_rd")
                    c0 = dof * T + cb * 512

                    def ldt(sy, tabc=tabc, tabs=tabs, cb=cb, vpre2=vpre2, c0=c0,
                            apre=apre):
                        if vpre2:
                            sy.wait_ge(sV, vpre2)
                        if apre:
                            sy.wait_ge(sV, apre)
                        sy.dma_start(out=tabA[:],
                                     in_=tabc[:, cb * 512:(cb + 1) * 512]).then_inc(sR, 16)
                        sy.dma_start(out=tabB[:],
                                     in_=tabs[:, cb * 512:(cb + 1) * 512]).then_inc(sR, 16)
                        sy.dma_start(out=rotb[0:32, :],
                                     in_=arena[32:64, c0:c0 + 512]).then_inc(sR, 16)
                        sy.dma_start(out=rotb[32:64, :],
                                     in_=arena[0:32, c0:c0 + 512]).then_inc(sR, 16)
                        sy.dma_start(out=rotb[64:96, :],
                                     in_=arena[96:128, c0:c0 + 512]).then_inc(sR, 16)
                        sy.dma_start(out=rotb[96:128, :],
                                     in_=arena[64:96, c0:c0 + 512]).then_inc(sR, 16)
                    SY(ldt)
                    wcnt = C.bump("sR", 96)

                    def rp(ve, cb=cb, wcnt=wcnt, c0=c0):
                        ve.wait_ge(sR, wcnt)
                        ve.tensor_tensor(arena[:, c0:c0 + 512], arena[:, c0:c0 + 512],
                                         tabA[:], OP.mult)
                        ve.tensor_tensor(rotb[:], rotb[:], tabB[:], OP.mult)
                        ve.drain()
                        ve.tensor_tensor(arena[:, c0:c0 + 512], arena[:, c0:c0 + 512],
                                         rotb[:], OP.add).then_inc(sV, 1)
                    VE(rp)
                    C.bump(sV, 1)
                    C.c["rope_tab_rd"] = C.val(sV)

        TAILQ = []

        def flush_tails():
            while TAILQ:
                TAILQ.pop(0)()

        def attention_head(sub, h, hh):
            qo = hh
            ko = 2 + hh
            vo = 4 + hh
            rr0 = hh * 64
            gate_pb = C.val("proj_banks_free")
            gate_rp = C.val("rope_tab_rd")
            for J in range(4):
                ntiles = 4 * J + 4
                for i in range(ntiles):
                    sc_bank = pb[i % 2]
                    diag = (sub == 1) and (i >= 4 * J)
                    use_mask = ((sub == 0) or diag) and dbg["attn_parts"] >= 2
                    mw = None
                    mb = i % 2
                    if use_mask:
                        vfree = C.val("msk_free%d" % mb)

                        def ldm(sy, i=i, J=J, mb=mb, vfree=vfree, sub=sub):
                            if vfree:
                                sy.wait_ge(sV, vfree)
                            if sub == 0:
                                src = mask_a[i * 128:(i + 1) * 128, J * 512:J * 512 + 512]
                            else:
                                ii = i - 4 * J
                                src = mask_c[ii * 128:(ii + 1) * 128, 0:512]
                            sy.dma_start(out=mskb[:, mb * 512:mb * 512 + 512],
                                         in_=src).then_inc(sM[mb], 16)
                        AE(ldm)
                        mw = C.bump("sM%d" % mb, 16)

                    apre = C.val("exp_done_t%d" % (i % 2))

                    def mmsc(te, i=i, J=J, sc_bank=sc_bank, qo=qo, ko=ko,
                             rr0=rr0, apre=apre, sub=sub, gate_pb=gate_pb,
                             gate_rp=gate_rp):
                        if apre:
                            te.wait_ge(sA, apre)
                        if i == 0 and J == 0:
                            if gate_pb:
                                te.wait_ge(sV, gate_pb)
                            if gate_rp:
                                te.wait_ge(sV, gate_rp)
                        ins = te.matmul(sc_bank[:, 0:512],
                                        arena[:, ko * T + i * 128:ko * T + (i + 1) * 128],
                                        arena[:, qo * T + J * 512:qo * T + J * 512 + 512],
                                        start=True,
                                        stop=(sub == 1), skip_group_check=True)
                        if sub == 0:
                            ins = te.matmul(sc_bank[:, 0:512],
                                            arena[rr0:rr0 + 64, 7 * T + i * 128:7 * T + (i + 1) * 128],
                                            arena[rr0:rr0 + 64, 6 * T + J * 512:6 * T + J * 512 + 512],
                                            start=False, stop=True, skip_group_check=True)
                        ins.then_inc(sT, 1)
                    TE(mmsc)
                    tcnt = C.bump(sT, 1)

                    if use_mask:
                        def mska(ve, sc_bank=sc_bank, mb=mb, tcnt=tcnt, mw=mw):
                            ve.wait_ge(sT, tcnt)
                            ve.wait_ge(sM[mb], mw)
                            ve.tensor_tensor(sc_bank[:, 0:512], sc_bank[:, 0:512],
                                             mskb[:, mb * 512:mb * 512 + 512],
                                             OP.add).then_inc(sV, 1)
                        VE(mska)
                        vcnt = C.bump(sV, 1)
                        C.c["msk_free%d" % mb] = vcnt
                        dep = ("V", vcnt)
                    else:
                        dep = ("T", tcnt)

                    if dbg["attn_parts"] < 3:
                        C.c["exp_done_t%d" % (i % 2)] = dep[1] if dep[0] == "V" else 0
                        if dep[0] != "V":
                            def drain(ve, sc_bank=sc_bank, tcnt=tcnt):
                                ve.wait_ge(sT, tcnt)
                                ve.tensor_copy(sc4[:], sc_bank[:, 0:512]).then_inc(sV, 1)
                            VE(drain)
                            C.c["exp_done_t%d" % (i % 2)] = 0
                            C.c["drain_v"] = C.bump(sV, 1)
                        continue
                    eb = i % 3
                    epre = C.val("exp_rd_%d" % eb)
                    epreV = C.val("exp_rdv_%d" % eb)

                    def expf(ae, sc_bank=sc_bank, eb=eb, dep=dep, epre=epre,
                             epreV=epreV):
                        if dep[0] == "V":
                            ae.wait_ge(sV, dep[1])
                        else:
                            ae.wait_ge(sT, dep[1])
                        if epre:
                            ae.wait_ge(sT, epre)
                        if epreV:
                            ae.wait_ge(sV, epreV)
                        ae.activation(out=expb[:, eb * 512:eb * 512 + 512],
                                      in_=sc_bank[:, 0:512], func=AF.Exp).then_inc(sA, 1)
                    AE(expf)
                    acnt = C.bump(sA, 1)
                    C.c["exp_done_t%d" % (i % 2)] = acnt
                    C.c["exp_rd"] = acnt
                    if dbg["attn_parts"] < 4:
                        continue

                    def mmav(te, i=i, eb=eb, vo=vo, acnt=acnt, J=J,
                             avfree=C.val("avrs_free%d" % (J % 4))):
                        te.wait_ge(sA, acnt)
                        if i == 0 and avfree:
                            te.wait_ge(sV, avfree)
                        av_b = pb[2 + J]
                        te.matmul(av_b[:, 0:512],
                                  arena[:, vo * T + i * 128:vo * T + (i + 1) * 128],
                                  expb[:, eb * 512:eb * 512 + 512],
                                  start=(i == 0), stop=False,
                                  skip_group_check=True).then_inc(sT, 1)
                    TE(mmav)
                    tcnt2 = C.bump(sT, 1)
                    C.c["exp_rd_%d" % eb] = tcnt2

                    aprev = C.val("acc_prev")
                    efree = C.val("eacc_free") if i == 0 else 0

                    def accf(ve, i=i, eb=eb, acnt=acnt, aprev=aprev,
                             efree=efree):
                        ve.wait_ge(sA, acnt)
                        if aprev:
                            ve.wait_ge(sV, aprev)
                        if i == 0:
                            if efree:
                                ve.wait_ge(sT, efree)
                            ve.tensor_copy(eacc[:],
                                           expb[:, eb * 512:eb * 512 + 512]).then_inc(sV, 1)
                        else:
                            ve.tensor_tensor(eacc[:], eacc[:],
                                             expb[:, eb * 512:eb * 512 + 512],
                                             OP.add).then_inc(sV, 1)
                    VE(accf)
                    vacc = C.bump(sV, 1)
                    C.c["acc_prev"] = vacc
                    C.c["exp_rdv_%d" % eb] = vacc
                    if i == 1:
                        flush_tails()

                if dbg["attn_parts"] < 5:
                    continue
                vlast = C.val("acc_prev")
                rfree = C.val("rsrow_free")

                def mmrs(te, vlast=vlast, rfree=rfree):
                    te.wait_ge(sV, vlast)
                    if rfree:
                        te.wait_ge(sV, rfree)
                    te.matmul(pb[7][0:1, 0:512], ones[:, 0:1], eacc[:],
                              start=True, stop=True,
                              skip_group_check=True).then_inc(sT, 1)
                TE(mmrs)
                trs = C.bump(sT, 1)
                C.c["eacc_free"] = trs

                def emit_tail(trs=trs, J=J, h=h):
                  wship = C.val("sRS")
                  tgate = C.val(sT)

                  def rsum(ve, trs=trs, wship=wship):
                    if wship:
                        ve.wait_ge(sRS, wship)
                    ve.wait_ge(sT, trs)
                    ve.drain()
                    ve.reciprocal(sc1[0:1, :], pb[7][0:1, 0:512]).then_inc(sV, 1)
                  VE(rsum)
                  vcnt = C.bump(sV, 1)
                  C.c["rsrow_free"] = vcnt

                  def shiprs(sy, vcnt=vcnt):
                    sy.wait_ge(sV, vcnt)
                    sy.dma_start(out=rowd.ap()[2, 0:512], in_=sc1[0:1, :]).then_inc(sRS, 16)
                  AE(shiprs)
                  wcnt = C.bump("sRS", 16)
                  gpre = C.val(sG)

                  def bcrs(sy, wcnt=wcnt, gpre=gpre):
                    sy.wait_ge(sRS, wcnt)
                    if gpre:
                        sy.wait_ge(sG, gpre)
                    sy.dma_start(out=sc3[:],
                                 in_=rowd.ap()[2:3, 0:512].to_broadcast([128, 512])).then_inc(sRS, 16)
                  AE(bcrs)
                  wcnt2 = C.bump("sRS", 16)

                  def norm(ve, wcnt2=wcnt2, gpre=C.val("sGP"), J=J,
                           tgate=tgate):
                    ve.wait_ge(sRS, wcnt2)
                    ve.wait_ge(sT, tgate)
                    if gpre:
                        ve.wait_ge(sGP, gpre)
                    ve.drain()
                    av_b = pb[2 + J]
                    ve.tensor_tensor(sc4[:], av_b[:, 0:512], sc3[:],
                                     OP.mult).then_inc(sV, 1)
                  VE(norm)
                  vcnt2 = C.bump(sV, 1)
                  C.c["avrs_free%d" % (J % 4)] = vcnt2
                  C.c["avrs_free"] = vcnt2
                  C.c["rs_ship"] = wcnt2

                  ocnt = split_to(sc4[:], vcnt2, (sOP, C.val("sOP")))

                  def oshp(sy, ocnt=ocnt, h=h, J=J):
                    sy.wait_ge(sV, ocnt)
                    sy.dma_start(out=opc.ap()[0, h * 128:(h + 1) * 128,
                                              J * 512:J * 512 + 512],
                                 in_=obh[:]).then_inc(sOP, 16)
                    sy.dma_start(out=opc.ap()[1, h * 128:(h + 1) * 128,
                                              J * 512:J * 512 + 512],
                                 in_=obl[:]).then_inc(sOP, 16)
                  AE(oshp)
                  C.bump("sOP", 32)
                TAILQ.append(emit_tail)

        def oproj_rs(sub):
            wo = wo_a if sub == 0 else wo_b
            gpre = C.val("sOP")
            for oc in range(16):
                wb = split_weights(wo, oc, 8)
                for tt in range(4):
                    for dc in range(8):
                        buf = dc % 3
                        tfree = C.val("xs_free%d" % buf)
                        wsplit = C.val("wsplit_dc%d" % dc)

                        def ldo(sy, dc=dc, buf=buf, tfree=tfree, tt=tt, gpre=gpre):
                            if tfree:
                                sy.wait_ge(sT, tfree)
                            sy.wait_ge(sOP, gpre)
                            src_ap = opc.ap()[:, dc * 128:(dc + 1) * 128,
                                              tt * 512:tt * 512 + 512].rearrange(
                                "h p c -> p h c")
                            dst_ap = xs_hl[:, buf * 1024:buf * 1024 + 1024].rearrange(
                                "p (h c) -> p h c", h=2)
                            sy.dma_start(out=dst_ap, in_=src_ap).then_inc(sX[buf], 16)
                        SY(ldo)
                        wcnt = C.bump("sX%d" % buf, 16)

                        def mmo(te, dc=dc, buf=buf, wcnt=wcnt, wsplit=wsplit,
                                wb=wb):
                            te.wait_ge(sX[buf], wcnt)
                            te.wait_ge(sV, wsplit[0])
                            te.wait_ge(sGP, wsplit[1])
                            if dc == 0:
                                te.wait_ge(sV, C.val("oproj_bank_free"))
                            i = mm3(te, pb[7][:, 0:512],
                                    wsh[:, wb + dc * 128:wb + dc * 128 + 128],
                                    wsl[:, wb + dc * 128:wb + dc * 128 + 128],
                                    xs_hl[:, buf * 1024:buf * 1024 + 512],
                                    xs_hl[:, buf * 1024 + 512:buf * 1024 + 1024],
                                    start=(dc == 0), stop=(dc == 7))
                            i.then_inc(sT, 1)
                        TE(mmo)
                        tcnt = C.bump(sT, 1)
                        C.c["xs_free%d" % buf] = tcnt

                    gfree = C.val("oev_shipped")

                    def evo(ve, tcnt=tcnt, gfree=gfree):
                        ve.wait_ge(sT, tcnt)
                        if gfree:
                            ve.wait_ge(sG, gfree)
                        ve.drain()
                        ve.tensor_copy(sc1[:], pb[7][:, 0:512]).then_inc(sV, 1)
                    VE(evo)
                    vcnt = C.bump(sV, 1)
                    C.c["oproj_bank_free"] = vcnt

                    def sho(gp, vcnt=vcnt, oc=oc, tt=tt):
                        gp.wait_ge(sV, vcnt)
                        gp.dma_start(out=rs_ins[oc // 4].ap()[tt // 2, (oc % 4) * 128:(oc % 4 + 1) * 128,
                                                              (tt % 2) * 512:(tt % 2) * 512 + 512],
                                     in_=sc1[:]).then_inc(sG, 16)
                    GE(sho)
                    C.c["oev_shipped"] = C.bump(sG, 16)
                C.c["wsh_rd%d" % (wb // 2048)] = C.val(sT)
                if oc % 4 == 3:
                    gq = C.val(sG)

                    def collq(gp, gq=gq, q=oc // 4):
                        gp.wait_ge(sG, gq)
                        gp.collective_compute(
                            "ReduceScatter", OP.add,
                            replica_groups=[[0, 1], [2, 3], [4, 5], [6, 7]],
                            ins=[rs_ins[q].ap()],
                            outs=[rs_outs[q].ap()],
                        ).then_inc(sCC, 1)
                    GE(collq)
                    C.c["rs_cc%d" % (oc // 4)] = C.bump(sCC, 1)

            for dc in range(16):
                vfree = C.val("res_free")
                cc2 = C.val("rs_cc%d" % (dc // 4))

                def ldr(sy, dc=dc, cc2=cc2, vfree=vfree):
                    if vfree:
                        sy.wait_ge(sV, vfree)
                    sy.wait_ge(sCC, cc2)
                    sy.dma_start(out=rsb[:, 0:512],
                                 in_=rs_outs[dc // 4].ap()[(dc % 4) * 128:(dc % 4 + 1) * 128,
                                                           0:512]).then_inc(sRES, 16)
                    sy.dma_start(out=rsb[:, 512:1024],
                                 in_=rs_outs[dc // 4].ap()[(dc % 4) * 128:(dc % 4 + 1) * 128,
                                                           512:1024]).then_inc(sRES, 16)
                SY(ldr)
                wcnt = C.bump("sRES", 32)

                def addr_(ve, dc=dc, wcnt=wcnt):
                    ve.wait_ge(sRES, wcnt)
                    ve.drain()
                    ve.tensor_tensor(xr[:, dc * TOK:dc * TOK + 512],
                                     xr[:, dc * TOK:dc * TOK + 512], rsb[:, 0:512], OP.add)
                    ve.tensor_tensor(xr[:, dc * TOK + 512:dc * TOK + 1024],
                                     xr[:, dc * TOK + 512:dc * TOK + 1024], rsb[:, 512:1024],
                                     OP.add).then_inc(sV, 1)
                VE(addr_)
                C.c["res_free"] = C.bump(sV, 1)
                C.c["res_dc%d" % dc] = C.val(sV)
            C.c["ln_gate"] = C.val(sV)

        def attn_a():
            if not dbg["ln"]:
                return
            ln_phase(gdst=0)
            cc = gather(0)
            for oc in range(4):
                wb = split_weights(wdkv, oc, 16)
                for tt in range(2):
                    bank = pb[2 + (oc % 2)]
                    bfree = C.val("ckv_bank_free%d" % (oc % 2))
                    lnv = [C.val("lnsp_v%d_%d" % (d, tt)) for d in range(16)]
                    lng = [C.val("lnsp_g%d_%d" % (d, tt)) for d in range(16)]
                    wsp = [C.val("wsplit_dc%d" % d) for d in range(16)]

                    def mmckv(te, tt=tt, wb=wb, bank=bank, bfree=bfree,
                              lnv=lnv, lng=lng, wsp=wsp):
                        if bfree:
                            te.wait_ge(sV, bfree)
                        for dc in range(16):
                            te.wait_ge(sV, max(wsp[dc][0], lnv[dc]))
                            te.wait_ge(sGP, max(wsp[dc][1], lng[dc]))
                            w0 = wb + dc * 128
                            i = mm3(te, bank[:, 0:512],
                                    wsh[:, w0:w0 + 128], wsl[:, w0:w0 + 128],
                                    AH(dc * TOK + tt * 512, 512),
                                    AL(dc * TOK + tt * 512, 512),
                                    start=(dc == 0), stop=(dc == 15))
                        i.then_inc(sT, 1)
                    TE(mmckv)
                    tcnt = C.bump(sT, 1)

                    def evc(ve, bank=bank, tcnt=tcnt, gpre=C.val("sGP")):
                        ve.wait_ge(sT, tcnt)
                        if gpre:
                            ve.wait_ge(sGP, gpre)
                        ve.drain()
                        ve.tensor_copy(sc4[:], bank[:, 0:512]).then_inc(sV, 1)
                    VE(evc)
                    vcnt = C.bump(sV, 1)
                    C.c["ckv_bank_free%d" % (oc % 2)] = vcnt

                    ocnt = split_to(sc4[:], vcnt, (sG, C.val(sG)))

                    def cshp(gp, ocnt=ocnt, oc=oc, tt=tt):
                        gp.wait_ge(sV, ocnt)
                        gp.dma_start(out=g1c_in.ap()[oc * 256:oc * 256 + 128,
                                                     tt * 512:tt * 512 + 512],
                                     in_=obh[:]).then_inc(sG, 16)
                        gp.dma_start(out=g1c_in.ap()[oc * 256 + 128:oc * 256 + 256,
                                                     tt * 512:tt * 512 + 512],
                                     in_=obl[:]).then_inc(sG, 16)
                    GE(cshp)
                    C.bump(sG, 32)
                C.c["wsh_rd%d" % (wb // 2048)] = C.val(sT)

            wq_next = [split_weights(wq_a, 0, 16), split_weights(wq_a, 1, 16)]
            ccc = gather(2)
            np_ = min(NHH // 2, dbg["pairs"])
            for hp in range(np_):
                stream_proj(gdst=0, ndc=16, row0=0,
                            outs=[(0, "qs", wq_next[0]), (1, "qs", wq_next[1])],
                            cc_need=cc)
                wb1 = split_weights(wqp, hp, 16)
                wb2 = split_weights(wkp, hp, 16)
                stream_proj(gdst=0, ndc=16, row0=0,
                            outs=[(6, "n", wb1), (7, "n", wb2)], cc_need=cc)
                wb1 = split_weights(wuk, 2 * hp, 4)
                wb2 = split_weights(wuv, 2 * hp, 4)
                stream_proj(gdst=2, ndc=4, row0=0,
                            outs=[(2, "n", wb1), (4, "vt", wb2)], cc_need=ccc)
                wb1 = split_weights(wuk, 2 * hp + 1, 4)
                wb2 = split_weights(wuv, 2 * hp + 1, 4)
                stream_proj(gdst=2, ndc=4, row0=0,
                            outs=[(3, "n", wb1), (5, "vt", wb2)], cc_need=ccc)
                if dbg["rope"]:
                    rope_pair()
                if hp + 1 < np_:
                    wq_next = [split_weights(wq_a, 2 * hp + 2, 16),
                               split_weights(wq_a, 2 * hp + 3, 16)]
                if dbg["attn"]:
                    for hh in range(dbg["heads"]):
                        attention_head(0, 2 * hp + hh, hh)
            flush_tails()
            C.c["ln_banks_free"] = C.val(sV)
            if dbg["oproj"]:
                oproj_rs(0)

        def attn_b():
            ln_phase(gdst=1)
            wq_next = [split_weights(wq_b, 0, 16), split_weights(wq_b, 1, 16)]
            cc = gather(1)
            for hp in range(NHH // 2):
                stream_proj(gdst=1, ndc=16, row0=0,
                            outs=[(0, "qs", wq_next[0]), (1, "qs", wq_next[1])],
                            cc_need=cc)
                wb1 = split_weights(wk_b, 2 * hp, 16)
                wb2 = split_weights(wk_b, 2 * hp + 1, 16)
                stream_proj(gdst=1, ndc=16, row0=0,
                            outs=[(2, "n", wb1), (3, "n", wb2)], cc_need=cc)
                wb1 = split_weights(wv_b, 2 * hp, 16)
                wb2 = split_weights(wv_b, 2 * hp + 1, 16)
                stream_proj(gdst=1, ndc=16, row0=0,
                            outs=[(4, "vt", wb1), (5, "vt", wb2)], cc_need=cc)
                if hp + 1 < NHH // 2:
                    wq_next = [split_weights(wq_b, 2 * hp + 2, 16),
                               split_weights(wq_b, 2 * hp + 3, 16)]
                attention_head(1, 2 * hp, 0)
                attention_head(1, 2 * hp + 1, 1)
            flush_tails()
            C.c["ln_banks_free"] = C.val(sV)
            oproj_rs(1)

        def ffn():
            ln_phase(gdst=None)
            vpre = C.val("ln_split_done")

            def ldw2(sy):
                sy.wait_ge(sT, C.val("w2sb_rd"))
                for oc in range(16):
                    sy.dma_start(out=w2sb[:, oc * 8:(oc + 1) * 8],
                                 in_=w2[oc, :, :]).then_inc(sB, 16)
                    sy.dma_start(out=b1sb[:, oc:oc + 1],
                                 in_=b1[oc, :, :]).then_inc(sB, 16)
                sy.dma_start(out=b2sb[:, :], in_=b2[:, :]).then_inc(sB, 16)
            SY(ldw2)
            w2w = C.bump("sB", 16 * 33)

            for oc in range(16):
                wb = split_weights(w1, oc, 16)
                for tt in range(2):
                    if True:
                        slot = oc % 2
                        ocp = oc // 2
                        hfree = C.val("h_bank_free%d" % slot)

                        def mmh(te, tt=tt, slot=slot, oc=oc, wb=wb,
                                hfree=hfree,
                                wsp=[C.val("wsplit_dc%d" % d) for d in range(16)],
                                lnv=[C.val("lnsp_v%d_%d" % (d, tt)) for d in range(16)],
                                lng=[C.val("lnsp_g%d_%d" % (d, tt)) for d in range(16)]):
                            if hfree:
                                te.wait_ge(sA, hfree)
                            if oc == 0 and tt == 0:
                                te.wait_ge(sV, C.val("ln_banks_free"))
                                te.wait_ge(sV, C.val("proj_banks_free"))
                                te.wait_ge(sA, C.val("exp_rd"))
                                te.wait_ge(sV, C.val("avrs_free"))
                            for dc in range(16):
                                te.wait_ge(sV, max(wsp[dc][0], lnv[dc]))
                                te.wait_ge(sGP, max(wsp[dc][1], lng[dc]))
                                w0 = wb + dc * 128
                                i = mm3(te, pb[slot][:, 0:512],
                                        wsh[:, w0:w0 + 128], wsl[:, w0:w0 + 128],
                                        AH(dc * TOK + tt * 512, (dc * TOK + tt * 512 + 512) - (dc * TOK + tt * 512)),
                                        AL(dc * TOK + tt * 512, (dc * TOK + tt * 512 + 512) - (dc * TOK + tt * 512)),
                                        start=(dc == 0), stop=(dc == 15))
                            i.then_inc(sT, 1)
                        TE(mmh)
                        tcnt = C.bump(sT, 1)

                        def rel(ae, slot=slot, oc=oc, tcnt=tcnt, w2w=w2w):
                            ae.wait_ge(sT, tcnt)
                            ae.wait_ge(sT, C.val("w2_rd%d" % slot))
                            ae.wait_ge(sB, w2w)
                            ae.activation(out=hbuf[:, slot * 512:slot * 512 + 512],
                                          in_=pb[slot][:, 0:512], func=AF.Relu,
                                          bias=b1sb[:, oc:oc + 1], scale=1.0).then_inc(sA, 1)
                        AE(rel)
                        acnt = C.bump(sA, 1)
                        C.c["h_bank_free%d" % slot] = acnt

                        def mm2(te, slot=slot, oc=oc, tt=tt, acnt=acnt,
                                w2w=w2w):
                            te.wait_ge(sA, acnt)
                            if oc == 0:
                                te.wait_ge(sV, C.val("alpha_bank_free"))
                                te.wait_ge(sV, C.val("ln_banks_free"))
                            te.wait_ge(sB, w2w)
                            i = te.matmul(pb[4 + tt][0:8, 0:512],
                                          w2sb[:, oc * 8:(oc + 1) * 8],
                                          hbuf[:, slot * 512:slot * 512 + 512],
                                          start=(oc == 0),
                                          stop=(oc == 15),
                                          skip_group_check=True)
                            i.then_inc(sT, 1)
                        TE(mm2)
                        tcnt2 = C.bump(sT, 1)
                        C.c["w2_rd%d" % slot] = tcnt2
                C.c["wsh_rd%d" % (wb // 2048)] = C.val(sT)
            C.c["w2sb_rd"] = C.val(sT)
            C.c["wsh_rd"] = C.val(sT)

            tfin = C.val(sT)
            for tt in range(2):
                def aev(ve, tt=tt, tfin=tfin):
                    ve.wait_ge(sT, tfin)
                    ve.tensor_scalar(hbuf[0:8, tt * 512:tt * 512 + 512],
                                     pb[4 + tt][0:8, 0:512], b2sb[:, :], 0.0,
                                     OP.add, OP.add).then_inc(sV, 1)
                VE(aev)
                vcnt = C.bump(sV, 1)

                def aex(ae, tt=tt, vcnt=vcnt):
                    ae.wait_ge(sV, vcnt)
                    ae.activation(out=hbuf[0:8, tt * 512:tt * 512 + 512],
                                  in_=hbuf[0:8, tt * 512:tt * 512 + 512],
                                  func=AF.Exp).then_inc(sA, 1)
                AE(aex)
                acnt = C.bump(sA, 1)

                vt_free = C.val("vt_ps_free")

                def asum(te, tt=tt, acnt=acnt, vt_free=vt_free):
                    te.wait_ge(sA, acnt)
                    if vt_free:
                        te.wait_ge(sV, vt_free)
                    i = te.matmul(pb[6][0:1, 0:512],
                                  ones[0:8, 0:1], hbuf[0:8, tt * 512:tt * 512 + 512],
                                  start=True, stop=True,
                                  skip_group_check=True)
                    i.then_inc(sT, 1)
                TE(asum)
                tcnt = C.bump(sT, 1)

                lnpre = C.val("sLN")

                def arc(ve, tt=tt, tcnt=tcnt, lnpre=lnpre):
                    ve.wait_ge(sT, tcnt)
                    if lnpre:
                        ve.wait_ge(sLN, lnpre)
                    ve.drain()
                    ve.reciprocal(sc1[0:1, :], pb[6][0:1, 0:512]).then_inc(sV, 1)
                VE(arc)
                vcnt2 = C.bump(sV, 1)
                C.c["vt_ps_free"] = vcnt2


                def aship(sy, tt=tt, vcnt2=vcnt2):
                    sy.wait_ge(sV, vcnt2)
                    sy.dma_start(out=rowd.ap()[3, tt * 512:tt * 512 + 512],
                                 in_=sc1[0:1, :]).then_inc(sLN, 16)
                SY(aship)
                C.bump("sLN", 16)

            wcnt = C.val("sLN")

            vgate2 = C.val(sV)

            def abc(sy, wcnt=wcnt, vgate2=vgate2):
                sy.wait_ge(sLN, wcnt)
                if vgate2:
                    sy.wait_ge(sV, vgate2)
                sy.dma_start(out=rowb[0:8, :],
                             in_=rowd.ap()[3:4, 0:TOK].to_broadcast([8, TOK])).then_inc(sLN, 16)
            SY(abc)
            wcnt2 = C.bump("sLN", 16)

            def anrm(ve, wcnt2=wcnt2):
                ve.wait_ge(sLN, wcnt2)
                ve.tensor_tensor(hbuf[0:8, 0:1024], hbuf[0:8, 0:1024], rowb[0:8, :],
                                 OP.mult).then_inc(sV, 1)
            VE(anrm)
            vcnt3 = C.bump(sV, 1)

            def aout(sy, vcnt3=vcnt3):
                sy.wait_ge(sV, vcnt3)
                sy.dma_start(out=alpd.ap()[:, :], in_=hbuf[0:8, 0:1024]).then_inc(sLN, 16)
            SY(aout)
            alpw = C.bump("sLN", 16)
            C.c["alpha_bank_free"] = vcnt3
            C.c["ln_banks_free"] = vcnt3

            for ph in range(2):
                vx_wb = [split_weights(v2w, 2 * ph, 16),
                         split_weights(v2w, 2 * ph + 1, 16)]
                for ci in range(2):
                    c = 2 * ph + ci
                    vbpre = C.val("alpb_rd")

                    def abr(sy, c=c, alpw=alpw, vbpre=vbpre):
                        sy.wait_ge(sLN, alpw)
                        if vbpre:
                            sy.wait_ge(sV, vbpre)
                        sy.dma_start(out=rowb[0:64, :],
                                     in_=alpd.ap()[2 * c:2 * c + 1, :].to_broadcast([64, TOK])).then_inc(sLN, 16)
                        sy.dma_start(out=rowb[64:128, :],
                                     in_=alpd.ap()[2 * c + 1:2 * c + 2, :].to_broadcast([64, TOK])).then_inc(sLN, 16)
                    SY(abr)
                    abw = C.bump("sLN", 32)

                    for tt in range(2):
                        bank = pb[ci]
                        bfree = C.val("vx_bank_free%d" % ci)

                        def mmvx(te, ci=ci, tt=tt, wb=vx_wb[ci],
                                 bank=bank, bfree=bfree,
                                 wsp=[C.val("wsplit_dc%d" % d) for d in range(16)],
                                 lnv=[C.val("lnsp_v%d_%d" % (d, tt)) for d in range(16)],
                                 lng=[C.val("lnsp_g%d_%d" % (d, tt)) for d in range(16)]):
                            if bfree:
                                te.wait_ge(sV, bfree)
                            for dc in range(16):
                                te.wait_ge(sV, max(wsp[dc][0], lnv[dc]))
                                te.wait_ge(sGP, max(wsp[dc][1], lng[dc]))
                                w0 = wb + dc * 128
                                i = mm3(te, bank[:, 0:512],
                                        wsh[:, w0:w0 + 128], wsl[:, w0:w0 + 128],
                                        AH(dc * TOK + tt * 512, 512),
                                        AL(dc * TOK + tt * 512, 512),
                                        start=(dc == 0), stop=(dc == 15))
                            i.then_inc(sT, 1)
                        TE(mmvx)
                        tcnt = C.bump(sT, 1)

                        def evx(ve, ci=ci, tt=tt, bank=bank, tcnt=tcnt, abw=abw,
                                gpre=C.val("sGP")):
                            ve.wait_ge(sT, tcnt)
                            ve.wait_ge(sLN, abw)
                            if gpre:
                                ve.wait_ge(sGP, gpre)
                            ve.drain()
                            ve.tensor_tensor(sc4[:], bank[:, 0:512],
                                             rowb[:, tt * 512:tt * 512 + 512],
                                             OP.mult).then_inc(sV, 1)
                        VE(evx)
                        vcnt = C.bump(sV, 1)
                        C.c["vx_bank_free%d" % ci] = vcnt

                        def vhi(ve, ci=ci, tt=tt, vcnt=vcnt):
                            ve.wait_ge(sV, vcnt)
                            ve.tensor_copy(
                                vxh[:, ci * TOK + tt * 512:ci * TOK + tt * 512 + 512],
                                sc4[:]).then_inc(sV, 1)
                        VE(vhi)
                        vcnt2 = C.bump(sV, 1)

                        def vlo(gp, ci=ci, tt=tt, vcnt2=vcnt2):
                            gp.wait_ge(sV, vcnt2)
                            gp.tensor_tensor(
                                vxl[:, ci * TOK + tt * 512:ci * TOK + tt * 512 + 512],
                                sc4[:],
                                vxh[:, ci * TOK + tt * 512:ci * TOK + tt * 512 + 512],
                                OP.subtract).then_inc(sGP, 1)
                        GE(vlo)
                        C.bump("sGP", 1)
                    C.c["alpb_rd"] = C.val(sV)
                    C.c["wsh_rd%d" % (vx_wb[ci] // 2048)] = C.val(sT)
                vx_done = (C.val(sV), C.val("sGP"))

                for oc in range(16):
                    u_wb = split_weights(u2w, oc, 4)
                    wsplit = C.val("wsplit_dc3")
                    for tt in range(2):
                        bfree = C.val("oproj_bank_free")

                        def mmu(te, oc=oc, tt=tt, ph=ph, wsplit=wsplit,
                                vx_done=vx_done, bfree=bfree, u_wb=u_wb):
                            te.wait_ge(sV, wsplit[0])
                            te.wait_ge(sGP, wsplit[1])
                            te.wait_ge(sV, vx_done[0])
                            te.wait_ge(sGP, vx_done[1])
                            if bfree:
                                te.wait_ge(sV, bfree)
                            for ci in range(2):
                                i = mm3(te, pb[7][:, 0:512],
                                        wsh[:, u_wb + (2 * ph + ci) * 128:u_wb + (2 * ph + ci) * 128 + 128],
                                        wsl[:, u_wb + (2 * ph + ci) * 128:u_wb + (2 * ph + ci) * 128 + 128],
                                        vxh[:, ci * TOK + tt * 512:ci * TOK + tt * 512 + 512],
                                        vxl[:, ci * TOK + tt * 512:ci * TOK + tt * 512 + 512],
                                        start=(ci == 0), stop=(ci == 1))
                            i.then_inc(sT, 1)
                        TE(mmu)
                        tcnt = C.bump(sT, 1)

                        def uadd(ve, oc=oc, tt=tt, tcnt=tcnt):
                            ve.wait_ge(sT, tcnt)
                            ve.tensor_tensor(xr[:, oc * TOK + tt * 512:oc * TOK + tt * 512 + 512],
                                             xr[:, oc * TOK + tt * 512:oc * TOK + tt * 512 + 512],
                                             pb[7][:, 0:512], OP.add).then_inc(sV, 1)
                        VE(uadd)
                        C.c["oproj_bank_free"] = C.bump(sV, 1)
                        if ph == 1 and tt == 1:
                            C.c["res_dc%d" % oc] = C.val(sV)
                    C.c["wsh_rd%d" % (u_wb // 2048)] = C.val(sT)
            C.c["ln_gate"] = C.val(sV)
            C.c["proj_banks_free"] = C.val(sV)

        for dc in range(16):
            def ld(sy, dc=dc):
                sy.dma_start(out=xr[:, dc * TOK:(dc + 1) * TOK],
                             in_=x0[dc, :, :]).then_inc(sW, 16)
            SY(ld)
            C.bump(sW, 16)

        def ldid(sy):
            sy.dma_start(out=idsb[:], in_=ident[:, :]).then_inc(sW, 16)
        SY(ldid)
        C.bump(sW, 16)
        wcnt0 = C.val(sW)

        def mkones(ve, wcnt0=wcnt0):
            ve.wait_ge(sW, wcnt0)
            ve.memset(ones[:], 1.0).then_inc(sV, 1)
        VE(mkones)
        C.c["ln_gate"] = C.bump(sV, 1)

        if n_sub >= 1:
            attn_a()
        if n_sub >= 2:
            ffn()
        if n_sub >= 3:
            attn_b()
        if n_sub >= 4:
            ffn()

        vfin, afin, tfin = C.val(sV), C.val(sA), C.val(sT)

        def store(sy, vfin=vfin, afin=afin, tfin=tfin,
                  gates=[C.val("res_dc%d" % dc) for dc in range(16)]):
            for dc in range(16):
                if gates[dc]:
                    sy.wait_ge(sV, gates[dc])
                else:
                    sy.wait_ge(sV, vfin)
                    sy.wait_ge(sA, afin)
                    sy.wait_ge(sT, tfin)
                sy.dma_start(out=y[dc, :, :],
                             in_=xr[:, dc * TOK:(dc + 1) * TOK]).then_inc(sOUT, 16)
        SY(store)
        wfin = C.bump("sOUT", 16 * 16)

        def fin(gp, wfin=wfin):
            gp.wait_ge(sOUT, wfin)
        GE(fin)

    return nc


def kernel(**inputs):
    cst = _host_consts()
    xin = np.asarray(inputs["L0_attn_a_x"], dtype=np.float32)

    def core_inputs(core):
        b, p = core // 2, core % 2
        d = {}
        xo = np.ascontiguousarray(xin[b, p * TOK:(p + 1) * TOK, :].T)
        d["x0"] = np.ascontiguousarray(xo.reshape(16, 128, TOK))
        for k in ("cosq2", "sinq2", "cosk2", "sink2", "ident"):
            d[k] = cst[k]
        d["mask_a"] = cst["mask_a"]
        d["mask_c"] = cst["mask_c"]
        hs = slice(p * 1024, (p + 1) * 1024)
        rs = slice(p * 512, (p + 1) * 512)
        d["wq_a"] = _chunked(np.asarray(inputs["L0_attn_a_w_q"])[:, hs])
        d["wdkv"] = _chunked(np.asarray(inputs["L0_attn_a_w_dkv"]))
        d["wuk"] = _chunked(np.asarray(inputs["L0_attn_a_w_uk"])[:, hs])
        d["wuv"] = _chunked(np.asarray(inputs["L0_attn_a_w_uv"])[:, hs])
        d["wqp"] = _chunked(np.asarray(inputs["L0_attn_a_w_q_pos"])[:, rs])
        d["wkp"] = _chunked(np.asarray(inputs["L0_attn_a_w_k_pos"])[:, rs])
        d["wo_a"] = _chunked(np.asarray(inputs["L0_attn_a_w_o"])[hs, :])
        d["wq_b"] = _chunked(np.ascontiguousarray(np.asarray(inputs["L0_attn_b_w_q"])[hs, :].T))
        d["wk_b"] = _chunked(np.ascontiguousarray(np.asarray(inputs["L0_attn_b_w_k"])[hs, :].T))
        d["wv_b"] = _chunked(np.ascontiguousarray(np.asarray(inputs["L0_attn_b_w_v"])[hs, :].T))
        d["wo_b"] = _chunked(np.ascontiguousarray(np.asarray(inputs["L0_attn_b_w_o"])[:, hs].T))
        d["w1"] = _chunked(np.asarray(inputs["L0_ffn_W1"]))
        d["b1"] = np.asarray(inputs["L0_ffn_b1"], dtype=np.float32).reshape(16, 128, 1)
        d["w2"] = np.asarray(inputs["L0_ffn_W2"], dtype=np.float32).reshape(16, 128, 8)
        d["b2"] = np.asarray(inputs["L0_ffn_b2"], dtype=np.float32).reshape(8, 1)
        V = np.asarray(inputs["L0_ffn_V"])
        U = np.asarray(inputs["L0_ffn_U"])
        V2 = np.ascontiguousarray(np.transpose(V, (2, 0, 1)).reshape(2048, 512))
        U2 = np.ascontiguousarray(np.transpose(U, (0, 2, 1)).reshape(512, 2048))
        d["v2w"] = _chunked(V2)
        d["u2w"] = _chunked(U2)
        return {k: (np.ascontiguousarray(v) if k in ("mask_a", "mask_c")
                    else np.ascontiguousarray(v, dtype=np.float32))
                for k, v in d.items()}

    nc = build_program(n_sub=N_SUB, dbg=DBG)
    in_maps = [core_inputs(c) for c in range(8)]
    res = run_bass_kernel_spmd(nc, in_maps, core_ids=list(range(8)))
    out = np.empty((B, T, D), dtype=np.float32)
    for c in range(8):
        b, p = c // 2, c % 2
        yc = np.asarray(res.results[c]["y"]).reshape(2048, TOK)
        out[b, p * TOK:(p + 1) * TOK, :] = yc.T
    return out
